# version 01
# baseline (speedup 1.0000x reference)
import numpy as np
import concourse.bass as bass
import concourse.mybir as mybir
from concourse.bass_utils import run_bass_kernel_spmd

F32 = mybir.dt.float32
BF16 = mybir.dt.bfloat16
AF = mybir.ActivationFunctionType
ALU = mybir.AluOpType
AX = mybir.AxisListType

P = 128
T = 2048
TH = 1024
D = 2048
FF = 768
E = 8
H = 16
NDC = D // P
NFC = FF // P
NOG = D // 512
LN_EPS = 1e-5
Z_EPS = 1e-6

_BUILD_UPTO = "full"
_SERIALIZE = False
_DMA_CONSERVATIVE = True
_SKIP_CC = False


class Ctx:

    ENGINES = ("sync", "gpsimd", "vector", "scalar", "tensor")
    DMA_ENGINES = ("sync", "gpsimd")

    def __init__(self, nc, stack):
        self.nc = nc
        self.sems = {e: stack.enter_context(nc.semaphore(f"sem_{e}")) for e in self.ENGINES}
        self.cnt = {e: 0 for e in self.ENGINES}
        self.last_wait = {}
        self.writer = {}
        self.readers = {}
        self.plans = {e: [] for e in self.ENGINES}
        self.n_waits = 0
        self.n_instr = 0

    def _need(self, needs, eng, cnt):
        if cnt > needs.get(eng, 0):
            needs[eng] = cnt

    serialize_now = False

    def emit(self, eng, fn, reads=(), writes=(), acquire=(), release=(), inc=None):
        needs = {}
        if _SERIALIZE or self.serialize_now:
            for teng in self.ENGINES:
                if self.cnt[teng] > 0:
                    needs[teng] = self.cnt[teng]
        for r in reads:
            for weng, wcnt in self.writer.get(r, {}).items():
                self._need(needs, weng, wcnt)
        for wres in tuple(writes) + tuple(acquire):
            for weng, wcnt in self.writer.get(wres, {}).items():
                self._need(needs, weng, wcnt)
            for reng, rcnt in self.readers.get(wres, {}).items():
                self._need(needs, reng, rcnt)
        if _DMA_CONSERVATIVE:
            for teng in self.DMA_ENGINES:
                if teng in needs:
                    needs[teng] = self.cnt[teng]
        wl = []
        for teng, val in needs.items():
            key = (eng, teng)
            if self.last_wait.get(key, 0) >= val:
                continue
            self.last_wait[key] = val
            wl.append((self.sems[teng], val))
        if inc is None:
            inc = 16 if eng in self.DMA_ENGINES else 1
        self.cnt[eng] += inc
        my = self.cnt[eng]
        sem = self.sems[eng]
        self.n_waits += len(wl)
        self.n_instr += 1

        def thunk(e, fn=fn, wl=wl, sem=sem, inc=inc):
            for s, v in wl:
                e.wait_ge(s, v)
            ins = fn(e)
            if inc:
                ins.then_inc(sem, inc)

        self.plans[eng].append(thunk)
        for r in reads:
            self.readers.setdefault(r, {})[eng] = my
        for wres in tuple(writes) + tuple(release):
            self.writer[wres] = {eng: my}
            self.readers[wres] = {}
        return my

    def region_switch(self, news, olds):
        merged = {}
        for o in olds:
            for eng, cnt in self.writer.pop(o, {}).items():
                if cnt > merged.get(eng, 0):
                    merged[eng] = cnt
            for eng, cnt in self.readers.pop(o, {}).items():
                if cnt > merged.get(eng, 0):
                    merged[eng] = cnt
        for n in news:
            w = dict(merged)
            for eng, cnt in self.writer.get(n, {}).items():
                if cnt > w.get(eng, 0):
                    w[eng] = cnt
            for eng, cnt in self.readers.get(n, {}).items():
                if cnt > w.get(eng, 0):
                    w[eng] = cnt
            self.writer[n] = w
            self.readers[n] = {}

    def final_wait(self, eng, targets):
        wl = [(self.sems[t], self.cnt[t]) for t in targets if self.cnt[t] > 0]

        def thunk(e, wl=wl):
            for s, v in wl:
                e.wait_ge(s, v)

        self.plans[eng].append(thunk)

    def run(self, block):
        for name in self.ENGINES:
            plan = self.plans[name]

            def make(plan):
                def body(e):
                    for t in plan:
                        t(e)
                return body

            getattr(block, name)(make(plan))


def build_program():
    nc = bass.Bass("TRN2", target_bir_lowering=False, debug=False)

    x_d = nc.dram_tensor("x", [T, D], F32, kind="ExternalInput").ap()
    wq_d = nc.dram_tensor("wq", [D, D], F32, kind="ExternalInput").ap()
    wk_d = nc.dram_tensor("wk", [D, D], F32, kind="ExternalInput").ap()
    wv_d = nc.dram_tensor("wv", [D, D], F32, kind="ExternalInput").ap()
    wo_d = nc.dram_tensor("wo", [D, D], F32, kind="ExternalInput").ap()
    bq_d = nc.dram_tensor("bq", [1, D], F32, kind="ExternalInput").ap()
    bk_d = nc.dram_tensor("bk", [1, D], F32, kind="ExternalInput").ap()
    bv_d = nc.dram_tensor("bv", [1, D], F32, kind="ExternalInput").ap()
    bo_d = nc.dram_tensor("bo", [1, D], F32, kind="ExternalInput").ap()
    wr_d = nc.dram_tensor("wr", [D, E], F32, kind="ExternalInput").ap()
    wg_d = nc.dram_tensor("wg", [E, D, FF], F32, kind="ExternalInput").ap()
    wu_d = nc.dram_tensor("wu", [E, D, FF], F32, kind="ExternalInput").ap()
    wd_d = nc.dram_tensor("wd", [E, FF, D], F32, kind="ExternalInput").ap()
    id_d = nc.dram_tensor("ident", [P, P], F32, kind="ExternalInput").ap()
    out_d = nc.dram_tensor("out", [T, D], F32, kind="ExternalOutput").ap()

    xh2_s = nc.dram_tensor("xh2_s", [D, T], BF16).ap()
    qpt_s = nc.dram_tensor("qpt_s", [T, D], BF16).ap()
    x1_s = nc.dram_tensor("x1_s", [T, D], F32).ap()
    x2_s = nc.dram_tensor("x2_s", [T, D], F32).ap()
    kvi_s = nc.dram_tensor("kvi_s", [P, 2080], F32)
    kvo_s = nc.dram_tensor("kvo_s", [P, 2080], F32)

    from contextlib import ExitStack
    with ExitStack() as st:
        ar1 = st.enter_context(nc.sbuf_tensor("ar1", [P, 16384], F32))
        ar2 = st.enter_context(nc.sbuf_tensor("ar2", [P, 20480], BF16))
        ar3 = st.enter_context(nc.sbuf_tensor("ar3", [P, 30720], BF16))
        ar4 = st.enter_context(nc.sbuf_tensor("ar4", [P, 3840], F32))
        ar5 = st.enter_context(nc.sbuf_tensor("ar5", [P, 6144], BF16))
        biasrow = st.enter_context(nc.sbuf_tensor("biasrow", [1, D], BF16))
        identB = st.enter_context(nc.sbuf_tensor("identB", [P, P], BF16))
        ones_r = st.enter_context(nc.sbuf_tensor("ones_r", [1, P], BF16))
        ones_c = st.enter_context(nc.sbuf_tensor("ones_c", [P, 1], BF16))
        wrt = st.enter_context(nc.sbuf_tensor("wrt", [P, P], BF16))

        pf = [st.enter_context(nc.psum_tensor(f"pf{i}", [P, 512], F32)) for i in range(6)]
        pb = [st.enter_context(nc.psum_tensor(f"pb{i}", [P, 1024], BF16)) for i in range(2)]
        cc_sem = st.enter_context(nc.semaphore("cc_sem"))

        ar1b = ar1.bitcast(BF16)
        ar2f = ar2.bitcast(F32)
        ar3f = ar3.bitcast(F32)

        ctx = Ctx(nc, st)
        block = st.enter_context(nc.Block())

        def slotX(tc8):
            return ar1[:, tc8 * D:(tc8 + 1) * D]

        def xh2f(dc):
            return ar1b[:, dc * D:(dc + 1) * D]

        def wovw(vc):
            return ar1b[:, vc * D:(vc + 1) * D]

        def xh(dc):
            return ar2[:, dc * TH:(dc + 1) * TH]

        def xhtm(i):
            return ar2[:, 16384 + i * D: 16384 + (i + 1) * D]

        def kpblk(tc16):
            return ar2[:, tc16 * 512:(tc16 + 1) * 512]

        def vblk(tc16):
            return ar2[:, 8192 + tc16 * 512: 8192 + (tc16 + 1) * 512]

        def qstg(i):
            return ar2[:, 16384 + i * 512: 16384 + (i + 1) * 512]

        def escr(i):
            return ar2f[:, 8704 + i * 512: 8704 + (i + 1) * 512]

        def qpfm(qc):
            return ar2[:, qc * 512:(qc + 1) * 512]

        def qptm(ts):
            return ar2[:, 8192 + ts * D: 8192 + (ts + 1) * D]

        def atm(i):
            return ar2[:, 16384 + i * D: 16384 + (i + 1) * D]

        def wgv(dc):
            return ar3[:, dc * FF:(dc + 1) * FF]

        def wuv(dc):
            return ar3[:, 12288 + dc * FF: 12288 + (dc + 1) * FF]

        def wdblk(par, fc):
            return ar3[:, 24576 + par * 3072 + fc * 512: 24576 + par * 3072 + (fc + 1) * 512]

        def wblk(par, dc):
            return ar3[:, par * 8192 + dc * 512: par * 8192 + (dc + 1) * 512]

        def afm(vc):
            return ar3[:, vc * 512:(vc + 1) * 512]

        def x1t(i):
            return ar3f[:, 4096 + i * D: 4096 + (i + 1) * D]

        def x2t(i):
            return ar3f[:, 8192 + i * D: 8192 + (i + 1) * D]

        kvsb = ar4[:, 0:2080]
        STAT0 = 2080

        def statc(par, j):
            return ar4[:, STAT0 + par * 8 + j: STAT0 + par * 8 + j + 1]

        WEFF0 = 2096

        def weffc(tc16):
            return ar4[:, WEFF0 + tc16 * E: WEFF0 + (tc16 + 1) * E]

        def gsv(i):
            return ar4[:, 2224 + i * 512: 2224 + (i + 1) * 512]

        def zinvv(par):
            return ar4[:, 3248 + par * 16: 3248 + (par + 1) * 16]

        def rscr(par, j):
            return ar4[:, 3280 + par * 24 + j * 8: 3280 + par * 24 + (j + 1) * 8]

        kvbf = ar5[:, 0:2048]
        zsel = ar5[:, 2048:2304]

        def hidv(fc):
            return ar5[:, fc * TH:(fc + 1) * TH]

        EM = ctx.emit

        EM("gpsimd", lambda e: e.dma_start(out=identB[:], in_=id_d[:, :]), writes=["identB"])
        EM("gpsimd", lambda e: e.dma_start(out=wrt[:].rearrange("p (c e) -> p c e", e=E),
                                           in_=wr_d.rearrange("(c p) e -> p c e", p=P)), writes=["wrt"])
        EM("vector", lambda e: e.memset(ones_r[:], 1.0), writes=["ones_r"])
        EM("vector", lambda e: e.memset(ones_c[:], 1.0), writes=["ones_c"])

        def emit_ln_tc(half, tc8, router, dst_res_prefix, call_tag):
            tc16 = half * 8 + tc8
            par = tc8 % 2
            xt = slotX(tc8)
            xres = f"X{tc8}"
            tmres = f"xhtm{par}"
            tm = xhtm(par)
            sums, sumsq, mcol, vcol = (statc(par, j) for j in range(4))
            rstd, nmr, nm1 = (statc(par, j) for j in range(4, 7))
            sres = f"stat{par}"
            EM("scalar", lambda e: e.activation(tm[:, 0:D], xt, AF.Square, accum_out=sumsq),
               reads=[xres], writes=[tmres, sres + "sq"])
            EM("vector", lambda e: e.reduce_sum(sums, xt, axis=AX.X),
               reads=[xres], writes=[sres + "s"])
            EM("vector", lambda e: e.tensor_scalar(mcol, sums, 1.0 / D, None, op0=ALU.mult),
               reads=[sres + "s"], writes=[sres + "m"])
            EM("vector", lambda e: e.tensor_scalar(vcol, sumsq, 1.0 / D, None, op0=ALU.mult),
               reads=[sres + "sq"], writes=[sres + "v"])
            EM("vector", lambda e: e.scalar_tensor_tensor(vcol, mcol, mcol, vcol, op0=ALU.mult, op1=ALU.subtract),
               reads=[sres + "m", sres + "v"], writes=[sres + "v"])
            EM("vector", lambda e: e.tensor_scalar(vcol, vcol, -1.0, LN_EPS, op0=ALU.mult, op1=ALU.add),
               reads=[sres + "v"], writes=[sres + "v"])
            EM("scalar", lambda e: e.activation(rstd, vcol, AF.Sqrt),
               reads=[sres + "v"], writes=[sres + "r"])
            EM("vector", lambda e: e.reciprocal(rstd, rstd),
               reads=[sres + "r"], writes=[sres + "r"])
            EM("vector", lambda e: e.tensor_scalar(nmr, mcol, rstd, -1.0, op0=ALU.mult, op1=ALU.mult),
               reads=[sres + "m", sres + "r"], writes=[sres + "n"])
            EM("scalar", lambda e: e.activation(tm[:, 0:D], xt, AF.Identity, bias=nmr, scale=rstd),
               reads=[xres, sres + "r", sres + "n"], writes=[tmres])
            for dc in range(NDC):
                bank = dc % 2
                EM("tensor", lambda e, dc=dc, bank=bank: e.transpose(
                    pb[bank][:, 0:P], tm[:, dc * P:(dc + 1) * P], identB[:]),
                    reads=[tmres, "identB"], acquire=[f"pb{bank}"], release=[f"pb{bank}"])
                EM("vector", lambda e, dc=dc, bank=bank: e.tensor_copy(
                    xh(dc)[:, tc8 * P:(tc8 + 1) * P], pb[bank][:, 0:P]),
                    reads=[f"pb{bank}"], writes=[f"{dst_res_prefix}{dc}c{tc8}"])
            if router:
                ps = pf[4]
                for dc in range(NDC):
                    kw = dict(reads=[f"{dst_res_prefix}{dc}c{tc8}", "wrt"])
                    if dc == 0:
                        kw["acquire"] = ["pf4"]
                    if dc == NDC - 1:
                        kw["release"] = ["pf4"]
                    EM("tensor", lambda e, dc=dc: e.matmul(
                        ps[:, 0:E], xh(dc)[:, tc8 * P:(tc8 + 1) * P], wrt[:, dc * E:(dc + 1) * E],
                        start=(dc == 0), stop=(dc == NDC - 1)), **kw)
                m1 = rscr(par, 0)[:, 0:1]
                m2c = rscr(par, 0)[:, 2:3]
                nm1c = statc(par, 6)
                msk = rscr(par, 1)
                unn = rscr(par, 2)
                den = rscr(par, 0)[:, 1:2]
                rr = f"rt{par}"
                EM("vector", lambda e: e.reduce_max(m1, ps[0:P, 0:E], axis=AX.X),
                   reads=["pf4"], writes=[rr + "m1"])
                EM("vector", lambda e: e.tensor_scalar(msk, ps[0:P, 0:E], m1, None, op0=ALU.is_ge),
                   reads=["pf4", rr + "m1"], writes=[rr + "msk"])
                EM("vector", lambda e: e.scalar_tensor_tensor(msk, msk, -1e30, ps[0:P, 0:E], op0=ALU.mult, op1=ALU.add),
                   reads=["pf4", rr + "msk"], writes=[rr + "msk"])
                EM("vector", lambda e: e.reduce_max(m2c, msk, axis=AX.X),
                   reads=[rr + "msk"], writes=[rr + "m2"])
                EM("vector", lambda e: e.tensor_scalar(msk, ps[0:P, 0:E], m2c, None, op0=ALU.is_ge),
                   reads=["pf4", rr + "m2"], writes=[rr + "msk"])
                EM("vector", lambda e: e.tensor_scalar(nm1c, m1, -1.0, None, op0=ALU.mult),
                   reads=[rr + "m1"], writes=[rr + "nm1"])
                EM("scalar", lambda e: e.activation(unn, ps[0:P, 0:E], AF.Exp, bias=nm1c),
                   reads=["pf4", rr + "nm1"], writes=[rr + "unn"])
                EM("vector", lambda e: e.tensor_tensor(unn, unn, msk, op=ALU.mult),
                   reads=[rr + "unn", rr + "msk"], writes=[rr + "unn"])
                EM("vector", lambda e: e.reduce_sum(den, unn, axis=AX.X),
                   reads=[rr + "unn"], writes=[rr + "den"])
                EM("vector", lambda e: e.reciprocal(den, den),
                   reads=[rr + "den"], writes=[rr + "den"])
                EM("vector", lambda e: e.tensor_scalar(weffc(tc16), unn, den, 0.5, op0=ALU.mult, op1=ALU.mult),
                   reads=[rr + "unn", rr + "den"], writes=[f"weff{tc16}"])

        def emit_ffn_half(half, x_src, store_out):
            for tc8 in range(8):
                tc16 = half * 8 + tc8
                EM("sync", lambda e, tc8=tc8, tc16=tc16: e.dma_start(
                    out=slotX(tc8), in_=x_src[tc16 * P:(tc16 + 1) * P, :]),
                    writes=[f"X{tc8}"], reads=[f"SRC{tc16}"])
            for tc8 in range(8):
                emit_ln_tc(half, tc8, router=True, dst_res_prefix="xh", call_tag="ffn")

            def load_gu(e_i):
                for dc in range(NDC):
                    EM("gpsimd", lambda e, dc=dc, e_i=e_i: e.dma_start(
                        out=wgv(dc)[:], in_=wg_d[e_i, dc * P:(dc + 1) * P, :]), writes=[f"wg{dc}"])
                for dc in range(NDC):
                    EM("gpsimd", lambda e, dc=dc, e_i=e_i: e.dma_start(
                        out=wuv(dc)[:], in_=wu_d[e_i, dc * P:(dc + 1) * P, :]), writes=[f"wu{dc}"])

            def load_wd(e_i, og, par):
                for fc in range(NFC):
                    EM("gpsimd", lambda e, fc=fc, e_i=e_i, og=og, par=par: e.dma_start(
                        out=wdblk(par, fc)[:], in_=wd_d[e_i, fc * P:(fc + 1) * P, og * 512:(og + 1) * 512]),
                        writes=[f"wd{par}_{fc}"])

            for e_i in range(E):
                if e_i == 0:
                    load_gu(0)
                for tt in range(2):
                    xh_reads = [f"xh{dc}c{tc8}" for dc in range(1) for tc8 in range(4 * tt, 4 * tt + 4)]
                    for fc in range(NFC):
                        gpar = fc % 2
                        gsb = gsv(gpar)
                        for dc in range(NDC):
                            kw = dict(reads=[f"wg{dc}"] + [f"xh{dc}c{tc8}" for tc8 in range(4 * tt, 4 * tt + 4)])
                            if dc == 0:
                                kw["acquire"] = ["pf0"]
                            if dc == NDC - 1:
                                kw["release"] = ["pf0"]
                            EM("tensor", lambda e, dc=dc, fc=fc, tt=tt: e.matmul(
                                pf[0][:, 0:512], wgv(dc)[:, fc * P:(fc + 1) * P],
                                xh(dc)[:, tt * 512:(tt + 1) * 512],
                                start=(dc == 0), stop=(dc == NDC - 1)), **kw)
                        EM("scalar", lambda e, gsb=gsb: e.activation(gsb, pf[0][:, 0:512], AF.Silu),
                           reads=["pf0"], writes=[f"gs{gpar}"])
                        for dc in range(NDC):
                            kw = dict(reads=[f"wu{dc}"] + [f"xh{dc}c{tc8}" for tc8 in range(4 * tt, 4 * tt + 4)])
                            if dc == 0:
                                kw["acquire"] = ["pf1"]
                            if dc == NDC - 1:
                                kw["release"] = ["pf1"]
                            EM("tensor", lambda e, dc=dc, fc=fc, tt=tt: e.matmul(
                                pf[1][:, 0:512], wuv(dc)[:, fc * P:(fc + 1) * P],
                                xh(dc)[:, tt * 512:(tt + 1) * 512],
                                start=(dc == 0), stop=(dc == NDC - 1)), **kw)
                        EM("vector", lambda e, fc=fc, tt=tt, gsb=gsb: e.tensor_tensor(
                            hidv(fc)[:, tt * 512:(tt + 1) * 512], pf[1][:, 0:512], gsb, op=ALU.mult),
                            reads=["pf1", f"gs{gpar}"], writes=[f"hid{fc}t{tt}"])
                load_wd(e_i, 0, 0)
                load_wd(e_i, 1, 1)
                for og in range(NOG):
                    par = og % 2
                    for tc8 in range(8):
                        tt = tc8 // 4
                        bank = 2 + (tc8 % 2)
                        for fc in range(NFC):
                            kw = dict(reads=[f"wd{par}_{fc}", f"hid{fc}t{tt}"])
                            if fc == 0:
                                kw["acquire"] = [f"pf{bank}"]
                            if fc == NFC - 1:
                                kw["release"] = [f"pf{bank}"]
                            EM("tensor", lambda e, fc=fc, tc8=tc8, og=og, par=par, bank=bank: e.matmul(
                                pf[bank][:, 0:512],
                                hidv(fc)[:, tc8 * P:(tc8 + 1) * P],
                                wdblk(par, fc)[:],
                                start=(fc == 0), stop=(fc == NFC - 1)), **kw)
                        tc16 = half * 8 + tc8
                        EM("vector", lambda e, tc8=tc8, og=og, bank=bank, e_i=e_i, tc16=tc16: e.scalar_tensor_tensor(
                            slotX(tc8)[:, og * 512:(og + 1) * 512],
                            pf[bank][:, 0:512],
                            weffc(tc16)[:, e_i:e_i + 1],
                            slotX(tc8)[:, og * 512:(og + 1) * 512],
                            op0=ALU.mult, op1=ALU.add),
                            reads=[f"pf{bank}", f"weff{tc16}", f"X{tc8}"], writes=[f"X{tc8}"])
                    if og + 2 < NOG:
                        load_wd(e_i, og + 2, (og + 2) % 2)
                    if og == 1 and e_i + 1 < E:
                        load_gu(e_i + 1)
            if store_out is not None:
                for tc8 in range(8):
                    tc16 = half * 8 + tc8
                    EM("sync", lambda e, tc8=tc8, tc16=tc16: e.dma_start(
                        out=store_out[tc16 * P:(tc16 + 1) * P, :], in_=slotX(tc8)),
                        reads=[f"X{tc8}"], writes=[f"OUT{tc16}"])

        def emit_phaseA_half(half):
            emit_ffn_half(half, x_d, None)
            for tc8 in range(8):
                emit_ln_tc(half, tc8, router=False, dst_res_prefix="xh", call_tag="ln2")
            for dc in range(NDC):
                EM("sync", lambda e, dc=dc, half=half: e.dma_start(
                    out=xh2_s[dc * P:(dc + 1) * P, half * TH:(half + 1) * TH], in_=xh(dc)[:]),
                    reads=[f"xh{dc}c{tc8}" for tc8 in range(8)], writes=[f"XH2d{dc}h{half}"])
            for tc8 in range(8):
                tc16 = half * 8 + tc8
                EM("sync", lambda e, tc8=tc8, tc16=tc16: e.dma_start(
                    out=x1_s[tc16 * P:(tc16 + 1) * P, :], in_=slotX(tc8)),
                    reads=[f"X{tc8}"], writes=[f"X1S{tc16}"])

        def emit_attnproj():
            ctx.region_switch([f"xh2f{dc}" for dc in range(NDC)],
                              [f"X{t}" for t in range(8)])
            ctx.region_switch([f"kp{t}" for t in range(16)] + [f"v{t}" for t in range(16)]
                              + ["qstg0", "qstg1", "escr0", "escr1"],
                              [f"xh{dc}c{tc8}" for dc in range(NDC) for tc8 in range(8)]
                              + ["xhtm0", "xhtm1"])
            ctx.region_switch(["wblk0", "wblk1"],
                              [f"wg{dc}" for dc in range(NDC)] + [f"wu{dc}" for dc in range(NDC)]
                              + [f"wd{p}_{fc}" for p in range(2) for fc in range(NFC)])
            ctx.region_switch([f"pf3h{h}" for h in range(16)], ["pf3"])
            ctx.region_switch([f"pf4h{hh}" for hh in range(4)], ["pf4"])
            ctx.region_switch([f"pf5h{hh}" for hh in range(4)], ["pf5"])
            for dc in range(NDC):
                EM("sync", lambda e, dc=dc: e.dma_start(out=xh2f(dc), in_=xh2_s[dc * P:(dc + 1) * P, :]),
                   reads=[f"XH2d{dc}h0", f"XH2d{dc}h1"], writes=[f"xh2f{dc}"])

            def load_wblk(src, og, par):
                EM("gpsimd", lambda e: e.dma_start(
                    out=ar3[:, par * 8192:(par + 1) * 8192].rearrange("p (dc c) -> p dc c", c=512),
                    in_=src[:, og * 512:(og + 1) * 512].rearrange("(dc p) c -> p dc c", p=P)),
                    writes=[f"wblk{par}"])

            def proj_group(tc16, par, bank, bias_slot):
                for dc in range(NDC):
                    kw = dict(reads=[f"xh2f{dc}", f"wblk{par}"])
                    if dc == 0:
                        kw["acquire"] = [f"pf{bank}"]
                    EM("tensor", lambda e, dc=dc, tc16=tc16, par=par, bank=bank: e.matmul(
                        pf[bank][:, 0:512], xh2f(dc)[:, tc16 * P:(tc16 + 1) * P],
                        wblk(par, dc)[:], start=(dc == 0), stop=False), **kw)
                EM("tensor", lambda e, tc16=tc16, bank=bank, bias_slot=bias_slot: e.matmul(
                    pf[bank][:, 0:512], ones_r[0:1, 0:P],
                    biasrow[0:1, bias_slot * 512:(bias_slot + 1) * 512],
                    start=False, stop=True),
                    reads=["ones_r", f"bias{bias_slot}"], release=[f"pf{bank}"])

            def elu_evac(bank, dst, dres):
                EM("vector", lambda e: e.tensor_scalar(escr(0), pf[bank][:, 0:512], 0.0, None, op0=ALU.min),
                   reads=[f"pf{bank}"], writes=["escr0"])
                EM("scalar", lambda e: e.activation(escr(0), escr(0), AF.Exp),
                   reads=["escr0"], writes=["escr0"])
                EM("vector", lambda e: e.tensor_scalar(escr(1), pf[bank][:, 0:512], 0.0, None, op0=ALU.max),
                   reads=[f"pf{bank}"], writes=["escr1"])
                EM("vector", lambda e: e.tensor_tensor(dst, escr(0), escr(1), op=ALU.add),
                   reads=["escr0", "escr1"], writes=[dres])

            nbias = [0]

            def load_bias(src, slot, og):
                EM("gpsimd", lambda e: e.dma_start(
                    out=biasrow[0:1, slot * 512:(slot + 1) * 512], in_=src[0:1, og * 512:(og + 1) * 512]),
                    writes=[f"bias{slot}"])

            pidx = 0
            for og in range(NOG):
                load_bias(bq_d, 0, og)
                load_bias(bk_d, 1, og)
                load_bias(bv_d, 2, og)
                load_wblk(wq_d, og, pidx % 2)
                for tc16 in range(16):
                    bank = tc16 % 2
                    proj_group(tc16, pidx % 2, bank, 0)
                    elu_evac(bank, qstg(tc16 % 2)[:], f"qstg{tc16 % 2}")
                    EM("sync", lambda e, tc16=tc16, og=og: e.dma_start(
                        out=qpt_s[tc16 * P:(tc16 + 1) * P, og * 512:(og + 1) * 512],
                        in_=qstg(tc16 % 2)[:]),
                        reads=[f"qstg{tc16 % 2}"], writes=[f"QPT{tc16}o{og}"])
                pidx += 1
                load_wblk(wk_d, og, pidx % 2)
                for tc16 in range(16):
                    bank = tc16 % 2
                    proj_group(tc16, pidx % 2, bank, 1)
                    elu_evac(bank, kpblk(tc16)[:], f"kp{tc16}")
                pidx += 1
                load_wblk(wv_d, og, pidx % 2)
                for tc16 in range(16):
                    bank = tc16 % 2
                    proj_group(tc16, pidx % 2, bank, 2)
                    EM("vector", lambda e, tc16=tc16, bank=bank: e.tensor_copy(vblk(tc16)[:], pf[bank][:, 0:512]),
                       reads=[f"pf{bank}"], writes=[f"v{tc16}"])
                pidx += 1
                kvbank = 4 + (og % 2)
                for hh in range(4):
                    for tc16 in range(16):
                        kw = dict(reads=[f"kp{tc16}", f"v{tc16}"])
                        if tc16 == 0:
                            kw["acquire"] = [f"pf{kvbank}h{hh}"]
                        if tc16 == 15:
                            kw["release"] = [f"pf{kvbank}h{hh}"]
                        EM("tensor", lambda e, hh=hh, tc16=tc16, kvbank=kvbank: e.matmul(
                            pf[kvbank][:, hh * P:(hh + 1) * P],
                            kpblk(tc16)[:, hh * P:(hh + 1) * P],
                            vblk(tc16)[:, hh * P:(hh + 1) * P],
                            start=(tc16 == 0), stop=(tc16 == 15)), **kw)
                for hh in range(4):
                    for tc16 in range(16):
                        kw = dict(reads=[f"kp{tc16}", "ones_c"])
                        if tc16 == 0:
                            kw["acquire"] = [f"pf3h{og * 4 + hh}"]
                        if tc16 == 15:
                            kw["release"] = [f"pf3h{og * 4 + hh}"]
                        EM("tensor", lambda e, hh=hh, tc16=tc16, og=og: e.matmul(
                            pf[3][:, og * 4 + hh: og * 4 + hh + 1],
                            kpblk(tc16)[:, hh * P:(hh + 1) * P],
                            ones_c[0:P, 0:1],
                            start=(tc16 == 0), stop=(tc16 == 15)), **kw)
                EM("vector", lambda e, og=og, kvbank=kvbank: e.tensor_copy(
                    kvsb[:, og * 512:(og + 1) * 512], pf[kvbank][:, 0:512]),
                    reads=[f"pf{kvbank}h{hh2}" for hh2 in range(4)], writes=[f"kvsb{og}"])
            EM("vector", lambda e: e.tensor_copy(kvsb[:, 2048:2064], pf[3][:, 0:16]),
               reads=[f"pf3h{h2}" for h2 in range(16)], writes=["kvsb_kp"])
            EM("gpsimd", lambda e: e.dma_start(out=kvi_s.ap()[:, :], in_=ar4[:, 0:2080]),
               reads=[f"kvsb{og}" for og in range(4)] + ["kvsb_kp"], writes=["KVI"])
            if _SKIP_CC:
                EM("gpsimd", lambda e: e.dma_start(out=kvo_s.ap()[:, :], in_=kvi_s.ap()[:, :]),
                   reads=["KVI"], writes=["KVO"])
            else:
                EM("gpsimd", lambda e: e.collective_compute(
                    "AllReduce", ALU.add,
                    replica_groups=[[0, 1], [2, 3], [4, 5], [6, 7]],
                    ins=[kvi_s.ap().opt()], outs=[kvo_s.ap().opt()]).then_inc(cc_sem),
                    reads=["KVI"], writes=[], inc=0)
                EM("gpsimd", lambda e: (e.wait_ge(cc_sem, 1), e.nop())[1],
                   reads=[], writes=["KVO"], inc=16)
            ctx.region_switch(["kvsball"], [f"kvsb{og}" for og in range(4)] + ["kvsb_kp"])
            EM("gpsimd", lambda e: e.dma_start(out=ar4[:, 0:2080], in_=kvo_s.ap()[:, :]),
               reads=["KVO"], writes=["kvsball"])

        def emit_phaseB():
            ctx.region_switch([f"wo{vc}" for vc in range(NDC)],
                              [f"xh2f{dc}" for dc in range(NDC)])
            ctx.region_switch([f"qpfm{qc}" for qc in range(NDC)] + [f"qptm{ts}" for ts in range(4)]
                              + ["atm0", "atm1"],
                              [f"kp{t}" for t in range(16)] + [f"v{t}" for t in range(16)]
                              + ["qstg0", "qstg1", "escr0", "escr1"])
            ctx.region_switch([f"afm{vc}" for vc in range(NDC)] + ["x1t0", "x1t1", "x2t0", "x2t1"],
                              ["wblk0", "wblk1"])
            ctx.region_switch(["kvbf", "zsel"], [f"hid{fc}t{tt}" for fc in range(NFC) for tt in range(2)])
            ctx.region_switch(["pf3"], [f"pf3h{h}" for h in range(16)])
            ctx.region_switch(["pf4"], [f"pf4h{hh}" for hh in range(4)])
            ctx.region_switch(["pf5"], [f"pf5h{hh}" for hh in range(4)])
            EM("vector", lambda e: e.tensor_copy(kvbf[:], kvsb[:, 0:2048]),
               reads=["kvsball"], writes=["kvbf"])
            EM("vector", lambda e: e.memset(zsel[:], 0.0), writes=["zsel"])
            for h in range(H):
                EM("vector", lambda e, h=h: e.tensor_copy(zsel[:, h * 16 + h: h * 16 + h + 1],
                                                          kvsb[:, 2048 + h: 2048 + h + 1]),
                   reads=["kvsball"], writes=["zsel"])
            EM("gpsimd", lambda e: e.dma_start(out=biasrow[0:1, :], in_=bo_d[0:1, :]),
               writes=["bias0", "bias1", "bias2", "bias3"])
            for vc in range(NDC):
                EM("gpsimd", lambda e, vc=vc: e.dma_start(out=wovw(vc), in_=wo_d[vc * P:(vc + 1) * P, :]),
                   writes=[f"wo{vc}"])

            for tt in range(4):
                for ts in range(4):
                    tc16 = tt * 4 + ts
                    EM("sync", lambda e, ts=ts, tc16=tc16: e.dma_start(
                        out=qptm(ts)[:], in_=qpt_s[tc16 * P:(tc16 + 1) * P, :]),
                        reads=[f"QPT{tc16}o{og}" for og in range(4)], writes=[f"qptm{ts}"])
                    for qc in range(NDC):
                        bank = qc % 2
                        EM("tensor", lambda e, qc=qc, ts=ts, bank=bank: e.transpose(
                            pb[bank][:, 0:P], qptm(ts)[:, qc * P:(qc + 1) * P], identB[:]),
                            reads=[f"qptm{ts}", "identB"], acquire=[f"pb{bank}"], release=[f"pb{bank}"])
                        EM("vector", lambda e, qc=qc, ts=ts, bank=bank: e.tensor_copy(
                            qpfm(qc)[:, ts * P:(ts + 1) * P], pb[bank][:, 0:P]),
                            reads=[f"pb{bank}"], writes=[f"qpfm{qc}ts{ts}"])
                    zpar = ts % 2
                    for qc in range(NDC):
                        kw = dict(reads=[f"qpfm{qc}ts{ts}", "zsel"])
                        if qc == 0:
                            kw["acquire"] = ["pf0"]
                        if qc == NDC - 1:
                            kw["release"] = ["pf0"]
                        EM("tensor", lambda e, qc=qc, ts=ts: e.matmul(
                            pf[0][:, 0:16], qpfm(qc)[:, ts * P:(ts + 1) * P],
                            zsel[:, qc * 16:(qc + 1) * 16],
                            start=(qc == 0), stop=(qc == NDC - 1)), **kw)
                    EM("vector", lambda e, zpar=zpar: e.tensor_scalar(zinvv(zpar), pf[0][:, 0:16], Z_EPS, None, op0=ALU.add),
                       reads=["pf0"], writes=[f"zinv{zpar}"])
                    EM("vector", lambda e, zpar=zpar: e.reciprocal(zinvv(zpar), zinvv(zpar)),
                       reads=[f"zinv{zpar}"], writes=[f"zinv{zpar}"])
                    apar = ts % 2
                    for og in range(NOG):
                        bank = 1 + (og % 2)
                        for hh in range(4):
                            h = og * 4 + hh
                            kw = dict(reads=[f"qpfm{h}ts{ts}", "kvbf"])
                            if hh == 0:
                                kw["acquire"] = [f"pf{bank}"]
                            if hh == 3:
                                kw["release"] = [f"pf{bank}"]
                            EM("tensor", lambda e, h=h, hh=hh, ts=ts, bank=bank: e.matmul(
                                pf[bank][:, hh * P:(hh + 1) * P],
                                qpfm(h)[:, ts * P:(ts + 1) * P],
                                kvbf[:, h * P:(h + 1) * P],
                                start=(hh == 0), stop=(hh == 3)), **kw)
                        for hh in range(4):
                            h = og * 4 + hh
                            EM("vector", lambda e, h=h, hh=hh, og=og, bank=bank, apar=apar, zpar=zpar: e.tensor_scalar(
                                atm(apar)[:, h * P:(h + 1) * P], pf[bank][:, hh * P:(hh + 1) * P],
                                zinvv(zpar)[:, h:h + 1], None, op0=ALU.mult),
                                reads=[f"pf{bank}", f"zinv{zpar}"], writes=[f"atm{apar}b{h}"])
                    for vc in range(NDC):
                        bank = vc % 2
                        EM("tensor", lambda e, vc=vc, apar=apar, bank=bank: e.transpose(
                            pb[bank][:, 0:P], atm(apar)[:, vc * P:(vc + 1) * P], identB[:]),
                            reads=[f"atm{apar}b{vc}", "identB"], acquire=[f"pb{bank}"], release=[f"pb{bank}"])
                        EM("vector", lambda e, vc=vc, ts=ts, bank=bank: e.tensor_copy(
                            afm(vc)[:, ts * P:(ts + 1) * P], pb[bank][:, 0:P]),
                            reads=[f"pb{bank}"], writes=[f"afm{vc}ts{ts}"])
                    xpar = ts % 2
                    EM("sync", lambda e, xpar=xpar, tc16=tc16: e.dma_start(
                        out=x1t(xpar), in_=x1_s[tc16 * P:(tc16 + 1) * P, :]),
                        reads=[f"X1S{tc16}"], writes=[f"x1t{xpar}"])
                    for og in range(NOG):
                        bank = 3 + (og % 2)
                        for vc in range(NDC):
                            kw = dict(reads=[f"afm{vc}ts{ts}", f"wo{vc}"])
                            if vc == 0:
                                kw["acquire"] = [f"pf{bank}"]
                            EM("tensor", lambda e, vc=vc, ts=ts, og=og, bank=bank: e.matmul(
                                pf[bank][:, 0:512], afm(vc)[:, ts * P:(ts + 1) * P],
                                wovw(vc)[:, og * 512:(og + 1) * 512],
                                start=(vc == 0), stop=False), **kw)
                        EM("tensor", lambda e, og=og, bank=bank: e.matmul(
                            pf[bank][:, 0:512], ones_r[0:1, 0:P],
                            biasrow[0:1, og * 512:(og + 1) * 512],
                            start=False, stop=True),
                            reads=["ones_r", f"bias{og}"], release=[f"pf{bank}"])
                        EM("vector", lambda e, og=og, bank=bank, xpar=xpar: e.tensor_tensor(
                            x2t(xpar)[:, og * 512:(og + 1) * 512], pf[bank][:, 0:512],
                            x1t(xpar)[:, og * 512:(og + 1) * 512], op=ALU.add),
                            reads=[f"pf{bank}", f"x1t{xpar}"], writes=[f"x2t{xpar}"])
                    EM("sync", lambda e, xpar=xpar, tc16=tc16: e.dma_start(
                        out=x2_s[tc16 * P:(tc16 + 1) * P, :], in_=x2t(xpar)),
                        reads=[f"x2t{xpar}"], writes=[f"X2S{tc16}", f"SRC2_{tc16}"])

        def emit_phaseC():
            ctx.region_switch([f"X{t}" for t in range(8)], [f"wo{vc}" for vc in range(NDC)])
            ctx.region_switch([f"xh{dc}c{tc8}" for dc in range(NDC) for tc8 in range(8)]
                              + ["xhtm0", "xhtm1"],
                              [f"qpfm{qc}ts{ts}" for qc in range(NDC) for ts in range(4)]
                              + [f"qptm{ts}" for ts in range(4)]
                              + [f"atm{i}b{h}" for i in range(2) for h in range(H)])
            ctx.region_switch([f"wg{dc}" for dc in range(NDC)] + [f"wu{dc}" for dc in range(NDC)]
                              + [f"wd{p}_{fc}" for p in range(2) for fc in range(NFC)],
                              [f"afm{vc}ts{ts}" for vc in range(NDC) for ts in range(4)]
                              + ["x1t0", "x1t1", "x2t0", "x2t1"])
            ctx.region_switch([f"hid{fc}t{tt}" for fc in range(NFC) for tt in range(2)],
                              ["kvbf", "zsel"])
            for half in range(2):
                for tc8 in range(8):
                    tc16 = half * 8 + tc8
                    ctx.region_switch([f"SRC{tc16}"], [f"SRC2_{tc16}"])
                emit_ffn_half(half, x2_s, out_d)

        emit_phaseA_half(0)
        emit_phaseA_half(1)
        if _BUILD_UPTO == "ffn1":
            for tc16 in range(16):
                EM("sync", lambda e, tc16=tc16: e.dma_start(
                    out=out_d[tc16 * P:(tc16 + 1) * P, :], in_=x1_s[tc16 * P:(tc16 + 1) * P, :]),
                    reads=[f"X1S{tc16}"], writes=[])
        else:
            emit_attnproj()
            emit_phaseB()
            if _BUILD_UPTO == "attn":
                for tc16 in range(16):
                    EM("sync", lambda e, tc16=tc16: e.dma_start(
                        out=out_d[tc16 * P:(tc16 + 1) * P, :], in_=x2_s[tc16 * P:(tc16 + 1) * P, :]),
                        reads=[f"X2S{tc16}"], writes=[])
            else:
                emit_phaseC()

        ctx.final_wait("sync", ["sync", "gpsimd"])
        ctx.run(block)
    return nc, ctx


def kernel(L0_attn_hidden_states, L0_attn_q_w, L0_attn_q_b, L0_attn_k_w, L0_attn_k_b,
           L0_attn_v_w, L0_attn_v_b, L0_attn_o_w, L0_attn_o_b,
           L0_ffn_w_gate_router, L0_ffn_w_gate_proj, L0_ffn_w_up_proj, L0_ffn_w_down_proj):
    f32 = np.float32
    hs = np.asarray(L0_attn_hidden_states, f32)
    c = np.ascontiguousarray

    wq = c(np.asarray(L0_attn_q_w, f32).T)
    wk = c(np.asarray(L0_attn_k_w, f32).T)
    wv = c(np.asarray(L0_attn_v_w, f32).T)
    wo = c(np.asarray(L0_attn_o_w, f32).T)
    bq = c(np.asarray(L0_attn_q_b, f32).reshape(1, D))
    bk = c(np.asarray(L0_attn_k_b, f32).reshape(1, D))
    bv = c(np.asarray(L0_attn_v_b, f32).reshape(1, D))
    bo = c(np.asarray(L0_attn_o_b, f32).reshape(1, D))
    wr = c(np.asarray(L0_ffn_w_gate_router, f32).T)
    wg = c(np.asarray(L0_ffn_w_gate_proj, f32).transpose(0, 2, 1))
    wu = c(np.asarray(L0_ffn_w_up_proj, f32).transpose(0, 2, 1))
    wd = c(np.asarray(L0_ffn_w_down_proj, f32).transpose(0, 2, 1))
    ident = np.eye(P, dtype=f32)

    nc, _ = build_program()
    shared = {"wq": wq, "wk": wk, "wv": wv, "wo": wo,
              "bq": bq, "bk": bk, "bv": bv, "bo": bo,
              "wr": wr, "wg": wg, "wu": wu, "wd": wd, "ident": ident}
    in_maps = []
    for core in range(8):
        b, h = core // 2, core % 2
        xc = c(hs[b, h * T:(h + 1) * T, :])
        in_maps.append({"x": xc, **shared})
    res = run_bass_kernel_spmd(nc, in_maps, core_ids=list(range(8)))
    out = np.empty((4, 4096, D), f32)
    for core in range(8):
        b, h = core // 2, core % 2
        out[b, h * T:(h + 1) * T, :] = res.results[core]["out"]
    return out


# version 04
# speedup vs baseline: 1.3522x; 1.3522x over previous
import numpy as np
import concourse.bass as bass
import concourse.mybir as mybir
from concourse.bass_utils import run_bass_kernel_spmd

F32 = mybir.dt.float32
BF16 = mybir.dt.bfloat16
AF = mybir.ActivationFunctionType
ALU = mybir.AluOpType
AX = mybir.AxisListType

P = 128
T = 2048
TH = 1024
D = 2048
FF = 768
E = 8
H = 16
NDC = D // P
NFC = FF // P
NOG = D // 512
LN_EPS = 1e-5
Z_EPS = 1e-6

_BUILD_UPTO = "full"
_SERIALIZE = False
_DMA_CONSERVATIVE = True
_SKIP_CC = False


class Ctx:

    ENGINES = ("sync", "gpsimd", "vector", "scalar", "tensor")
    DMA_ENGINES = ("sync", "gpsimd")

    def __init__(self, nc, stack):
        self.nc = nc
        self.sems = {e: stack.enter_context(nc.semaphore(f"sem_{e}")) for e in self.ENGINES}
        self.cnt = {e: 0 for e in self.ENGINES}
        self.last_wait = {}
        self.writer = {}
        self.readers = {}
        self.plans = {e: [] for e in self.ENGINES}
        self.n_waits = 0
        self.n_instr = 0

    def _need(self, needs, eng, cnt):
        if cnt > needs.get(eng, 0):
            needs[eng] = cnt

    serialize_now = False

    def emit(self, eng, fn, reads=(), writes=(), acquire=(), release=(), inc=None):
        needs = {}
        if _SERIALIZE or self.serialize_now:
            for teng in self.ENGINES:
                if self.cnt[teng] > 0:
                    needs[teng] = self.cnt[teng]
        for r in reads:
            for weng, wcnt in self.writer.get(r, {}).items():
                self._need(needs, weng, wcnt)
        for wres in tuple(writes) + tuple(acquire):
            for weng, wcnt in self.writer.get(wres, {}).items():
                self._need(needs, weng, wcnt)
            for reng, rcnt in self.readers.get(wres, {}).items():
                self._need(needs, reng, rcnt)
        if _DMA_CONSERVATIVE:
            for teng in self.DMA_ENGINES:
                if teng in needs:
                    needs[teng] = self.cnt[teng]
        wl = []
        for teng, val in needs.items():
            key = (eng, teng)
            if self.last_wait.get(key, 0) >= val:
                continue
            self.last_wait[key] = val
            wl.append((self.sems[teng], val))
        if inc is None:
            inc = 16 if eng in self.DMA_ENGINES else 1
        self.cnt[eng] += inc
        my = self.cnt[eng]
        sem = self.sems[eng]
        self.n_waits += len(wl)
        self.n_instr += 1

        def thunk(e, fn=fn, wl=wl, sem=sem, inc=inc):
            for s, v in wl:
                e.wait_ge(s, v)
            ins = fn(e)
            if inc:
                ins.then_inc(sem, inc)

        self.plans[eng].append(thunk)
        for r in reads:
            self.readers.setdefault(r, {})[eng] = my
        for wres in tuple(writes) + tuple(release):
            self.writer[wres] = {eng: my}
            self.readers[wres] = {}
        return my

    def region_switch(self, news, olds):
        merged = {}
        for o in olds:
            for eng, cnt in self.writer.pop(o, {}).items():
                if cnt > merged.get(eng, 0):
                    merged[eng] = cnt
            for eng, cnt in self.readers.pop(o, {}).items():
                if cnt > merged.get(eng, 0):
                    merged[eng] = cnt
        for n in news:
            w = dict(merged)
            for eng, cnt in self.writer.get(n, {}).items():
                if cnt > w.get(eng, 0):
                    w[eng] = cnt
            for eng, cnt in self.readers.get(n, {}).items():
                if cnt > w.get(eng, 0):
                    w[eng] = cnt
            self.writer[n] = w
            self.readers[n] = {}

    def final_wait(self, eng, targets):
        wl = [(self.sems[t], self.cnt[t]) for t in targets if self.cnt[t] > 0]

        def thunk(e, wl=wl):
            for s, v in wl:
                e.wait_ge(s, v)

        self.plans[eng].append(thunk)

    def run(self, block):
        for name in self.ENGINES:
            plan = self.plans[name]

            def make(plan):
                def body(e):
                    for t in plan:
                        t(e)
                return body

            getattr(block, name)(make(plan))


def build_program():
    nc = bass.Bass("TRN2", target_bir_lowering=False, debug=False)

    x_d = nc.dram_tensor("x", [T, D], F32, kind="ExternalInput").ap()
    wq_d = nc.dram_tensor("wq", [NOG, P, 8192], F32, kind="ExternalInput").ap()
    wk_d = nc.dram_tensor("wk", [NOG, P, 8192], F32, kind="ExternalInput").ap()
    wv_d = nc.dram_tensor("wv", [NOG, P, 8192], F32, kind="ExternalInput").ap()
    wo_d = nc.dram_tensor("wo", [P, 32768], F32, kind="ExternalInput").ap()
    bq_d = nc.dram_tensor("bq", [1, D], F32, kind="ExternalInput").ap()
    bk_d = nc.dram_tensor("bk", [1, D], F32, kind="ExternalInput").ap()
    bv_d = nc.dram_tensor("bv", [1, D], F32, kind="ExternalInput").ap()
    bo_d = nc.dram_tensor("bo", [1, D], F32, kind="ExternalInput").ap()
    wr_d = nc.dram_tensor("wr", [D, E], F32, kind="ExternalInput").ap()
    wg_d = nc.dram_tensor("wg", [E, P, 12288], F32, kind="ExternalInput").ap()
    wu_d = nc.dram_tensor("wu", [E, P, 12288], F32, kind="ExternalInput").ap()
    wd_d = nc.dram_tensor("wd", [E, NOG, P, 3072], F32, kind="ExternalInput").ap()
    id_d = nc.dram_tensor("ident", [P, P], F32, kind="ExternalInput").ap()
    out_d = nc.dram_tensor("out", [T, D], F32, kind="ExternalOutput").ap()

    wg_s = nc.dram_tensor("wg_s", [E, P, 12288], BF16).ap()
    wu_s = nc.dram_tensor("wu_s", [E, P, 12288], BF16).ap()
    wd_s = nc.dram_tensor("wd_s", [E, NOG, P, 3072], BF16).ap()
    xh2_s = nc.dram_tensor("xh2_s", [D, T], BF16).ap()
    qpt_s = nc.dram_tensor("qpt_s", [T, D], BF16).ap()
    x1_s = nc.dram_tensor("x1_s", [T, D], F32).ap()
    x2_s = nc.dram_tensor("x2_s", [T, D], F32).ap()
    kvi_s = nc.dram_tensor("kvi_s", [P, 2080], F32)
    kvo_s = nc.dram_tensor("kvo_s", [P, 2080], F32)

    from contextlib import ExitStack
    with ExitStack() as st:
        ar1 = st.enter_context(nc.sbuf_tensor("ar1", [P, 16384], F32))
        ar2 = st.enter_context(nc.sbuf_tensor("ar2", [P, 20480], BF16))
        ar3 = st.enter_context(nc.sbuf_tensor("ar3", [P, 30720], BF16))
        ar4 = st.enter_context(nc.sbuf_tensor("ar4", [P, 3840], F32))
        ar5 = st.enter_context(nc.sbuf_tensor("ar5", [P, 6144], BF16))
        biasrow = st.enter_context(nc.sbuf_tensor("biasrow", [1, D], BF16))
        identB = st.enter_context(nc.sbuf_tensor("identB", [P, P], BF16))
        ones_r = st.enter_context(nc.sbuf_tensor("ones_r", [1, P], BF16))
        ones_c = st.enter_context(nc.sbuf_tensor("ones_c", [P, 1], BF16))
        wrt = st.enter_context(nc.sbuf_tensor("wrt", [P, P], BF16))

        pf = [st.enter_context(nc.psum_tensor(f"pf{i}", [P, 512], F32)) for i in range(6)]
        pb = [st.enter_context(nc.psum_tensor(f"pb{i}", [P, 1024], BF16)) for i in range(2)]
        cc_sem = st.enter_context(nc.semaphore("cc_sem"))

        ar1b = ar1.bitcast(BF16)
        ar2f = ar2.bitcast(F32)
        ar3f = ar3.bitcast(F32)
        ar4b = ar4.bitcast(BF16)

        ctx = Ctx(nc, st)
        block = st.enter_context(nc.Block())

        def slotX(tc8):
            return ar1[:, tc8 * D:(tc8 + 1) * D]

        def xh2f(dc):
            return ar1b[:, dc * D:(dc + 1) * D]

        def wovw(vc):
            return ar1b[:, vc * D:(vc + 1) * D]

        def xh(dc):
            return ar2[:, dc * TH:(dc + 1) * TH]

        def xhtm(i):
            return ar2[:, 16384 + i * D: 16384 + (i + 1) * D]

        def kpblk(tc16):
            return ar2[:, tc16 * 512:(tc16 + 1) * 512]

        def vblk(tc16):
            return ar2[:, 8192 + tc16 * 512: 8192 + (tc16 + 1) * 512]

        def qstg(i):
            return ar2[:, 16384 + i * 512: 16384 + (i + 1) * 512]

        def escr(i):
            return ar2f[:, 8704 + i * 512: 8704 + (i + 1) * 512]

        def qpfm(qc):
            return ar2[:, qc * 512:(qc + 1) * 512]

        def qptm(ts):
            return ar2[:, 8192 + ts * D: 8192 + (ts + 1) * D]

        def atm(i):
            return ar2[:, 16384 + i * D: 16384 + (i + 1) * D]

        def wgv(dc):
            return ar3[:, dc * FF:(dc + 1) * FF]

        def wuv(dc):
            return ar3[:, 12288 + dc * FF: 12288 + (dc + 1) * FF]

        def wdblk(par, fc):
            return ar3[:, 24576 + par * 3072 + fc * 512: 24576 + par * 3072 + (fc + 1) * 512]

        def wblk(par, dc):
            return ar3[:, par * 8192 + dc * 512: par * 8192 + (dc + 1) * 512]

        def afm(vc):
            return ar3[:, vc * 512:(vc + 1) * 512]

        def x1t(i):
            return ar3f[:, 4096 + i * D: 4096 + (i + 1) * D]

        def x2t(i):
            return ar3f[:, 8192 + i * D: 8192 + (i + 1) * D]

        kvsb = ar4[:, 0:2080]
        STAT0 = 2080

        def statc(par, j):
            return ar4[:, STAT0 + par * 8 + j: STAT0 + par * 8 + j + 1]

        WEFF0 = 2096

        def weffc(tc16):
            return ar4[:, WEFF0 + tc16 * E: WEFF0 + (tc16 + 1) * E]

        def gsv(i):
            return ar4[:, 2224 + i * 512: 2224 + (i + 1) * 512]

        def zinvv(par):
            return ar4[:, 3248 + par * 16: 3248 + (par + 1) * 16]

        def rscr(par, j):
            return ar4[:, 3280 + par * 24 + j * 8: 3280 + par * 24 + (j + 1) * 8]

        kvbf = ar5[:, 0:2048]
        zsel = ar5[:, 2048:2304]

        def hidv(fc):
            return ar5[:, fc * TH:(fc + 1) * TH]

        EM = ctx.emit

        EM("gpsimd", lambda e: e.dma_start(out=identB[:], in_=id_d[:, :]), writes=["identB"])
        EM("gpsimd", lambda e: e.dma_start(out=wrt[:].rearrange("p (c e) -> p c e", e=E),
                                           in_=wr_d.rearrange("(c p) e -> p c e", p=P)), writes=["wrt"])
        EM("vector", lambda e: e.memset(ones_r[:], 1.0), writes=["ones_r"])
        EM("vector", lambda e: e.memset(ones_c[:], 1.0), writes=["ones_c"])

        def emit_ln_tc(half, tc8, router, dst_res_prefix, call_tag):
            tc16 = half * 8 + tc8
            par = tc8 % 2
            xt = slotX(tc8)
            xres = f"X{tc8}"
            tmres = f"xhtm{par}"
            tm = xhtm(par)
            sums, sumsq, mcol, vcol = (statc(par, j) for j in range(4))
            rstd, nmr, nm1 = (statc(par, j) for j in range(4, 7))
            sres = f"stat{par}"
            EM("scalar", lambda e: e.activation(tm[:, 0:D], xt, AF.Square, accum_out=sumsq),
               reads=[xres], writes=[tmres, sres + "sq"])
            EM("vector", lambda e: e.reduce_sum(sums, xt, axis=AX.X),
               reads=[xres], writes=[sres + "s"])
            EM("vector", lambda e: e.tensor_scalar(mcol, sums, 1.0 / D, None, op0=ALU.mult),
               reads=[sres + "s"], writes=[sres + "m"])
            EM("vector", lambda e: e.tensor_scalar(vcol, sumsq, 1.0 / D, None, op0=ALU.mult),
               reads=[sres + "sq"], writes=[sres + "v"])
            EM("vector", lambda e: e.scalar_tensor_tensor(vcol, mcol, mcol, vcol, op0=ALU.mult, op1=ALU.subtract),
               reads=[sres + "m", sres + "v"], writes=[sres + "v"])
            EM("vector", lambda e: e.tensor_scalar(vcol, vcol, -1.0, LN_EPS, op0=ALU.mult, op1=ALU.add),
               reads=[sres + "v"], writes=[sres + "v"])
            EM("scalar", lambda e: e.activation(rstd, vcol, AF.Sqrt),
               reads=[sres + "v"], writes=[sres + "r"])
            EM("vector", lambda e: e.reciprocal(rstd, rstd),
               reads=[sres + "r"], writes=[sres + "r"])
            EM("vector", lambda e: e.tensor_scalar(nmr, mcol, rstd, -1.0, op0=ALU.mult, op1=ALU.mult),
               reads=[sres + "m", sres + "r"], writes=[sres + "n"])
            EM("scalar", lambda e: e.activation(tm[:, 0:D], xt, AF.Identity, bias=nmr, scale=rstd),
               reads=[xres, sres + "r", sres + "n"], writes=[tmres])
            for dc in range(NDC):
                bank = dc % 2
                EM("tensor", lambda e, dc=dc, bank=bank: e.transpose(
                    pb[bank][:, 0:P], tm[:, dc * P:(dc + 1) * P], identB[:]),
                    reads=[tmres, "identB"], acquire=[f"pb{bank}"], release=[f"pb{bank}"])
                EM("vector", lambda e, dc=dc, bank=bank: e.tensor_copy(
                    xh(dc)[:, tc8 * P:(tc8 + 1) * P], pb[bank][:, 0:P]),
                    reads=[f"pb{bank}"], writes=[f"{dst_res_prefix}{dc}c{tc8}"])
            if router:
                ps = pf[4]
                for dc in range(NDC):
                    kw = dict(reads=[f"{dst_res_prefix}{dc}c{tc8}", "wrt"])
                    if dc == 0:
                        kw["acquire"] = ["pf4"]
                    if dc == NDC - 1:
                        kw["release"] = ["pf4"]
                    EM("tensor", lambda e, dc=dc: e.matmul(
                        ps[:, 0:E], xh(dc)[:, tc8 * P:(tc8 + 1) * P], wrt[:, dc * E:(dc + 1) * E],
                        start=(dc == 0), stop=(dc == NDC - 1)), **kw)
                m1 = rscr(par, 0)[:, 0:1]
                m2c = rscr(par, 0)[:, 2:3]
                nm1c = statc(par, 6)
                msk = rscr(par, 1)
                unn = rscr(par, 2)
                den = rscr(par, 0)[:, 1:2]
                rr = f"rt{par}"
                EM("vector", lambda e: e.reduce_max(m1, ps[0:P, 0:E], axis=AX.X),
                   reads=["pf4"], writes=[rr + "m1"])
                EM("vector", lambda e: e.tensor_scalar(msk, ps[0:P, 0:E], m1, None, op0=ALU.is_ge),
                   reads=["pf4", rr + "m1"], writes=[rr + "msk"])
                EM("vector", lambda e: e.scalar_tensor_tensor(msk, msk, -1e30, ps[0:P, 0:E], op0=ALU.mult, op1=ALU.add),
                   reads=["pf4", rr + "msk"], writes=[rr + "msk"])
                EM("vector", lambda e: e.reduce_max(m2c, msk, axis=AX.X),
                   reads=[rr + "msk"], writes=[rr + "m2"])
                EM("vector", lambda e: e.tensor_scalar(msk, ps[0:P, 0:E], m2c, None, op0=ALU.is_ge),
                   reads=["pf4", rr + "m2"], writes=[rr + "msk"])
                EM("vector", lambda e: e.tensor_scalar(nm1c, m1, -1.0, None, op0=ALU.mult),
                   reads=[rr + "m1"], writes=[rr + "nm1"])
                EM("scalar", lambda e: e.activation(unn, ps[0:P, 0:E], AF.Exp, bias=nm1c),
                   reads=["pf4", rr + "nm1"], writes=[rr + "unn"])
                EM("vector", lambda e: e.tensor_tensor(unn, unn, msk, op=ALU.mult),
                   reads=[rr + "unn", rr + "msk"], writes=[rr + "unn"])
                EM("vector", lambda e: e.reduce_sum(den, unn, axis=AX.X),
                   reads=[rr + "unn"], writes=[rr + "den"])
                EM("vector", lambda e: e.reciprocal(den, den),
                   reads=[rr + "den"], writes=[rr + "den"])
                EM("vector", lambda e: e.tensor_scalar(weffc(tc16), unn, den, 0.5, op0=ALU.mult, op1=ALU.mult),
                   reads=[rr + "unn", rr + "den"], writes=[f"weff{tc16}"])

        def cvb(par):
            return ar4b[:, par * 2048:(par + 1) * 2048]

        cvi = [0]

        def emit_convert(e_i):
            segs = []
            for m_src, m_dst, res in ((wg_d, wg_s, "wgS"), (wu_d, wu_s, "wuS")):
                for ch in range(6):
                    segs.append((m_src[e_i][:, ch * 2048:(ch + 1) * 2048],
                                 m_dst[e_i][:, ch * 2048:(ch + 1) * 2048],
                                 2048, f"{res}{e_i}"))
            for og in range(NOG):
                for (o0, w) in ((0, 2048), (2048, 1024)):
                    segs.append((wd_d[e_i, og][:, o0:o0 + w],
                                 wd_s[e_i, og][:, o0:o0 + w], w, f"wdS{e_i}_{og}"))
            for (src, dst, w, res) in segs:
                par = cvi[0] % 2
                cvi[0] += 1
                EM("gpsimd", lambda e, src=src, par=par, w=w: e.dma_start(
                    out=cvb(par)[:, 0:w], in_=src), writes=[f"cvb{par}"])
                EM("sync", lambda e, dst=dst, par=par, w=w: e.dma_start(
                    out=dst, in_=cvb(par)[:, 0:w]), reads=[f"cvb{par}"], writes=[res])

        def emit_ffn_half(half, x_src, store_out, post_expert=None):
            for tc8 in range(8):
                tc16 = half * 8 + tc8
                EM("sync", lambda e, tc8=tc8, tc16=tc16: e.dma_start(
                    out=slotX(tc8), in_=x_src[tc16 * P:(tc16 + 1) * P, :]),
                    writes=[f"X{tc8}"], reads=[f"SRC{tc16}"])
            for tc8 in range(8):
                emit_ln_tc(half, tc8, router=True, dst_res_prefix="xh", call_tag="ffn")

            wgreg = ar3[:, 0:12288]
            wureg = ar3[:, 12288:24576]

            def wdreg(par):
                return ar3[:, 24576 + par * 3072: 24576 + (par + 1) * 3072]

            wgres = [f"wg{dc}" for dc in range(NDC)]
            wures = [f"wu{dc}" for dc in range(NDC)]

            def load_gu(e_i):
                uq = "sync" if e_i == 0 else "gpsimd"
                EM("sync", lambda e, e_i=e_i: e.dma_start(out=wgreg, in_=wg_s[e_i]),
                   reads=[f"wgS{e_i}"], writes=wgres)
                EM(uq, lambda e, e_i=e_i: e.dma_start(out=wureg, in_=wu_s[e_i]),
                   reads=[f"wuS{e_i}"], writes=wures)

            def load_wd(e_i, og, par):
                wdres = [f"wd{par}_{fc}" for fc in range(NFC)]
                EM("gpsimd", lambda e, e_i=e_i, og=og, par=par: e.dma_start(
                    out=wdreg(par), in_=wd_s[e_i, og]),
                    reads=[f"wdS{e_i}_{og}"], writes=wdres)

            for e_i in range(E):
                if e_i == 0:
                    load_gu(0)
                for tt in range(2):
                    xh_reads = [f"xh{dc}c{tc8}" for dc in range(1) for tc8 in range(4 * tt, 4 * tt + 4)]
                    for fc in range(NFC):
                        gpar = fc % 2
                        gsb = gsv(gpar)
                        for dc in range(NDC):
                            kw = dict(reads=[f"wg{dc}"] + [f"xh{dc}c{tc8}" for tc8 in range(4 * tt, 4 * tt + 4)])
                            if dc == 0:
                                kw["acquire"] = ["pf0"]
                            if dc == NDC - 1:
                                kw["release"] = ["pf0"]
                            EM("tensor", lambda e, dc=dc, fc=fc, tt=tt: e.matmul(
                                pf[0][:, 0:512], wgv(dc)[:, fc * P:(fc + 1) * P],
                                xh(dc)[:, tt * 512:(tt + 1) * 512],
                                start=(dc == 0), stop=(dc == NDC - 1)), **kw)
                        EM("scalar", lambda e, gsb=gsb: e.activation(gsb, pf[0][:, 0:512], AF.Silu),
                           reads=["pf0"], writes=[f"gs{gpar}"])
                        for dc in range(NDC):
                            kw = dict(reads=[f"wu{dc}"] + [f"xh{dc}c{tc8}" for tc8 in range(4 * tt, 4 * tt + 4)])
                            if dc == 0:
                                kw["acquire"] = ["pf1"]
                            if dc == NDC - 1:
                                kw["release"] = ["pf1"]
                            EM("tensor", lambda e, dc=dc, fc=fc, tt=tt: e.matmul(
                                pf[1][:, 0:512], wuv(dc)[:, fc * P:(fc + 1) * P],
                                xh(dc)[:, tt * 512:(tt + 1) * 512],
                                start=(dc == 0), stop=(dc == NDC - 1)), **kw)
                        EM("vector", lambda e, fc=fc, tt=tt, gsb=gsb: e.tensor_tensor(
                            hidv(fc)[:, tt * 512:(tt + 1) * 512], pf[1][:, 0:512], gsb, op=ALU.mult),
                            reads=["pf1", f"gs{gpar}"], writes=[f"hid{fc}t{tt}"])
                load_wd(e_i, 0, 0)
                load_wd(e_i, 1, 1)
                for og in range(NOG):
                    par = og % 2
                    for tc8 in range(8):
                        tt = tc8 // 4
                        bank = 2 + (tc8 % 2)
                        for fc in range(NFC):
                            kw = dict(reads=[f"wd{par}_{fc}", f"hid{fc}t{tt}"])
                            if fc == 0:
                                kw["acquire"] = [f"pf{bank}"]
                            if fc == NFC - 1:
                                kw["release"] = [f"pf{bank}"]
                            EM("tensor", lambda e, fc=fc, tc8=tc8, og=og, par=par, bank=bank: e.matmul(
                                pf[bank][:, 0:512],
                                hidv(fc)[:, tc8 * P:(tc8 + 1) * P],
                                wdblk(par, fc)[:],
                                start=(fc == 0), stop=(fc == NFC - 1)), **kw)
                        tc16 = half * 8 + tc8
                        EM("vector", lambda e, tc8=tc8, og=og, bank=bank, e_i=e_i, tc16=tc16: e.scalar_tensor_tensor(
                            slotX(tc8)[:, og * 512:(og + 1) * 512],
                            pf[bank][:, 0:512],
                            weffc(tc16)[:, e_i:e_i + 1],
                            slotX(tc8)[:, og * 512:(og + 1) * 512],
                            op0=ALU.mult, op1=ALU.add),
                            reads=[f"pf{bank}", f"weff{tc16}", f"X{tc8}"], writes=[f"X{tc8}"])
                    if og + 2 < NOG:
                        load_wd(e_i, og + 2, (og + 2) % 2)
                    if og == 0 and e_i + 1 < E:
                        load_gu(e_i + 1)
                if post_expert is not None:
                    post_expert(e_i)
            if store_out is not None:
                for tc8 in range(8):
                    tc16 = half * 8 + tc8
                    EM("sync", lambda e, tc8=tc8, tc16=tc16: e.dma_start(
                        out=store_out[tc16 * P:(tc16 + 1) * P, :], in_=slotX(tc8)),
                        reads=[f"X{tc8}"], writes=[f"OUT{tc16}"])

        def emit_phaseA_half(half):
            if half == 0:
                emit_convert(0)
                emit_convert(1)
                emit_ffn_half(half, x_d, None,
                              post_expert=lambda e_i: emit_convert(e_i + 2)
                              if e_i + 2 < E else None)
            else:
                emit_ffn_half(half, x_d, None)
            for tc8 in range(8):
                emit_ln_tc(half, tc8, router=False, dst_res_prefix="xh", call_tag="ln2")
            for dc in range(NDC):
                EM("sync", lambda e, dc=dc, half=half: e.dma_start(
                    out=xh2_s[dc * P:(dc + 1) * P, half * TH:(half + 1) * TH], in_=xh(dc)[:]),
                    reads=[f"xh{dc}c{tc8}" for tc8 in range(8)], writes=[f"XH2d{dc}h{half}"])
            for tc8 in range(8):
                tc16 = half * 8 + tc8
                EM("sync", lambda e, tc8=tc8, tc16=tc16: e.dma_start(
                    out=x1_s[tc16 * P:(tc16 + 1) * P, :], in_=slotX(tc8)),
                    reads=[f"X{tc8}"], writes=[f"X1S{tc16}"])

        def emit_attnproj():
            ctx.region_switch([f"kvsb{og}" for og in range(4)] + ["kvsb_kp"],
                              ["cvb0", "cvb1"])
            ctx.region_switch([f"xh2f{dc}" for dc in range(NDC)],
                              [f"X{t}" for t in range(8)])
            ctx.region_switch([f"kp{t}" for t in range(16)] + [f"v{t}" for t in range(16)]
                              + ["qstg0", "qstg1", "escr0", "escr1"],
                              [f"xh{dc}c{tc8}" for dc in range(NDC) for tc8 in range(8)]
                              + ["xhtm0", "xhtm1"])
            ctx.region_switch(["wblk0", "wblk1"],
                              [f"wg{dc}" for dc in range(NDC)] + [f"wu{dc}" for dc in range(NDC)]
                              + [f"wd{p}_{fc}" for p in range(2) for fc in range(NFC)])
            ctx.region_switch([f"pf3h{h}" for h in range(16)], ["pf3"])
            ctx.region_switch([f"pf4h{hh}" for hh in range(4)], ["pf4"])
            ctx.region_switch([f"pf5h{hh}" for hh in range(4)], ["pf5"])
            for dc in range(NDC):
                EM("sync", lambda e, dc=dc: e.dma_start(out=xh2f(dc), in_=xh2_s[dc * P:(dc + 1) * P, :]),
                   reads=[f"XH2d{dc}h0", f"XH2d{dc}h1"], writes=[f"xh2f{dc}"])

            def load_wblk(src, og, par):
                EM("gpsimd", lambda e: e.dma_start(
                    out=ar3[:, par * 8192:(par + 1) * 8192], in_=src[og]),
                    writes=[f"wblk{par}"])

            def proj_group(tc16, par, bank, bias_slot):
                for dc in range(NDC):
                    kw = dict(reads=[f"xh2f{dc}", f"wblk{par}"])
                    if dc == 0:
                        kw["acquire"] = [f"pf{bank}"]
                    EM("tensor", lambda e, dc=dc, tc16=tc16, par=par, bank=bank: e.matmul(
                        pf[bank][:, 0:512], xh2f(dc)[:, tc16 * P:(tc16 + 1) * P],
                        wblk(par, dc)[:], start=(dc == 0), stop=False), **kw)
                EM("tensor", lambda e, tc16=tc16, bank=bank, bias_slot=bias_slot: e.matmul(
                    pf[bank][:, 0:512], ones_r[0:1, 0:P],
                    biasrow[0:1, bias_slot * 512:(bias_slot + 1) * 512],
                    start=False, stop=True),
                    reads=["ones_r", f"bias{bias_slot}"], release=[f"pf{bank}"])

            def elu_evac(bank, dst, dres):
                EM("vector", lambda e: e.tensor_scalar(escr(0), pf[bank][:, 0:512], 0.0, None, op0=ALU.min),
                   reads=[f"pf{bank}"], writes=["escr0"])
                EM("scalar", lambda e: e.activation(escr(0), escr(0), AF.Exp),
                   reads=["escr0"], writes=["escr0"])
                EM("vector", lambda e: e.tensor_scalar(escr(1), pf[bank][:, 0:512], 0.0, None, op0=ALU.max),
                   reads=[f"pf{bank}"], writes=["escr1"])
                EM("vector", lambda e: e.tensor_tensor(dst, escr(0), escr(1), op=ALU.add),
                   reads=["escr0", "escr1"], writes=[dres])

            nbias = [0]

            def load_bias(src, slot, og):
                EM("gpsimd", lambda e: e.dma_start(
                    out=biasrow[0:1, slot * 512:(slot + 1) * 512], in_=src[0:1, og * 512:(og + 1) * 512]),
                    writes=[f"bias{slot}"])

            pidx = 0
            for og in range(NOG):
                load_bias(bq_d, 0, og)
                load_bias(bk_d, 1, og)
                load_bias(bv_d, 2, og)
                load_wblk(wq_d, og, pidx % 2)
                for tc16 in range(16):
                    bank = tc16 % 2
                    proj_group(tc16, pidx % 2, bank, 0)
                    elu_evac(bank, qstg(tc16 % 2)[:], f"qstg{tc16 % 2}")
                    EM("sync", lambda e, tc16=tc16, og=og: e.dma_start(
                        out=qpt_s[tc16 * P:(tc16 + 1) * P, og * 512:(og + 1) * 512],
                        in_=qstg(tc16 % 2)[:]),
                        reads=[f"qstg{tc16 % 2}"], writes=[f"QPT{tc16}o{og}"])
                pidx += 1
                load_wblk(wk_d, og, pidx % 2)
                for tc16 in range(16):
                    bank = tc16 % 2
                    proj_group(tc16, pidx % 2, bank, 1)
                    elu_evac(bank, kpblk(tc16)[:], f"kp{tc16}")
                pidx += 1
                load_wblk(wv_d, og, pidx % 2)
                for tc16 in range(16):
                    bank = tc16 % 2
                    proj_group(tc16, pidx % 2, bank, 2)
                    EM("vector", lambda e, tc16=tc16, bank=bank: e.tensor_copy(vblk(tc16)[:], pf[bank][:, 0:512]),
                       reads=[f"pf{bank}"], writes=[f"v{tc16}"])
                pidx += 1
                kvbank = 4 + (og % 2)
                for hh in range(4):
                    for tc16 in range(16):
                        kw = dict(reads=[f"kp{tc16}", f"v{tc16}"])
                        if tc16 == 0:
                            kw["acquire"] = [f"pf{kvbank}h{hh}"]
                        if tc16 == 15:
                            kw["release"] = [f"pf{kvbank}h{hh}"]
                        EM("tensor", lambda e, hh=hh, tc16=tc16, kvbank=kvbank: e.matmul(
                            pf[kvbank][:, hh * P:(hh + 1) * P],
                            kpblk(tc16)[:, hh * P:(hh + 1) * P],
                            vblk(tc16)[:, hh * P:(hh + 1) * P],
                            start=(tc16 == 0), stop=(tc16 == 15)), **kw)
                for hh in range(4):
                    for tc16 in range(16):
                        kw = dict(reads=[f"kp{tc16}", "ones_c"])
                        if tc16 == 0:
                            kw["acquire"] = [f"pf3h{og * 4 + hh}"]
                        if tc16 == 15:
                            kw["release"] = [f"pf3h{og * 4 + hh}"]
                        EM("tensor", lambda e, hh=hh, tc16=tc16, og=og: e.matmul(
                            pf[3][:, og * 4 + hh: og * 4 + hh + 1],
                            kpblk(tc16)[:, hh * P:(hh + 1) * P],
                            ones_c[0:P, 0:1],
                            start=(tc16 == 0), stop=(tc16 == 15)), **kw)
                EM("vector", lambda e, og=og, kvbank=kvbank: e.tensor_copy(
                    kvsb[:, og * 512:(og + 1) * 512], pf[kvbank][:, 0:512]),
                    reads=[f"pf{kvbank}h{hh2}" for hh2 in range(4)], writes=[f"kvsb{og}"])
            EM("vector", lambda e: e.tensor_copy(kvsb[:, 2048:2064], pf[3][:, 0:16]),
               reads=[f"pf3h{h2}" for h2 in range(16)], writes=["kvsb_kp"])
            EM("gpsimd", lambda e: e.dma_start(out=kvi_s.ap()[:, :], in_=ar4[:, 0:2080]),
               reads=[f"kvsb{og}" for og in range(4)] + ["kvsb_kp"], writes=["KVI"])
            if _SKIP_CC:
                EM("gpsimd", lambda e: e.dma_start(out=kvo_s.ap()[:, :], in_=kvi_s.ap()[:, :]),
                   reads=["KVI"], writes=["KVO"])
            else:
                EM("gpsimd", lambda e: e.collective_compute(
                    "AllReduce", ALU.add,
                    replica_groups=[[0, 1], [2, 3], [4, 5], [6, 7]],
                    ins=[kvi_s.ap().opt()], outs=[kvo_s.ap().opt()]).then_inc(cc_sem),
                    reads=["KVI"], writes=[], inc=0)
                EM("gpsimd", lambda e: (e.wait_ge(cc_sem, 1), e.nop())[1],
                   reads=[], writes=["KVO"], inc=16)
            ctx.region_switch(["kvsball"], [f"kvsb{og}" for og in range(4)] + ["kvsb_kp"])
            EM("gpsimd", lambda e: e.dma_start(out=ar4[:, 0:2080], in_=kvo_s.ap()[:, :]),
               reads=["KVO"], writes=["kvsball"])

        def emit_phaseB():
            ctx.region_switch([f"wo{vc}" for vc in range(NDC)],
                              [f"xh2f{dc}" for dc in range(NDC)])
            ctx.region_switch([f"qpfm{qc}" for qc in range(NDC)] + [f"qptm{ts}" for ts in range(4)]
                              + ["atm0", "atm1"],
                              [f"kp{t}" for t in range(16)] + [f"v{t}" for t in range(16)]
                              + ["qstg0", "qstg1", "escr0", "escr1"])
            ctx.region_switch([f"afm{vc}" for vc in range(NDC)] + ["x1t0", "x1t1", "x2t0", "x2t1"],
                              ["wblk0", "wblk1"])
            ctx.region_switch(["kvbf", "zsel"], [f"hid{fc}t{tt}" for fc in range(NFC) for tt in range(2)])
            ctx.region_switch(["pf3"], [f"pf3h{h}" for h in range(16)])
            ctx.region_switch(["pf4"], [f"pf4h{hh}" for hh in range(4)])
            ctx.region_switch(["pf5"], [f"pf5h{hh}" for hh in range(4)])
            EM("vector", lambda e: e.tensor_copy(kvbf[:], kvsb[:, 0:2048]),
               reads=["kvsball"], writes=["kvbf"])
            EM("vector", lambda e: e.memset(zsel[:], 0.0), writes=["zsel"])
            for h in range(H):
                EM("vector", lambda e, h=h: e.tensor_copy(zsel[:, h * 16 + h: h * 16 + h + 1],
                                                          kvsb[:, 2048 + h: 2048 + h + 1]),
                   reads=["kvsball"], writes=["zsel"])
            EM("gpsimd", lambda e: e.dma_start(out=biasrow[0:1, :], in_=bo_d[0:1, :]),
               writes=["bias0", "bias1", "bias2", "bias3"])
            EM("gpsimd", lambda e: e.dma_start(out=ar1b[:, 0:32768], in_=wo_d[:, :]),
               writes=[f"wo{vc}" for vc in range(NDC)])

            for tt in range(4):
                for ts in range(4):
                    tc16 = tt * 4 + ts
                    EM("sync", lambda e, ts=ts, tc16=tc16: e.dma_start(
                        out=qptm(ts)[:], in_=qpt_s[tc16 * P:(tc16 + 1) * P, :]),
                        reads=[f"QPT{tc16}o{og}" for og in range(4)], writes=[f"qptm{ts}"])
                    for qc in range(NDC):
                        bank = qc % 2
                        EM("tensor", lambda e, qc=qc, ts=ts, bank=bank: e.transpose(
                            pb[bank][:, 0:P], qptm(ts)[:, qc * P:(qc + 1) * P], identB[:]),
                            reads=[f"qptm{ts}", "identB"], acquire=[f"pb{bank}"], release=[f"pb{bank}"])
                        EM("vector", lambda e, qc=qc, ts=ts, bank=bank: e.tensor_copy(
                            qpfm(qc)[:, ts * P:(ts + 1) * P], pb[bank][:, 0:P]),
                            reads=[f"pb{bank}"], writes=[f"qpfm{qc}ts{ts}"])
                    zpar = ts % 2
                    for qc in range(NDC):
                        kw = dict(reads=[f"qpfm{qc}ts{ts}", "zsel"])
                        if qc == 0:
                            kw["acquire"] = ["pf0"]
                        if qc == NDC - 1:
                            kw["release"] = ["pf0"]
                        EM("tensor", lambda e, qc=qc, ts=ts: e.matmul(
                            pf[0][:, 0:16], qpfm(qc)[:, ts * P:(ts + 1) * P],
                            zsel[:, qc * 16:(qc + 1) * 16],
                            start=(qc == 0), stop=(qc == NDC - 1)), **kw)
                    EM("vector", lambda e, zpar=zpar: e.tensor_scalar(zinvv(zpar), pf[0][:, 0:16], Z_EPS, None, op0=ALU.add),
                       reads=["pf0"], writes=[f"zinv{zpar}"])
                    EM("vector", lambda e, zpar=zpar: e.reciprocal(zinvv(zpar), zinvv(zpar)),
                       reads=[f"zinv{zpar}"], writes=[f"zinv{zpar}"])
                    apar = ts % 2
                    for og in range(NOG):
                        bank = 1 + (og % 2)
                        for hh in range(4):
                            h = og * 4 + hh
                            kw = dict(reads=[f"qpfm{h}ts{ts}", "kvbf"])
                            if hh == 0:
                                kw["acquire"] = [f"pf{bank}"]
                            if hh == 3:
                                kw["release"] = [f"pf{bank}"]
                            EM("tensor", lambda e, h=h, hh=hh, ts=ts, bank=bank: e.matmul(
                                pf[bank][:, hh * P:(hh + 1) * P],
                                qpfm(h)[:, ts * P:(ts + 1) * P],
                                kvbf[:, h * P:(h + 1) * P],
                                start=(hh == 0), stop=(hh == 3)), **kw)
                        for hh in range(4):
                            h = og * 4 + hh
                            EM("vector", lambda e, h=h, hh=hh, og=og, bank=bank, apar=apar, zpar=zpar: e.tensor_scalar(
                                atm(apar)[:, h * P:(h + 1) * P], pf[bank][:, hh * P:(hh + 1) * P],
                                zinvv(zpar)[:, h:h + 1], None, op0=ALU.mult),
                                reads=[f"pf{bank}", f"zinv{zpar}"], writes=[f"atm{apar}b{h}"])
                    for vc in range(NDC):
                        bank = vc % 2
                        EM("tensor", lambda e, vc=vc, apar=apar, bank=bank: e.transpose(
                            pb[bank][:, 0:P], atm(apar)[:, vc * P:(vc + 1) * P], identB[:]),
                            reads=[f"atm{apar}b{vc}", "identB"], acquire=[f"pb{bank}"], release=[f"pb{bank}"])
                        EM("vector", lambda e, vc=vc, ts=ts, bank=bank: e.tensor_copy(
                            afm(vc)[:, ts * P:(ts + 1) * P], pb[bank][:, 0:P]),
                            reads=[f"pb{bank}"], writes=[f"afm{vc}ts{ts}"])
                    xpar = ts % 2
                    EM("sync", lambda e, xpar=xpar, tc16=tc16: e.dma_start(
                        out=x1t(xpar), in_=x1_s[tc16 * P:(tc16 + 1) * P, :]),
                        reads=[f"X1S{tc16}"], writes=[f"x1t{xpar}"])
                    for og in range(NOG):
                        bank = 3 + (og % 2)
                        for vc in range(NDC):
                            kw = dict(reads=[f"afm{vc}ts{ts}", f"wo{vc}"])
                            if vc == 0:
                                kw["acquire"] = [f"pf{bank}"]
                            EM("tensor", lambda e, vc=vc, ts=ts, og=og, bank=bank: e.matmul(
                                pf[bank][:, 0:512], afm(vc)[:, ts * P:(ts + 1) * P],
                                wovw(vc)[:, og * 512:(og + 1) * 512],
                                start=(vc == 0), stop=False), **kw)
                        EM("tensor", lambda e, og=og, bank=bank: e.matmul(
                            pf[bank][:, 0:512], ones_r[0:1, 0:P],
                            biasrow[0:1, og * 512:(og + 1) * 512],
                            start=False, stop=True),
                            reads=["ones_r", f"bias{og}"], release=[f"pf{bank}"])
                        EM("vector", lambda e, og=og, bank=bank, xpar=xpar: e.tensor_tensor(
                            x2t(xpar)[:, og * 512:(og + 1) * 512], pf[bank][:, 0:512],
                            x1t(xpar)[:, og * 512:(og + 1) * 512], op=ALU.add),
                            reads=[f"pf{bank}", f"x1t{xpar}"], writes=[f"x2t{xpar}"])
                    EM("sync", lambda e, xpar=xpar, tc16=tc16: e.dma_start(
                        out=x2_s[tc16 * P:(tc16 + 1) * P, :], in_=x2t(xpar)),
                        reads=[f"x2t{xpar}"], writes=[f"X2S{tc16}", f"SRC2_{tc16}"])

        def emit_phaseC():
            ctx.region_switch([f"X{t}" for t in range(8)], [f"wo{vc}" for vc in range(NDC)])
            ctx.region_switch([f"xh{dc}c{tc8}" for dc in range(NDC) for tc8 in range(8)]
                              + ["xhtm0", "xhtm1"],
                              [f"qpfm{qc}ts{ts}" for qc in range(NDC) for ts in range(4)]
                              + [f"qptm{ts}" for ts in range(4)]
                              + [f"atm{i}b{h}" for i in range(2) for h in range(H)])
            ctx.region_switch([f"wg{dc}" for dc in range(NDC)] + [f"wu{dc}" for dc in range(NDC)]
                              + [f"wd{p}_{fc}" for p in range(2) for fc in range(NFC)],
                              [f"afm{vc}ts{ts}" for vc in range(NDC) for ts in range(4)]
                              + ["x1t0", "x1t1", "x2t0", "x2t1"])
            ctx.region_switch([f"hid{fc}t{tt}" for fc in range(NFC) for tt in range(2)],
                              ["kvbf", "zsel"])
            for half in range(2):
                for tc8 in range(8):
                    tc16 = half * 8 + tc8
                    ctx.region_switch([f"SRC{tc16}"], [f"SRC2_{tc16}"])
                emit_ffn_half(half, x2_s, out_d)

        emit_phaseA_half(0)
        emit_phaseA_half(1)
        if _BUILD_UPTO == "ffn1":
            for tc16 in range(16):
                EM("sync", lambda e, tc16=tc16: e.dma_start(
                    out=out_d[tc16 * P:(tc16 + 1) * P, :], in_=x1_s[tc16 * P:(tc16 + 1) * P, :]),
                    reads=[f"X1S{tc16}"], writes=[])
        else:
            emit_attnproj()
            emit_phaseB()
            if _BUILD_UPTO == "attn":
                for tc16 in range(16):
                    EM("sync", lambda e, tc16=tc16: e.dma_start(
                        out=out_d[tc16 * P:(tc16 + 1) * P, :], in_=x2_s[tc16 * P:(tc16 + 1) * P, :]),
                        reads=[f"X2S{tc16}"], writes=[])
            else:
                emit_phaseC()

        ctx.final_wait("sync", ["sync", "gpsimd"])
        ctx.run(block)
    return nc, ctx


def kernel(L0_attn_hidden_states, L0_attn_q_w, L0_attn_q_b, L0_attn_k_w, L0_attn_k_b,
           L0_attn_v_w, L0_attn_v_b, L0_attn_o_w, L0_attn_o_b,
           L0_ffn_w_gate_router, L0_ffn_w_gate_proj, L0_ffn_w_up_proj, L0_ffn_w_down_proj):
    f32 = np.float32
    hs = np.asarray(L0_attn_hidden_states, f32)
    c = np.ascontiguousarray

    def pack_qkv(w):
        return c(np.asarray(w, f32).T.reshape(16, P, 4, 512)
                 .transpose(2, 1, 0, 3).reshape(4, P, 8192))

    wq = pack_qkv(L0_attn_q_w)
    wk = pack_qkv(L0_attn_k_w)
    wv = pack_qkv(L0_attn_v_w)
    wo = c(np.asarray(L0_attn_o_w, f32).T.reshape(16, P, 2048)
           .transpose(1, 0, 2).reshape(P, 32768))
    bq = c(np.asarray(L0_attn_q_b, f32).reshape(1, D))
    bk = c(np.asarray(L0_attn_k_b, f32).reshape(1, D))
    bv = c(np.asarray(L0_attn_v_b, f32).reshape(1, D))
    bo = c(np.asarray(L0_attn_o_b, f32).reshape(1, D))
    wr = c(np.asarray(L0_ffn_w_gate_router, f32).T)
    def pack_gu(w):
        return c(np.asarray(w, f32).transpose(0, 2, 1).reshape(E, 16, P, FF)
                 .transpose(0, 2, 1, 3).reshape(E, P, 12288))

    wg = pack_gu(L0_ffn_w_gate_proj)
    wu = pack_gu(L0_ffn_w_up_proj)
    wd = c(np.asarray(L0_ffn_w_down_proj, f32).transpose(0, 2, 1)
           .reshape(E, NFC, P, 4, 512).transpose(0, 3, 2, 1, 4)
           .reshape(E, 4, P, 3072))
    ident = np.eye(P, dtype=f32)

    nc, _ = build_program()
    shared = {"wq": wq, "wk": wk, "wv": wv, "wo": wo,
              "bq": bq, "bk": bk, "bv": bv, "bo": bo,
              "wr": wr, "wg": wg, "wu": wu, "wd": wd, "ident": ident}
    in_maps = []
    for core in range(8):
        b, h = core // 2, core % 2
        xc = c(hs[b, h * T:(h + 1) * T, :])
        in_maps.append({"x": xc, **shared})
    res = run_bass_kernel_spmd(nc, in_maps, core_ids=list(range(8)))
    out = np.empty((4, 4096, D), f32)
    for core in range(8):
        b, h = core // 2, core % 2
        out[b, h * T:(h + 1) * T, :] = res.results[core]["out"]
    return out


# version 05
# speedup vs baseline: 1.3524x; 1.0001x over previous
import numpy as np
import concourse.bass as bass
import concourse.mybir as mybir
from concourse.bass_utils import run_bass_kernel_spmd

F32 = mybir.dt.float32
BF16 = mybir.dt.bfloat16
AF = mybir.ActivationFunctionType
ALU = mybir.AluOpType
AX = mybir.AxisListType

P = 128
T = 2048
TH = 1024
D = 2048
FF = 768
E = 8
H = 16
NDC = D // P
NFC = FF // P
NOG = D // 512
LN_EPS = 1e-5
Z_EPS = 1e-6

_BUILD_UPTO = "full"
_SERIALIZE = False
_DMA_CONSERVATIVE = True
_SKIP_CC = False


class Ctx:

    ENGINES = ("sync", "gpsimd", "vector", "scalar", "tensor")
    DMA_ENGINES = ("sync", "gpsimd")

    def __init__(self, nc, stack):
        self.nc = nc
        self.sems = {e: stack.enter_context(nc.semaphore(f"sem_{e}")) for e in self.ENGINES}
        self.cnt = {e: 0 for e in self.ENGINES}
        self.last_wait = {}
        self.writer = {}
        self.readers = {}
        self.plans = {e: [] for e in self.ENGINES}
        self.n_waits = 0
        self.n_instr = 0

    def _need(self, needs, eng, cnt):
        if cnt > needs.get(eng, 0):
            needs[eng] = cnt

    serialize_now = False

    def emit(self, eng, fn, reads=(), writes=(), acquire=(), release=(), inc=None):
        needs = {}
        if _SERIALIZE or self.serialize_now:
            for teng in self.ENGINES:
                if self.cnt[teng] > 0:
                    needs[teng] = self.cnt[teng]
        for r in reads:
            for weng, wcnt in self.writer.get(r, {}).items():
                self._need(needs, weng, wcnt)
        for wres in tuple(writes) + tuple(acquire):
            for weng, wcnt in self.writer.get(wres, {}).items():
                self._need(needs, weng, wcnt)
            for reng, rcnt in self.readers.get(wres, {}).items():
                self._need(needs, reng, rcnt)
        if _DMA_CONSERVATIVE:
            for teng in self.DMA_ENGINES:
                if teng in needs:
                    needs[teng] = self.cnt[teng]
        wl = []
        for teng, val in needs.items():
            key = (eng, teng)
            if self.last_wait.get(key, 0) >= val:
                continue
            self.last_wait[key] = val
            wl.append((self.sems[teng], val))
        if inc is None:
            inc = 16 if eng in self.DMA_ENGINES else 1
        self.cnt[eng] += inc
        my = self.cnt[eng]
        sem = self.sems[eng]
        self.n_waits += len(wl)
        self.n_instr += 1

        def thunk(e, fn=fn, wl=wl, sem=sem, inc=inc):
            for s, v in wl:
                e.wait_ge(s, v)
            ins = fn(e)
            if inc:
                ins.then_inc(sem, inc)

        self.plans[eng].append(thunk)
        for r in reads:
            self.readers.setdefault(r, {})[eng] = my
        for wres in tuple(writes) + tuple(release):
            self.writer[wres] = {eng: my}
            self.readers[wres] = {}
        return my

    def region_switch(self, news, olds):
        merged = {}
        for o in olds:
            for eng, cnt in self.writer.pop(o, {}).items():
                if cnt > merged.get(eng, 0):
                    merged[eng] = cnt
            for eng, cnt in self.readers.pop(o, {}).items():
                if cnt > merged.get(eng, 0):
                    merged[eng] = cnt
        for n in news:
            w = dict(merged)
            for eng, cnt in self.writer.get(n, {}).items():
                if cnt > w.get(eng, 0):
                    w[eng] = cnt
            for eng, cnt in self.readers.get(n, {}).items():
                if cnt > w.get(eng, 0):
                    w[eng] = cnt
            self.writer[n] = w
            self.readers[n] = {}

    def final_wait(self, eng, targets):
        wl = [(self.sems[t], self.cnt[t]) for t in targets if self.cnt[t] > 0]

        def thunk(e, wl=wl):
            for s, v in wl:
                e.wait_ge(s, v)

        self.plans[eng].append(thunk)

    def run(self, block):
        for name in self.ENGINES:
            plan = self.plans[name]

            def make(plan):
                def body(e):
                    for t in plan:
                        t(e)
                return body

            getattr(block, name)(make(plan))


def build_program():
    nc = bass.Bass("TRN2", target_bir_lowering=False, debug=False)

    x_d = nc.dram_tensor("x", [T, D], F32, kind="ExternalInput").ap()
    wq_d = nc.dram_tensor("wq", [NOG, P, 8192], F32, kind="ExternalInput").ap()
    wk_d = nc.dram_tensor("wk", [NOG, P, 8192], F32, kind="ExternalInput").ap()
    wv_d = nc.dram_tensor("wv", [NOG, P, 8192], F32, kind="ExternalInput").ap()
    wo_d = nc.dram_tensor("wo", [P, 32768], F32, kind="ExternalInput").ap()
    bq_d = nc.dram_tensor("bq", [1, D], F32, kind="ExternalInput").ap()
    bk_d = nc.dram_tensor("bk", [1, D], F32, kind="ExternalInput").ap()
    bv_d = nc.dram_tensor("bv", [1, D], F32, kind="ExternalInput").ap()
    bo_d = nc.dram_tensor("bo", [1, D], F32, kind="ExternalInput").ap()
    wr_d = nc.dram_tensor("wr", [D, E], F32, kind="ExternalInput").ap()
    wg_d = nc.dram_tensor("wg", [E, P, 12288], F32, kind="ExternalInput").ap()
    wu_d = nc.dram_tensor("wu", [E, P, 12288], F32, kind="ExternalInput").ap()
    wd_d = nc.dram_tensor("wd", [E, NOG, P, 3072], F32, kind="ExternalInput").ap()
    id_d = nc.dram_tensor("ident", [P, P], F32, kind="ExternalInput").ap()
    out_d = nc.dram_tensor("out", [T, D], F32, kind="ExternalOutput").ap()

    wg_s = nc.dram_tensor("wg_s", [E, P, 12288], BF16).ap()
    wu_s = nc.dram_tensor("wu_s", [E, P, 12288], BF16).ap()
    wd_s = nc.dram_tensor("wd_s", [E, NOG, P, 3072], BF16).ap()
    xh2_s = nc.dram_tensor("xh2_s", [D, T], BF16).ap()
    qpt_s = nc.dram_tensor("qpt_s", [T, D], BF16).ap()
    x1_s = nc.dram_tensor("x1_s", [T, D], F32).ap()
    x2_s = nc.dram_tensor("x2_s", [T, D], F32).ap()
    kvi_s = nc.dram_tensor("kvi_s", [P, 2080], F32)
    kvo_s = nc.dram_tensor("kvo_s", [P, 2080], F32)

    from contextlib import ExitStack
    with ExitStack() as st:
        ar1 = st.enter_context(nc.sbuf_tensor("ar1", [P, 16384], F32))
        ar2 = st.enter_context(nc.sbuf_tensor("ar2", [P, 20480], BF16))
        ar3 = st.enter_context(nc.sbuf_tensor("ar3", [P, 30720], BF16))
        ar4 = st.enter_context(nc.sbuf_tensor("ar4", [P, 3840], F32))
        ar5 = st.enter_context(nc.sbuf_tensor("ar5", [P, 6144], BF16))
        biasrow = st.enter_context(nc.sbuf_tensor("biasrow", [1, D], BF16))
        identB = st.enter_context(nc.sbuf_tensor("identB", [P, P], BF16))
        ones_r = st.enter_context(nc.sbuf_tensor("ones_r", [1, P], BF16))
        ones_c = st.enter_context(nc.sbuf_tensor("ones_c", [P, 1], BF16))
        wrt = st.enter_context(nc.sbuf_tensor("wrt", [P, P], BF16))

        pf = [st.enter_context(nc.psum_tensor(f"pf{i}", [P, 512], F32)) for i in range(6)]
        pb = [st.enter_context(nc.psum_tensor(f"pb{i}", [P, 1024], BF16)) for i in range(2)]
        cc_sem = st.enter_context(nc.semaphore("cc_sem"))

        ar1b = ar1.bitcast(BF16)
        ar2f = ar2.bitcast(F32)
        ar3f = ar3.bitcast(F32)
        ar4b = ar4.bitcast(BF16)

        ctx = Ctx(nc, st)
        block = st.enter_context(nc.Block())

        def slotX(tc8):
            return ar1[:, tc8 * D:(tc8 + 1) * D]

        def xh2f(dc):
            return ar1b[:, dc * D:(dc + 1) * D]

        def wovw(vc):
            return ar1b[:, vc * D:(vc + 1) * D]

        def xh(dc):
            return ar2[:, dc * TH:(dc + 1) * TH]

        def xhtm(i):
            return ar2[:, 16384 + i * D: 16384 + (i + 1) * D]

        def kpblk(tc16):
            return ar2[:, tc16 * 512:(tc16 + 1) * 512]

        def vblk(tc16):
            return ar2[:, 8192 + tc16 * 512: 8192 + (tc16 + 1) * 512]

        def qstg(i):
            return ar2[:, 16384 + i * 512: 16384 + (i + 1) * 512]

        def escr(i):
            return ar2f[:, 8704 + i * 512: 8704 + (i + 1) * 512]

        def qpfm(qc):
            return ar2[:, qc * 512:(qc + 1) * 512]

        def qptm(ts):
            return ar2[:, 8192 + ts * D: 8192 + (ts + 1) * D]

        def atm(i):
            return ar2[:, 16384 + i * D: 16384 + (i + 1) * D]

        def wgv(dc):
            return ar3[:, dc * FF:(dc + 1) * FF]

        def wuv(dc):
            return ar3[:, 12288 + dc * FF: 12288 + (dc + 1) * FF]

        def wdblk(par, fc):
            return ar3[:, 24576 + par * 3072 + fc * 512: 24576 + par * 3072 + (fc + 1) * 512]

        def wblk(par, dc):
            return ar3[:, par * 8192 + dc * 512: par * 8192 + (dc + 1) * 512]

        def afm(vc):
            return ar3[:, vc * 512:(vc + 1) * 512]

        def x1t(i):
            return ar3f[:, 4096 + i * D: 4096 + (i + 1) * D]

        def x2t(i):
            return ar3f[:, 8192 + i * D: 8192 + (i + 1) * D]

        kvsb = ar4[:, 0:2080]
        STAT0 = 2080

        def statc(par, j):
            return ar4[:, STAT0 + par * 8 + j: STAT0 + par * 8 + j + 1]

        WEFF0 = 2096

        def weffc(tc16):
            return ar4[:, WEFF0 + tc16 * E: WEFF0 + (tc16 + 1) * E]

        def gsv(i):
            return ar4[:, 2224 + i * 512: 2224 + (i + 1) * 512]

        def zinvv(par):
            return ar4[:, 3248 + par * 16: 3248 + (par + 1) * 16]

        def rscr(par, j):
            return ar4[:, 3280 + par * 24 + j * 8: 3280 + par * 24 + (j + 1) * 8]

        kvbf = ar5[:, 0:2048]
        zsel = ar5[:, 2048:2304]

        def hidv(fc):
            return ar5[:, fc * TH:(fc + 1) * TH]

        EM = ctx.emit

        EM("gpsimd", lambda e: e.dma_start(out=identB[:], in_=id_d[:, :]), writes=["identB"])
        EM("gpsimd", lambda e: e.dma_start(out=wrt[:].rearrange("p (c e) -> p c e", e=E),
                                           in_=wr_d.rearrange("(c p) e -> p c e", p=P)), writes=["wrt"])
        EM("vector", lambda e: e.memset(ones_r[:], 1.0), writes=["ones_r"])
        EM("vector", lambda e: e.memset(ones_c[:], 1.0), writes=["ones_c"])

        def emit_ln_tc(half, tc8, router, dst_res_prefix, call_tag):
            tc16 = half * 8 + tc8
            par = tc8 % 2
            xt = slotX(tc8)
            xres = f"X{tc8}"
            tmres = f"xhtm{par}"
            tm = xhtm(par)
            sums, sumsq, mcol, vcol = (statc(par, j) for j in range(4))
            rstd, nmr, nm1 = (statc(par, j) for j in range(4, 7))
            sres = f"stat{par}"
            EM("scalar", lambda e: e.activation(tm[:, 0:D], xt, AF.Square, accum_out=sumsq),
               reads=[xres], writes=[tmres, sres + "sq"])
            EM("vector", lambda e: e.reduce_sum(sums, xt, axis=AX.X),
               reads=[xres], writes=[sres + "s"])
            EM("vector", lambda e: e.tensor_scalar(mcol, sums, 1.0 / D, None, op0=ALU.mult),
               reads=[sres + "s"], writes=[sres + "m"])
            EM("vector", lambda e: e.tensor_scalar(vcol, sumsq, 1.0 / D, None, op0=ALU.mult),
               reads=[sres + "sq"], writes=[sres + "v"])
            EM("vector", lambda e: e.scalar_tensor_tensor(vcol, mcol, mcol, vcol, op0=ALU.mult, op1=ALU.subtract),
               reads=[sres + "m", sres + "v"], writes=[sres + "v"])
            EM("vector", lambda e: e.tensor_scalar(vcol, vcol, -1.0, LN_EPS, op0=ALU.mult, op1=ALU.add),
               reads=[sres + "v"], writes=[sres + "v"])
            EM("scalar", lambda e: e.activation(rstd, vcol, AF.Sqrt),
               reads=[sres + "v"], writes=[sres + "r"])
            EM("vector", lambda e: e.reciprocal(rstd, rstd),
               reads=[sres + "r"], writes=[sres + "r"])
            EM("vector", lambda e: e.tensor_scalar(nmr, mcol, rstd, -1.0, op0=ALU.mult, op1=ALU.mult),
               reads=[sres + "m", sres + "r"], writes=[sres + "n"])
            EM("scalar", lambda e: e.activation(tm[:, 0:D], xt, AF.Identity, bias=nmr, scale=rstd),
               reads=[xres, sres + "r", sres + "n"], writes=[tmres])
            for dc in range(NDC):
                bank = dc % 2
                EM("tensor", lambda e, dc=dc, bank=bank: e.transpose(
                    pb[bank][:, 0:P], tm[:, dc * P:(dc + 1) * P], identB[:]),
                    reads=[tmres, "identB"], acquire=[f"pb{bank}"], release=[f"pb{bank}"])
                EM("vector", lambda e, dc=dc, bank=bank: e.tensor_copy(
                    xh(dc)[:, tc8 * P:(tc8 + 1) * P], pb[bank][:, 0:P]),
                    reads=[f"pb{bank}"], writes=[f"{dst_res_prefix}{dc}c{tc8}"])
            if router:
                ps = pf[4]
                for dc in range(NDC):
                    kw = dict(reads=[f"{dst_res_prefix}{dc}c{tc8}", "wrt"])
                    if dc == 0:
                        kw["acquire"] = ["pf4"]
                    if dc == NDC - 1:
                        kw["release"] = ["pf4"]
                    EM("tensor", lambda e, dc=dc: e.matmul(
                        ps[:, 0:E], xh(dc)[:, tc8 * P:(tc8 + 1) * P], wrt[:, dc * E:(dc + 1) * E],
                        start=(dc == 0), stop=(dc == NDC - 1)), **kw)
                m1 = rscr(par, 0)[:, 0:1]
                m2c = rscr(par, 0)[:, 2:3]
                nm1c = statc(par, 6)
                msk = rscr(par, 1)
                unn = rscr(par, 2)
                den = rscr(par, 0)[:, 1:2]
                rr = f"rt{par}"
                EM("vector", lambda e: e.reduce_max(m1, ps[0:P, 0:E], axis=AX.X),
                   reads=["pf4"], writes=[rr + "m1"])
                EM("vector", lambda e: e.tensor_scalar(msk, ps[0:P, 0:E], m1, None, op0=ALU.is_ge),
                   reads=["pf4", rr + "m1"], writes=[rr + "msk"])
                EM("vector", lambda e: e.scalar_tensor_tensor(msk, msk, -1e30, ps[0:P, 0:E], op0=ALU.mult, op1=ALU.add),
                   reads=["pf4", rr + "msk"], writes=[rr + "msk"])
                EM("vector", lambda e: e.reduce_max(m2c, msk, axis=AX.X),
                   reads=[rr + "msk"], writes=[rr + "m2"])
                EM("vector", lambda e: e.tensor_scalar(msk, ps[0:P, 0:E], m2c, None, op0=ALU.is_ge),
                   reads=["pf4", rr + "m2"], writes=[rr + "msk"])
                EM("vector", lambda e: e.tensor_scalar(nm1c, m1, -1.0, None, op0=ALU.mult),
                   reads=[rr + "m1"], writes=[rr + "nm1"])
                EM("scalar", lambda e: e.activation(unn, ps[0:P, 0:E], AF.Exp, bias=nm1c),
                   reads=["pf4", rr + "nm1"], writes=[rr + "unn"])
                EM("vector", lambda e: e.tensor_tensor(unn, unn, msk, op=ALU.mult),
                   reads=[rr + "unn", rr + "msk"], writes=[rr + "unn"])
                EM("vector", lambda e: e.reduce_sum(den, unn, axis=AX.X),
                   reads=[rr + "unn"], writes=[rr + "den"])
                EM("vector", lambda e: e.reciprocal(den, den),
                   reads=[rr + "den"], writes=[rr + "den"])
                EM("vector", lambda e: e.tensor_scalar(weffc(tc16), unn, den, 0.5, op0=ALU.mult, op1=ALU.mult),
                   reads=[rr + "unn", rr + "den"], writes=[f"weff{tc16}"])

        def cvb(par):
            return ar4b[:, par * 2048:(par + 1) * 2048]

        cvi = [0]

        def emit_convert(e_i):
            segs = []
            for m_src, m_dst, res in ((wg_d, wg_s, "wgS"), (wu_d, wu_s, "wuS")):
                for ch in range(6):
                    segs.append((m_src[e_i][:, ch * 2048:(ch + 1) * 2048],
                                 m_dst[e_i][:, ch * 2048:(ch + 1) * 2048],
                                 2048, f"{res}{e_i}"))
            for og in range(NOG):
                for (o0, w) in ((0, 2048), (2048, 1024)):
                    segs.append((wd_d[e_i, og][:, o0:o0 + w],
                                 wd_s[e_i, og][:, o0:o0 + w], w, f"wdS{e_i}_{og}"))
            for (src, dst, w, res) in segs:
                par = cvi[0] % 2
                cvi[0] += 1
                EM("gpsimd", lambda e, src=src, par=par, w=w: e.dma_start(
                    out=cvb(par)[:, 0:w], in_=src), writes=[f"cvb{par}"])
                EM("sync", lambda e, dst=dst, par=par, w=w: e.dma_start(
                    out=dst, in_=cvb(par)[:, 0:w]), reads=[f"cvb{par}"], writes=[res])

        def emit_ffn_half(half, x_src, store_out, post_expert=None):
            for tc8 in range(8):
                tc16 = half * 8 + tc8
                EM("sync", lambda e, tc8=tc8, tc16=tc16: e.dma_start(
                    out=slotX(tc8), in_=x_src[tc16 * P:(tc16 + 1) * P, :]),
                    writes=[f"X{tc8}"], reads=[f"SRC{tc16}"])
            for tc8 in range(8):
                emit_ln_tc(half, tc8, router=True, dst_res_prefix="xh", call_tag="ffn")

            wgreg = ar3[:, 0:12288]
            wureg = ar3[:, 12288:24576]

            def wdreg(par):
                return ar3[:, 24576 + par * 3072: 24576 + (par + 1) * 3072]

            wgres = [f"wg{dc}" for dc in range(NDC)]
            wures = [f"wu{dc}" for dc in range(NDC)]

            def load_gu(e_i):
                uq = "sync" if e_i == 0 else "gpsimd"
                EM("sync", lambda e, e_i=e_i: e.dma_start(out=wgreg, in_=wg_s[e_i]),
                   reads=[f"wgS{e_i}"], writes=wgres)
                EM(uq, lambda e, e_i=e_i: e.dma_start(out=wureg, in_=wu_s[e_i]),
                   reads=[f"wuS{e_i}"], writes=wures)

            def load_wd(e_i, og, par):
                wdres = [f"wd{par}_{fc}" for fc in range(NFC)]
                EM("gpsimd", lambda e, e_i=e_i, og=og, par=par: e.dma_start(
                    out=wdreg(par), in_=wd_s[e_i, og]),
                    reads=[f"wdS{e_i}_{og}"], writes=wdres)

            for e_i in range(E):
                if e_i == 0:
                    load_gu(0)
                for tt in range(2):
                    xh_reads = [f"xh{dc}c{tc8}" for dc in range(1) for tc8 in range(4 * tt, 4 * tt + 4)]
                    for fc in range(NFC):
                        gpar = fc % 2
                        gsb = gsv(gpar)
                        for dc in range(NDC):
                            kw = dict(reads=[f"wg{dc}"] + [f"xh{dc}c{tc8}" for tc8 in range(4 * tt, 4 * tt + 4)])
                            if dc == 0:
                                kw["acquire"] = ["pf0"]
                            if dc == NDC - 1:
                                kw["release"] = ["pf0"]
                            EM("tensor", lambda e, dc=dc, fc=fc, tt=tt: e.matmul(
                                pf[0][:, 0:512], wgv(dc)[:, fc * P:(fc + 1) * P],
                                xh(dc)[:, tt * 512:(tt + 1) * 512],
                                start=(dc == 0), stop=(dc == NDC - 1)), **kw)
                        EM("scalar", lambda e, gsb=gsb: e.activation(gsb, pf[0][:, 0:512], AF.Silu),
                           reads=["pf0"], writes=[f"gs{gpar}"])
                        for dc in range(NDC):
                            kw = dict(reads=[f"wu{dc}"] + [f"xh{dc}c{tc8}" for tc8 in range(4 * tt, 4 * tt + 4)])
                            if dc == 0:
                                kw["acquire"] = ["pf1"]
                            if dc == NDC - 1:
                                kw["release"] = ["pf1"]
                            EM("tensor", lambda e, dc=dc, fc=fc, tt=tt: e.matmul(
                                pf[1][:, 0:512], wuv(dc)[:, fc * P:(fc + 1) * P],
                                xh(dc)[:, tt * 512:(tt + 1) * 512],
                                start=(dc == 0), stop=(dc == NDC - 1)), **kw)
                        EM("vector", lambda e, fc=fc, tt=tt, gsb=gsb: e.tensor_tensor(
                            hidv(fc)[:, tt * 512:(tt + 1) * 512], pf[1][:, 0:512], gsb, op=ALU.mult),
                            reads=["pf1", f"gs{gpar}"], writes=[f"hid{fc}t{tt}"])
                load_wd(e_i, 0, 0)
                load_wd(e_i, 1, 1)
                for og in range(NOG):
                    par = og % 2
                    for tc8 in range(8):
                        tt = tc8 // 4
                        bank = 2 + (tc8 % 2)
                        for fc in range(NFC):
                            kw = dict(reads=[f"wd{par}_{fc}", f"hid{fc}t{tt}"])
                            if fc == 0:
                                kw["acquire"] = [f"pf{bank}"]
                            if fc == NFC - 1:
                                kw["release"] = [f"pf{bank}"]
                            EM("tensor", lambda e, fc=fc, tc8=tc8, og=og, par=par, bank=bank: e.matmul(
                                pf[bank][:, 0:512],
                                hidv(fc)[:, tc8 * P:(tc8 + 1) * P],
                                wdblk(par, fc)[:],
                                start=(fc == 0), stop=(fc == NFC - 1)), **kw)
                        tc16 = half * 8 + tc8
                        EM("vector", lambda e, tc8=tc8, og=og, bank=bank, e_i=e_i, tc16=tc16: e.scalar_tensor_tensor(
                            slotX(tc8)[:, og * 512:(og + 1) * 512],
                            pf[bank][:, 0:512],
                            weffc(tc16)[:, e_i:e_i + 1],
                            slotX(tc8)[:, og * 512:(og + 1) * 512],
                            op0=ALU.mult, op1=ALU.add),
                            reads=[f"pf{bank}", f"weff{tc16}", f"X{tc8}"], writes=[f"X{tc8}"])
                    if og + 2 < NOG:
                        load_wd(e_i, og + 2, (og + 2) % 2)
                    if og == 1 and e_i + 1 < E:
                        load_gu(e_i + 1)
                if post_expert is not None:
                    post_expert(e_i)
            if store_out is not None:
                for tc8 in range(8):
                    tc16 = half * 8 + tc8
                    EM("sync", lambda e, tc8=tc8, tc16=tc16: e.dma_start(
                        out=store_out[tc16 * P:(tc16 + 1) * P, :], in_=slotX(tc8)),
                        reads=[f"X{tc8}"], writes=[f"OUT{tc16}"])

        def emit_phaseA_half(half):
            if half == 0:
                emit_convert(0)
                emit_convert(1)
                emit_ffn_half(half, x_d, None,
                              post_expert=lambda e_i: emit_convert(e_i + 2)
                              if e_i + 2 < E else None)
            else:
                emit_ffn_half(half, x_d, None)
            for tc8 in range(8):
                emit_ln_tc(half, tc8, router=False, dst_res_prefix="xh", call_tag="ln2")
            for dc in range(NDC):
                EM("sync", lambda e, dc=dc, half=half: e.dma_start(
                    out=xh2_s[dc * P:(dc + 1) * P, half * TH:(half + 1) * TH], in_=xh(dc)[:]),
                    reads=[f"xh{dc}c{tc8}" for tc8 in range(8)], writes=[f"XH2d{dc}h{half}"])
            for tc8 in range(8):
                tc16 = half * 8 + tc8
                EM("sync", lambda e, tc8=tc8, tc16=tc16: e.dma_start(
                    out=x1_s[tc16 * P:(tc16 + 1) * P, :], in_=slotX(tc8)),
                    reads=[f"X{tc8}"], writes=[f"X1S{tc16}"])

        def emit_attnproj():
            ctx.region_switch([f"kvsb{og}" for og in range(4)] + ["kvsb_kp"],
                              ["cvb0", "cvb1"])
            ctx.region_switch([f"xh2f{dc}" for dc in range(NDC)],
                              [f"X{t}" for t in range(8)])
            ctx.region_switch([f"kp{t}" for t in range(16)] + [f"v{t}" for t in range(16)]
                              + ["qstg0", "qstg1", "escr0", "escr1"],
                              [f"xh{dc}c{tc8}" for dc in range(NDC) for tc8 in range(8)]
                              + ["xhtm0", "xhtm1"])
            ctx.region_switch(["wblk0", "wblk1"],
                              [f"wg{dc}" for dc in range(NDC)] + [f"wu{dc}" for dc in range(NDC)]
                              + [f"wd{p}_{fc}" for p in range(2) for fc in range(NFC)])
            ctx.region_switch([f"pf3h{h}" for h in range(16)], ["pf3"])
            ctx.region_switch([f"pf4h{hh}" for hh in range(4)], ["pf4"])
            ctx.region_switch([f"pf5h{hh}" for hh in range(4)], ["pf5"])
            for dc in range(NDC):
                EM("sync", lambda e, dc=dc: e.dma_start(out=xh2f(dc), in_=xh2_s[dc * P:(dc + 1) * P, :]),
                   reads=[f"XH2d{dc}h0", f"XH2d{dc}h1"], writes=[f"xh2f{dc}"])

            def load_wblk(src, og, par):
                EM("gpsimd", lambda e: e.dma_start(
                    out=ar3[:, par * 8192:(par + 1) * 8192], in_=src[og]),
                    writes=[f"wblk{par}"])

            def proj_group(tc16, par, bank, bias_slot):
                for dc in range(NDC):
                    kw = dict(reads=[f"xh2f{dc}", f"wblk{par}"])
                    if dc == 0:
                        kw["acquire"] = [f"pf{bank}"]
                    EM("tensor", lambda e, dc=dc, tc16=tc16, par=par, bank=bank: e.matmul(
                        pf[bank][:, 0:512], xh2f(dc)[:, tc16 * P:(tc16 + 1) * P],
                        wblk(par, dc)[:], start=(dc == 0), stop=False), **kw)
                EM("tensor", lambda e, tc16=tc16, bank=bank, bias_slot=bias_slot: e.matmul(
                    pf[bank][:, 0:512], ones_r[0:1, 0:P],
                    biasrow[0:1, bias_slot * 512:(bias_slot + 1) * 512],
                    start=False, stop=True),
                    reads=["ones_r", f"bias{bias_slot}"], release=[f"pf{bank}"])

            def elu_evac(bank, dst, dres):
                EM("vector", lambda e: e.tensor_scalar(escr(0), pf[bank][:, 0:512], 0.0, None, op0=ALU.min),
                   reads=[f"pf{bank}"], writes=["escr0"])
                EM("scalar", lambda e: e.activation(escr(0), escr(0), AF.Exp),
                   reads=["escr0"], writes=["escr0"])
                EM("vector", lambda e: e.tensor_scalar(escr(1), pf[bank][:, 0:512], 0.0, None, op0=ALU.max),
                   reads=[f"pf{bank}"], writes=["escr1"])
                EM("vector", lambda e: e.tensor_tensor(dst, escr(0), escr(1), op=ALU.add),
                   reads=["escr0", "escr1"], writes=[dres])

            nbias = [0]

            def load_bias(src, slot, og):
                EM("gpsimd", lambda e: e.dma_start(
                    out=biasrow[0:1, slot * 512:(slot + 1) * 512], in_=src[0:1, og * 512:(og + 1) * 512]),
                    writes=[f"bias{slot}"])

            pidx = 0
            for og in range(NOG):
                load_bias(bq_d, 0, og)
                load_bias(bk_d, 1, og)
                load_bias(bv_d, 2, og)
                load_wblk(wq_d, og, pidx % 2)
                for tc16 in range(16):
                    bank = tc16 % 2
                    proj_group(tc16, pidx % 2, bank, 0)
                    elu_evac(bank, qstg(tc16 % 2)[:], f"qstg{tc16 % 2}")
                    EM("sync", lambda e, tc16=tc16, og=og: e.dma_start(
                        out=qpt_s[tc16 * P:(tc16 + 1) * P, og * 512:(og + 1) * 512],
                        in_=qstg(tc16 % 2)[:]),
                        reads=[f"qstg{tc16 % 2}"], writes=[f"QPT{tc16}o{og}"])
                pidx += 1
                load_wblk(wk_d, og, pidx % 2)
                for tc16 in range(16):
                    bank = tc16 % 2
                    proj_group(tc16, pidx % 2, bank, 1)
                    elu_evac(bank, kpblk(tc16)[:], f"kp{tc16}")
                pidx += 1
                load_wblk(wv_d, og, pidx % 2)
                for tc16 in range(16):
                    bank = tc16 % 2
                    proj_group(tc16, pidx % 2, bank, 2)
                    EM("vector", lambda e, tc16=tc16, bank=bank: e.tensor_copy(vblk(tc16)[:], pf[bank][:, 0:512]),
                       reads=[f"pf{bank}"], writes=[f"v{tc16}"])
                pidx += 1
                kvbank = 4 + (og % 2)
                for hh in range(4):
                    for tc16 in range(16):
                        kw = dict(reads=[f"kp{tc16}", f"v{tc16}"])
                        if tc16 == 0:
                            kw["acquire"] = [f"pf{kvbank}h{hh}"]
                        if tc16 == 15:
                            kw["release"] = [f"pf{kvbank}h{hh}"]
                        EM("tensor", lambda e, hh=hh, tc16=tc16, kvbank=kvbank: e.matmul(
                            pf[kvbank][:, hh * P:(hh + 1) * P],
                            kpblk(tc16)[:, hh * P:(hh + 1) * P],
                            vblk(tc16)[:, hh * P:(hh + 1) * P],
                            start=(tc16 == 0), stop=(tc16 == 15)), **kw)
                for hh in range(4):
                    for tc16 in range(16):
                        kw = dict(reads=[f"kp{tc16}", "ones_c"])
                        if tc16 == 0:
                            kw["acquire"] = [f"pf3h{og * 4 + hh}"]
                        if tc16 == 15:
                            kw["release"] = [f"pf3h{og * 4 + hh}"]
                        EM("tensor", lambda e, hh=hh, tc16=tc16, og=og: e.matmul(
                            pf[3][:, og * 4 + hh: og * 4 + hh + 1],
                            kpblk(tc16)[:, hh * P:(hh + 1) * P],
                            ones_c[0:P, 0:1],
                            start=(tc16 == 0), stop=(tc16 == 15)), **kw)
                EM("vector", lambda e, og=og, kvbank=kvbank: e.tensor_copy(
                    kvsb[:, og * 512:(og + 1) * 512], pf[kvbank][:, 0:512]),
                    reads=[f"pf{kvbank}h{hh2}" for hh2 in range(4)], writes=[f"kvsb{og}"])
            EM("vector", lambda e: e.tensor_copy(kvsb[:, 2048:2064], pf[3][:, 0:16]),
               reads=[f"pf3h{h2}" for h2 in range(16)], writes=["kvsb_kp"])
            EM("gpsimd", lambda e: e.dma_start(out=kvi_s.ap()[:, :], in_=ar4[:, 0:2080]),
               reads=[f"kvsb{og}" for og in range(4)] + ["kvsb_kp"], writes=["KVI"])
            if _SKIP_CC:
                EM("gpsimd", lambda e: e.dma_start(out=kvo_s.ap()[:, :], in_=kvi_s.ap()[:, :]),
                   reads=["KVI"], writes=["KVO"])
            else:
                EM("gpsimd", lambda e: e.collective_compute(
                    "AllReduce", ALU.add,
                    replica_groups=[[0, 1], [2, 3], [4, 5], [6, 7]],
                    ins=[kvi_s.ap().opt()], outs=[kvo_s.ap().opt()]).then_inc(cc_sem),
                    reads=["KVI"], writes=[], inc=0)
                EM("gpsimd", lambda e: (e.wait_ge(cc_sem, 1), e.nop())[1],
                   reads=[], writes=["KVO"], inc=16)
            ctx.region_switch(["kvsball"], [f"kvsb{og}" for og in range(4)] + ["kvsb_kp"])
            EM("gpsimd", lambda e: e.dma_start(out=ar4[:, 0:2080], in_=kvo_s.ap()[:, :]),
               reads=["KVO"], writes=["kvsball"])

        def emit_phaseB():
            ctx.region_switch([f"wo{vc}" for vc in range(NDC)],
                              [f"xh2f{dc}" for dc in range(NDC)])
            ctx.region_switch([f"qpfm{qc}" for qc in range(NDC)] + [f"qptm{ts}" for ts in range(4)]
                              + ["atm0", "atm1"],
                              [f"kp{t}" for t in range(16)] + [f"v{t}" for t in range(16)]
                              + ["qstg0", "qstg1", "escr0", "escr1"])
            ctx.region_switch([f"afm{vc}" for vc in range(NDC)] + ["x1t0", "x1t1", "x2t0", "x2t1"],
                              ["wblk0", "wblk1"])
            ctx.region_switch(["kvbf", "zsel"], [f"hid{fc}t{tt}" for fc in range(NFC) for tt in range(2)])
            ctx.region_switch(["pf3"], [f"pf3h{h}" for h in range(16)])
            ctx.region_switch(["pf4"], [f"pf4h{hh}" for hh in range(4)])
            ctx.region_switch(["pf5"], [f"pf5h{hh}" for hh in range(4)])
            EM("vector", lambda e: e.tensor_copy(kvbf[:], kvsb[:, 0:2048]),
               reads=["kvsball"], writes=["kvbf"])
            EM("vector", lambda e: e.memset(zsel[:], 0.0), writes=["zsel"])
            for h in range(H):
                EM("vector", lambda e, h=h: e.tensor_copy(zsel[:, h * 16 + h: h * 16 + h + 1],
                                                          kvsb[:, 2048 + h: 2048 + h + 1]),
                   reads=["kvsball"], writes=["zsel"])
            EM("gpsimd", lambda e: e.dma_start(out=biasrow[0:1, :], in_=bo_d[0:1, :]),
               writes=["bias0", "bias1", "bias2", "bias3"])
            EM("gpsimd", lambda e: e.dma_start(out=ar1b[:, 0:32768], in_=wo_d[:, :]),
               writes=[f"wo{vc}" for vc in range(NDC)])

            for tt in range(4):
                for ts in range(4):
                    tc16 = tt * 4 + ts
                    EM("sync", lambda e, ts=ts, tc16=tc16: e.dma_start(
                        out=qptm(ts)[:], in_=qpt_s[tc16 * P:(tc16 + 1) * P, :]),
                        reads=[f"QPT{tc16}o{og}" for og in range(4)], writes=[f"qptm{ts}"])
                    for qc in range(NDC):
                        bank = qc % 2
                        EM("tensor", lambda e, qc=qc, ts=ts, bank=bank: e.transpose(
                            pb[bank][:, 0:P], qptm(ts)[:, qc * P:(qc + 1) * P], identB[:]),
                            reads=[f"qptm{ts}", "identB"], acquire=[f"pb{bank}"], release=[f"pb{bank}"])
                        EM("vector", lambda e, qc=qc, ts=ts, bank=bank: e.tensor_copy(
                            qpfm(qc)[:, ts * P:(ts + 1) * P], pb[bank][:, 0:P]),
                            reads=[f"pb{bank}"], writes=[f"qpfm{qc}ts{ts}"])
                    zpar = ts % 2
                    for qc in range(NDC):
                        kw = dict(reads=[f"qpfm{qc}ts{ts}", "zsel"])
                        if qc == 0:
                            kw["acquire"] = ["pf0"]
                        if qc == NDC - 1:
                            kw["release"] = ["pf0"]
                        EM("tensor", lambda e, qc=qc, ts=ts: e.matmul(
                            pf[0][:, 0:16], qpfm(qc)[:, ts * P:(ts + 1) * P],
                            zsel[:, qc * 16:(qc + 1) * 16],
                            start=(qc == 0), stop=(qc == NDC - 1)), **kw)
                    EM("vector", lambda e, zpar=zpar: e.tensor_scalar(zinvv(zpar), pf[0][:, 0:16], Z_EPS, None, op0=ALU.add),
                       reads=["pf0"], writes=[f"zinv{zpar}"])
                    EM("vector", lambda e, zpar=zpar: e.reciprocal(zinvv(zpar), zinvv(zpar)),
                       reads=[f"zinv{zpar}"], writes=[f"zinv{zpar}"])
                    apar = ts % 2
                    for og in range(NOG):
                        bank = 1 + (og % 2)
                        for hh in range(4):
                            h = og * 4 + hh
                            kw = dict(reads=[f"qpfm{h}ts{ts}", "kvbf"])
                            if hh == 0:
                                kw["acquire"] = [f"pf{bank}"]
                            if hh == 3:
                                kw["release"] = [f"pf{bank}"]
                            EM("tensor", lambda e, h=h, hh=hh, ts=ts, bank=bank: e.matmul(
                                pf[bank][:, hh * P:(hh + 1) * P],
                                qpfm(h)[:, ts * P:(ts + 1) * P],
                                kvbf[:, h * P:(h + 1) * P],
                                start=(hh == 0), stop=(hh == 3)), **kw)
                        for hh in range(4):
                            h = og * 4 + hh
                            EM("vector", lambda e, h=h, hh=hh, og=og, bank=bank, apar=apar, zpar=zpar: e.tensor_scalar(
                                atm(apar)[:, h * P:(h + 1) * P], pf[bank][:, hh * P:(hh + 1) * P],
                                zinvv(zpar)[:, h:h + 1], None, op0=ALU.mult),
                                reads=[f"pf{bank}", f"zinv{zpar}"], writes=[f"atm{apar}b{h}"])
                    for vc in range(NDC):
                        bank = vc % 2
                        EM("tensor", lambda e, vc=vc, apar=apar, bank=bank: e.transpose(
                            pb[bank][:, 0:P], atm(apar)[:, vc * P:(vc + 1) * P], identB[:]),
                            reads=[f"atm{apar}b{vc}", "identB"], acquire=[f"pb{bank}"], release=[f"pb{bank}"])
                        EM("vector", lambda e, vc=vc, ts=ts, bank=bank: e.tensor_copy(
                            afm(vc)[:, ts * P:(ts + 1) * P], pb[bank][:, 0:P]),
                            reads=[f"pb{bank}"], writes=[f"afm{vc}ts{ts}"])
                    xpar = ts % 2
                    EM("sync", lambda e, xpar=xpar, tc16=tc16: e.dma_start(
                        out=x1t(xpar), in_=x1_s[tc16 * P:(tc16 + 1) * P, :]),
                        reads=[f"X1S{tc16}"], writes=[f"x1t{xpar}"])
                    for og in range(NOG):
                        bank = 3 + (og % 2)
                        for vc in range(NDC):
                            kw = dict(reads=[f"afm{vc}ts{ts}", f"wo{vc}"])
                            if vc == 0:
                                kw["acquire"] = [f"pf{bank}"]
                            EM("tensor", lambda e, vc=vc, ts=ts, og=og, bank=bank: e.matmul(
                                pf[bank][:, 0:512], afm(vc)[:, ts * P:(ts + 1) * P],
                                wovw(vc)[:, og * 512:(og + 1) * 512],
                                start=(vc == 0), stop=False), **kw)
                        EM("tensor", lambda e, og=og, bank=bank: e.matmul(
                            pf[bank][:, 0:512], ones_r[0:1, 0:P],
                            biasrow[0:1, og * 512:(og + 1) * 512],
                            start=False, stop=True),
                            reads=["ones_r", f"bias{og}"], release=[f"pf{bank}"])
                        EM("vector", lambda e, og=og, bank=bank, xpar=xpar: e.tensor_tensor(
                            x2t(xpar)[:, og * 512:(og + 1) * 512], pf[bank][:, 0:512],
                            x1t(xpar)[:, og * 512:(og + 1) * 512], op=ALU.add),
                            reads=[f"pf{bank}", f"x1t{xpar}"], writes=[f"x2t{xpar}"])
                    EM("sync", lambda e, xpar=xpar, tc16=tc16: e.dma_start(
                        out=x2_s[tc16 * P:(tc16 + 1) * P, :], in_=x2t(xpar)),
                        reads=[f"x2t{xpar}"], writes=[f"X2S{tc16}", f"SRC2_{tc16}"])

        def emit_phaseC():
            ctx.region_switch([f"X{t}" for t in range(8)], [f"wo{vc}" for vc in range(NDC)])
            ctx.region_switch([f"xh{dc}c{tc8}" for dc in range(NDC) for tc8 in range(8)]
                              + ["xhtm0", "xhtm1"],
                              [f"qpfm{qc}ts{ts}" for qc in range(NDC) for ts in range(4)]
                              + [f"qptm{ts}" for ts in range(4)]
                              + [f"atm{i}b{h}" for i in range(2) for h in range(H)])
            ctx.region_switch([f"wg{dc}" for dc in range(NDC)] + [f"wu{dc}" for dc in range(NDC)]
                              + [f"wd{p}_{fc}" for p in range(2) for fc in range(NFC)],
                              [f"afm{vc}ts{ts}" for vc in range(NDC) for ts in range(4)]
                              + ["x1t0", "x1t1", "x2t0", "x2t1"])
            ctx.region_switch([f"hid{fc}t{tt}" for fc in range(NFC) for tt in range(2)],
                              ["kvbf", "zsel"])
            for half in range(2):
                for tc8 in range(8):
                    tc16 = half * 8 + tc8
                    ctx.region_switch([f"SRC{tc16}"], [f"SRC2_{tc16}"])
                emit_ffn_half(half, x2_s, out_d)

        emit_phaseA_half(0)
        emit_phaseA_half(1)
        if _BUILD_UPTO == "ffn1":
            for tc16 in range(16):
                EM("sync", lambda e, tc16=tc16: e.dma_start(
                    out=out_d[tc16 * P:(tc16 + 1) * P, :], in_=x1_s[tc16 * P:(tc16 + 1) * P, :]),
                    reads=[f"X1S{tc16}"], writes=[])
        else:
            emit_attnproj()
            emit_phaseB()
            if _BUILD_UPTO == "attn":
                for tc16 in range(16):
                    EM("sync", lambda e, tc16=tc16: e.dma_start(
                        out=out_d[tc16 * P:(tc16 + 1) * P, :], in_=x2_s[tc16 * P:(tc16 + 1) * P, :]),
                        reads=[f"X2S{tc16}"], writes=[])
            else:
                emit_phaseC()

        ctx.final_wait("sync", ["sync", "gpsimd"])
        ctx.run(block)
    return nc, ctx


def kernel(L0_attn_hidden_states, L0_attn_q_w, L0_attn_q_b, L0_attn_k_w, L0_attn_k_b,
           L0_attn_v_w, L0_attn_v_b, L0_attn_o_w, L0_attn_o_b,
           L0_ffn_w_gate_router, L0_ffn_w_gate_proj, L0_ffn_w_up_proj, L0_ffn_w_down_proj):
    f32 = np.float32
    hs = np.asarray(L0_attn_hidden_states, f32)
    c = np.ascontiguousarray

    def pack_qkv(w):
        return c(np.asarray(w, f32).T.reshape(16, P, 4, 512)
                 .transpose(2, 1, 0, 3).reshape(4, P, 8192))

    wq = pack_qkv(L0_attn_q_w)
    wk = pack_qkv(L0_attn_k_w)
    wv = pack_qkv(L0_attn_v_w)
    wo = c(np.asarray(L0_attn_o_w, f32).T.reshape(16, P, 2048)
           .transpose(1, 0, 2).reshape(P, 32768))
    bq = c(np.asarray(L0_attn_q_b, f32).reshape(1, D))
    bk = c(np.asarray(L0_attn_k_b, f32).reshape(1, D))
    bv = c(np.asarray(L0_attn_v_b, f32).reshape(1, D))
    bo = c(np.asarray(L0_attn_o_b, f32).reshape(1, D))
    wr = c(np.asarray(L0_ffn_w_gate_router, f32).T)
    def pack_gu(w):
        return c(np.asarray(w, f32).transpose(0, 2, 1).reshape(E, 16, P, FF)
                 .transpose(0, 2, 1, 3).reshape(E, P, 12288))

    wg = pack_gu(L0_ffn_w_gate_proj)
    wu = pack_gu(L0_ffn_w_up_proj)
    wd = c(np.asarray(L0_ffn_w_down_proj, f32).transpose(0, 2, 1)
           .reshape(E, NFC, P, 4, 512).transpose(0, 3, 2, 1, 4)
           .reshape(E, 4, P, 3072))
    ident = np.eye(P, dtype=f32)

    nc, _ = build_program()
    shared = {"wq": wq, "wk": wk, "wv": wv, "wo": wo,
              "bq": bq, "bk": bk, "bv": bv, "bo": bo,
              "wr": wr, "wg": wg, "wu": wu, "wd": wd, "ident": ident}
    in_maps = []
    for core in range(8):
        b, h = core // 2, core % 2
        xc = c(hs[b, h * T:(h + 1) * T, :])
        in_maps.append({"x": xc, **shared})
    res = run_bass_kernel_spmd(nc, in_maps, core_ids=list(range(8)))
    out = np.empty((4, 4096, D), f32)
    for core in range(8):
        b, h = core // 2, core % 2
        out[b, h * T:(h + 1) * T, :] = res.results[core]["out"]
    return out


# version 10
# speedup vs baseline: 1.4295x; 1.0570x over previous
import numpy as np
import concourse.bass as bass
import concourse.mybir as mybir
from concourse.bass_utils import run_bass_kernel_spmd

F32 = mybir.dt.float32
BF16 = mybir.dt.bfloat16
AF = mybir.ActivationFunctionType
ALU = mybir.AluOpType
AX = mybir.AxisListType

P = 128
T = 2048
TH = 1024
D = 2048
FF = 768
E = 8
H = 16
NDC = D // P
NFC = FF // P
NOG = D // 512
LN_EPS = 1e-5
Z_EPS = 1e-6

_BUILD_UPTO = "full"
_SERIALIZE = False
_DMA_CONSERVATIVE = True
_SKIP_CC = False


class Ctx:

    ENGINES = ("sync", "gpsimd", "vector", "scalar", "tensor")
    DMA_ENGINES = ("sync", "gpsimd")

    def __init__(self, nc, stack):
        self.nc = nc
        self.sems = {e: stack.enter_context(nc.semaphore(f"sem_{e}")) for e in self.ENGINES}
        self.cnt = {e: 0 for e in self.ENGINES}
        self.last_wait = {}
        self.writer = {}
        self.readers = {}
        self.plans = {e: [] for e in self.ENGINES}
        self.n_waits = 0
        self.n_instr = 0

    def _need(self, needs, eng, cnt):
        if cnt > needs.get(eng, 0):
            needs[eng] = cnt

    serialize_now = False

    def emit(self, eng, fn, reads=(), writes=(), acquire=(), release=(), inc=None):
        needs = {}
        if _SERIALIZE or self.serialize_now:
            for teng in self.ENGINES:
                if self.cnt[teng] > 0:
                    needs[teng] = self.cnt[teng]
        for r in reads:
            for weng, wcnt in self.writer.get(r, {}).items():
                self._need(needs, weng, wcnt)
        for wres in tuple(writes) + tuple(acquire):
            for weng, wcnt in self.writer.get(wres, {}).items():
                self._need(needs, weng, wcnt)
            for reng, rcnt in self.readers.get(wres, {}).items():
                self._need(needs, reng, rcnt)
        if _DMA_CONSERVATIVE:
            for teng in self.DMA_ENGINES:
                if teng in needs:
                    needs[teng] = self.cnt[teng]
        wl = []
        for teng, val in needs.items():
            key = (eng, teng)
            if self.last_wait.get(key, 0) >= val:
                continue
            self.last_wait[key] = val
            wl.append((self.sems[teng], val))
        if inc is None:
            inc = 16 if eng in self.DMA_ENGINES else 1
        self.cnt[eng] += inc
        my = self.cnt[eng]
        sem = self.sems[eng]
        self.n_waits += len(wl)
        self.n_instr += 1

        def thunk(e, fn=fn, wl=wl, sem=sem, inc=inc):
            for s, v in wl:
                e.wait_ge(s, v)
            ins = fn(e)
            if inc:
                ins.then_inc(sem, inc)

        self.plans[eng].append(thunk)
        for r in reads:
            self.readers.setdefault(r, {})[eng] = my
        for wres in tuple(writes) + tuple(release):
            self.writer[wres] = {eng: my}
            self.readers[wres] = {}
        return my

    def region_switch(self, news, olds):
        merged = {}
        for o in olds:
            for eng, cnt in self.writer.pop(o, {}).items():
                if cnt > merged.get(eng, 0):
                    merged[eng] = cnt
            for eng, cnt in self.readers.pop(o, {}).items():
                if cnt > merged.get(eng, 0):
                    merged[eng] = cnt
        for n in news:
            w = dict(merged)
            for eng, cnt in self.writer.get(n, {}).items():
                if cnt > w.get(eng, 0):
                    w[eng] = cnt
            for eng, cnt in self.readers.get(n, {}).items():
                if cnt > w.get(eng, 0):
                    w[eng] = cnt
            self.writer[n] = w
            self.readers[n] = {}

    def final_wait(self, eng, targets):
        wl = [(self.sems[t], self.cnt[t]) for t in targets if self.cnt[t] > 0]

        def thunk(e, wl=wl):
            for s, v in wl:
                e.wait_ge(s, v)

        self.plans[eng].append(thunk)

    def run(self, block):
        for name in self.ENGINES:
            plan = self.plans[name]

            def make(plan):
                def body(e):
                    for t in plan:
                        t(e)
                return body

            getattr(block, name)(make(plan))


def build_program():
    nc = bass.Bass("TRN2", target_bir_lowering=False, debug=False)

    x_d = nc.dram_tensor("x", [T, D], F32, kind="ExternalInput").ap()
    wq_d = nc.dram_tensor("wq", [NOG, P, 8192], F32, kind="ExternalInput").ap()
    wk_d = nc.dram_tensor("wk", [NOG, P, 8192], F32, kind="ExternalInput").ap()
    wv_d = nc.dram_tensor("wv", [NOG, P, 8192], F32, kind="ExternalInput").ap()
    wo_d = nc.dram_tensor("wo", [P, 32768], F32, kind="ExternalInput").ap()
    bq_d = nc.dram_tensor("bq", [1, D], F32, kind="ExternalInput").ap()
    bk_d = nc.dram_tensor("bk", [1, D], F32, kind="ExternalInput").ap()
    bv_d = nc.dram_tensor("bv", [1, D], F32, kind="ExternalInput").ap()
    bo_d = nc.dram_tensor("bo", [1, D], F32, kind="ExternalInput").ap()
    wr_d = nc.dram_tensor("wr", [D, E], F32, kind="ExternalInput").ap()
    wg_d = nc.dram_tensor("wg", [E, P, 12288], F32, kind="ExternalInput").ap()
    wu_d = nc.dram_tensor("wu", [E, P, 12288], F32, kind="ExternalInput").ap()
    wd_d = nc.dram_tensor("wd", [E, NOG, P, 3072], F32, kind="ExternalInput").ap()
    id_d = nc.dram_tensor("ident", [P, P], F32, kind="ExternalInput").ap()
    out_d = nc.dram_tensor("out", [T, D], F32, kind="ExternalOutput").ap()

    wg_s = nc.dram_tensor("wg_s", [E, P, 12288], BF16).ap()
    wu_s = nc.dram_tensor("wu_s", [E, P, 12288], BF16).ap()
    wd_s = nc.dram_tensor("wd_s", [E, NOG, P, 3072], BF16).ap()
    xh2_s = nc.dram_tensor("xh2_s", [D, T], BF16).ap()
    qpt_s = nc.dram_tensor("qpt_s", [T, D], BF16).ap()
    x1_s = nc.dram_tensor("x1_s", [T, D], F32).ap()
    x2_s = nc.dram_tensor("x2_s", [T, D], F32).ap()
    kvi_s = nc.dram_tensor("kvi_s", [P, 2080], F32)
    kvo_s = nc.dram_tensor("kvo_s", [P, 2080], F32)

    from contextlib import ExitStack
    with ExitStack() as st:
        ar1 = st.enter_context(nc.sbuf_tensor("ar1", [P, 16384], F32))
        ar2 = st.enter_context(nc.sbuf_tensor("ar2", [P, 20480], BF16))
        ar3 = st.enter_context(nc.sbuf_tensor("ar3", [P, 30720], BF16))
        ar4 = st.enter_context(nc.sbuf_tensor("ar4", [P, 3840], F32))
        ar5 = st.enter_context(nc.sbuf_tensor("ar5", [P, 6144], BF16))
        biasrow = st.enter_context(nc.sbuf_tensor("biasrow", [1, D], BF16))
        identB = st.enter_context(nc.sbuf_tensor("identB", [P, P], BF16))
        ones_r = st.enter_context(nc.sbuf_tensor("ones_r", [1, P], BF16))
        ones_c = st.enter_context(nc.sbuf_tensor("ones_c", [P, 1], BF16))
        wrt = st.enter_context(nc.sbuf_tensor("wrt", [P, P], BF16))

        pf = [st.enter_context(nc.psum_tensor(f"pf{i}", [P, 512], F32)) for i in range(6)]
        pb = [st.enter_context(nc.psum_tensor(f"pb{i}", [P, 1024], BF16)) for i in range(2)]
        cc_sem = st.enter_context(nc.semaphore("cc_sem"))

        ar1b = ar1.bitcast(BF16)
        ar2f = ar2.bitcast(F32)
        ar3f = ar3.bitcast(F32)

        ctx = Ctx(nc, st)
        block = st.enter_context(nc.Block())

        def slotX(tc8):
            return ar1[:, tc8 * D:(tc8 + 1) * D]

        def xh2f(dc):
            return ar1b[:, dc * D:(dc + 1) * D]

        def wovw(vc):
            return ar1b[:, vc * D:(vc + 1) * D]

        def xh(dc):
            return ar2[:, dc * TH:(dc + 1) * TH]

        def xhtm(i):
            return ar2[:, 16384 + i * D: 16384 + (i + 1) * D]

        def kpblk(tc16):
            return ar2[:, tc16 * 512:(tc16 + 1) * 512]

        def vblk(tc16):
            return ar2[:, 8192 + tc16 * 512: 8192 + (tc16 + 1) * 512]

        def qstg(i):
            return ar2[:, 16384 + i * 512: 16384 + (i + 1) * 512]

        def escr(i):
            return ar2f[:, 8704 + i * 512: 8704 + (i + 1) * 512]

        def qpfm(qc):
            return ar2[:, qc * 512:(qc + 1) * 512]

        def qptm(ts):
            return ar2[:, 8192 + ts * D: 8192 + (ts + 1) * D]

        def atm(i):
            return ar2[:, 16384 + i * D: 16384 + (i + 1) * D]

        def wgv(dc):
            return ar3[:, dc * FF:(dc + 1) * FF]

        def wuv(dc):
            return ar3[:, 12288 + dc * FF: 12288 + (dc + 1) * FF]

        def wdblk(par, fc):
            return ar3[:, 24576 + par * 3072 + fc * 512: 24576 + par * 3072 + (fc + 1) * 512]

        def wblk(par, dc):
            return ar3[:, par * 8192 + dc * 512: par * 8192 + (dc + 1) * 512]

        def afm(vc):
            return ar3[:, vc * 512:(vc + 1) * 512]

        def x1t(i):
            return ar3f[:, 4096 + i * D: 4096 + (i + 1) * D]

        def x2t(i):
            return ar3f[:, 8192 + i * D: 8192 + (i + 1) * D]

        kvsb = ar4[:, 0:2080]
        STAT0 = 2080

        def statc(par, j):
            return ar4[:, STAT0 + par * 8 + j: STAT0 + par * 8 + j + 1]

        WEFF0 = 2096

        def weffc(tc16):
            return ar4[:, WEFF0 + tc16 * E: WEFF0 + (tc16 + 1) * E]

        def gsv(i):
            return ar4[:, 2224 + i * 512: 2224 + (i + 1) * 512]

        def zinvv(par):
            return ar4[:, 3248 + par * 16: 3248 + (par + 1) * 16]

        def rscr(par, j):
            return ar4[:, 3280 + par * 24 + j * 8: 3280 + par * 24 + (j + 1) * 8]

        kvbf = ar5[:, 0:2048]
        zsel = ar5[:, 2048:2304]

        def hidv(fc):
            return ar5[:, fc * TH:(fc + 1) * TH]

        EM = ctx.emit

        EM("gpsimd", lambda e: e.dma_start(out=identB[:], in_=id_d[:, :]), writes=["identB"])
        EM("gpsimd", lambda e: e.dma_start(out=wrt[:].rearrange("p (c e) -> p c e", e=E),
                                           in_=wr_d.rearrange("(c p) e -> p c e", p=P)), writes=["wrt"])
        EM("vector", lambda e: e.memset(ones_r[:], 1.0), writes=["ones_r"])
        EM("vector", lambda e: e.memset(ones_c[:], 1.0), writes=["ones_c"])

        def emit_ln_tc(half, tc8, router, dst_res_prefix, call_tag):
            tc16 = half * 8 + tc8
            par = tc8 % 2
            xt = slotX(tc8)
            xres = f"X{tc8}"
            tmres = f"xhtm{par}"
            tm = xhtm(par)
            sums, sumsq, mcol, vcol = (statc(par, j) for j in range(4))
            rstd, nmr, nm1 = (statc(par, j) for j in range(4, 7))
            sres = f"stat{par}"
            EM("scalar", lambda e: e.activation(tm[:, 0:D], xt, AF.Square, accum_out=sumsq),
               reads=[xres], writes=[tmres, sres + "sq"])
            EM("vector", lambda e: e.reduce_sum(sums, xt, axis=AX.X),
               reads=[xres], writes=[sres + "s"])
            EM("vector", lambda e: e.tensor_scalar(mcol, sums, 1.0 / D, None, op0=ALU.mult),
               reads=[sres + "s"], writes=[sres + "m"])
            EM("vector", lambda e: e.tensor_scalar(vcol, sumsq, 1.0 / D, None, op0=ALU.mult),
               reads=[sres + "sq"], writes=[sres + "v"])
            EM("vector", lambda e: e.scalar_tensor_tensor(vcol, mcol, mcol, vcol, op0=ALU.mult, op1=ALU.subtract),
               reads=[sres + "m", sres + "v"], writes=[sres + "v"])
            EM("vector", lambda e: e.tensor_scalar(vcol, vcol, -1.0, LN_EPS, op0=ALU.mult, op1=ALU.add),
               reads=[sres + "v"], writes=[sres + "v"])
            EM("scalar", lambda e: e.activation(rstd, vcol, AF.Sqrt),
               reads=[sres + "v"], writes=[sres + "r"])
            EM("vector", lambda e: e.reciprocal(rstd, rstd),
               reads=[sres + "r"], writes=[sres + "r"])
            EM("vector", lambda e: e.tensor_scalar(nmr, mcol, rstd, -1.0, op0=ALU.mult, op1=ALU.mult),
               reads=[sres + "m", sres + "r"], writes=[sres + "n"])
            EM("scalar", lambda e: e.activation(tm[:, 0:D], xt, AF.Identity, bias=nmr, scale=rstd),
               reads=[xres, sres + "r", sres + "n"], writes=[tmres])
            for dc in range(NDC):
                bank = dc % 2
                EM("tensor", lambda e, dc=dc, bank=bank: e.transpose(
                    pb[bank][:, 0:P], tm[:, dc * P:(dc + 1) * P], identB[:]),
                    reads=[tmres, "identB"], acquire=[f"pb{bank}"], release=[f"pb{bank}"])
                EM("vector", lambda e, dc=dc, bank=bank: e.tensor_copy(
                    xh(dc)[:, tc8 * P:(tc8 + 1) * P], pb[bank][:, 0:P]),
                    reads=[f"pb{bank}"], writes=[f"{dst_res_prefix}{dc}c{tc8}"])
            if router:
                ps = pf[4]
                for dc in range(NDC):
                    kw = dict(reads=[f"{dst_res_prefix}{dc}c{tc8}", "wrt"])
                    if dc == 0:
                        kw["acquire"] = ["pf4"]
                    if dc == NDC - 1:
                        kw["release"] = ["pf4"]
                    EM("tensor", lambda e, dc=dc: e.matmul(
                        ps[:, 0:E], xh(dc)[:, tc8 * P:(tc8 + 1) * P], wrt[:, dc * E:(dc + 1) * E],
                        start=(dc == 0), stop=(dc == NDC - 1)), **kw)
                m1 = rscr(par, 0)[:, 0:1]
                m2c = rscr(par, 0)[:, 2:3]
                nm1c = statc(par, 6)
                msk = rscr(par, 1)
                unn = rscr(par, 2)
                den = rscr(par, 0)[:, 1:2]
                rr = f"rt{par}"
                EM("vector", lambda e: e.reduce_max(m1, ps[0:P, 0:E], axis=AX.X),
                   reads=["pf4"], writes=[rr + "m1"])
                EM("vector", lambda e: e.tensor_scalar(msk, ps[0:P, 0:E], m1, None, op0=ALU.is_ge),
                   reads=["pf4", rr + "m1"], writes=[rr + "msk"])
                EM("vector", lambda e: e.scalar_tensor_tensor(msk, msk, -1e30, ps[0:P, 0:E], op0=ALU.mult, op1=ALU.add),
                   reads=["pf4", rr + "msk"], writes=[rr + "msk"])
                EM("vector", lambda e: e.reduce_max(m2c, msk, axis=AX.X),
                   reads=[rr + "msk"], writes=[rr + "m2"])
                EM("vector", lambda e: e.tensor_scalar(msk, ps[0:P, 0:E], m2c, None, op0=ALU.is_ge),
                   reads=["pf4", rr + "m2"], writes=[rr + "msk"])
                EM("vector", lambda e: e.tensor_scalar(nm1c, m1, -1.0, None, op0=ALU.mult),
                   reads=[rr + "m1"], writes=[rr + "nm1"])
                EM("scalar", lambda e: e.activation(unn, ps[0:P, 0:E], AF.Exp, bias=nm1c),
                   reads=["pf4", rr + "nm1"], writes=[rr + "unn"])
                EM("vector", lambda e: e.tensor_tensor(unn, unn, msk, op=ALU.mult),
                   reads=[rr + "unn", rr + "msk"], writes=[rr + "unn"])
                EM("vector", lambda e: e.reduce_sum(den, unn, axis=AX.X),
                   reads=[rr + "unn"], writes=[rr + "den"])
                EM("vector", lambda e: e.reciprocal(den, den),
                   reads=[rr + "den"], writes=[rr + "den"])
                EM("vector", lambda e: e.tensor_scalar(weffc(tc16), unn, den, 0.5, op0=ALU.mult, op1=ALU.mult),
                   reads=[rr + "unn", rr + "den"], writes=[f"weff{tc16}"])

        def emit_ffn_half(half, x_src, store_out, wsrc="bf16", writeback=False):
            for tc8 in range(8):
                tc16 = half * 8 + tc8
                EM("sync", lambda e, tc8=tc8, tc16=tc16: e.dma_start(
                    out=slotX(tc8), in_=x_src[tc16 * P:(tc16 + 1) * P, :]),
                    writes=[f"X{tc8}"], reads=[f"SRC{tc16}"])
            for tc8 in range(8):
                emit_ln_tc(half, tc8, router=True, dst_res_prefix="xh", call_tag="ffn")

            wgreg = ar3[:, 0:12288]
            wureg = ar3[:, 12288:24576]

            def wdreg(par):
                return ar3[:, 24576 + par * 3072: 24576 + (par + 1) * 3072]

            wgres = [f"wg{dc}" for dc in range(NDC)]
            wures = [f"wu{dc}" for dc in range(NDC)]

            def load_gu(e_i):
                if wsrc == "f32":
                    EM("gpsimd", lambda e, e_i=e_i: e.dma_start(out=wgreg, in_=wg_d[e_i]),
                       writes=wgres)
                    EM("gpsimd", lambda e, e_i=e_i: e.dma_start(out=wureg, in_=wu_d[e_i]),
                       writes=wures)
                    if writeback:
                        EM("sync", lambda e, e_i=e_i: e.dma_start(out=wg_s[e_i], in_=wgreg),
                           reads=wgres, writes=[f"wgS{e_i}"])
                        EM("sync", lambda e, e_i=e_i: e.dma_start(out=wu_s[e_i], in_=wureg),
                           reads=wures, writes=[f"wuS{e_i}"])
                else:
                    EM("sync", lambda e, e_i=e_i: e.dma_start(out=wgreg, in_=wg_s[e_i]),
                       reads=[f"wgS{e_i}"], writes=wgres)
                    EM("sync", lambda e, e_i=e_i: e.dma_start(out=wureg, in_=wu_s[e_i]),
                       reads=[f"wuS{e_i}"], writes=wures)

            def load_wd(e_i, og, par):
                wdres = [f"wd{par}_{fc}" for fc in range(NFC)]
                if wsrc == "f32":
                    EM("gpsimd", lambda e, e_i=e_i, og=og, par=par: e.dma_start(
                        out=wdreg(par), in_=wd_d[e_i, og]), writes=wdres)
                    if writeback:
                        EM("sync", lambda e, e_i=e_i, og=og, par=par: e.dma_start(
                            out=wd_s[e_i, og], in_=wdreg(par)),
                            reads=wdres, writes=[f"wdS{e_i}_{og}"])
                else:
                    EM("gpsimd", lambda e, e_i=e_i, og=og, par=par: e.dma_start(
                        out=wdreg(par), in_=wd_s[e_i, og]),
                        reads=[f"wdS{e_i}_{og}"], writes=wdres)

            for e_i in range(E):
                if e_i == 0:
                    load_gu(0)
                for tt in range(2):
                    xh_reads = [f"xh{dc}c{tc8}" for dc in range(1) for tc8 in range(4 * tt, 4 * tt + 4)]
                    for fc in range(NFC):
                        gpar = fc % 2
                        gsb = gsv(gpar)
                        gb = (0, 2)[fc % 2]
                        ub = (1, 3)[fc % 2]
                        for dc in range(NDC):
                            kw = dict(reads=[f"wg{dc}"] + [f"xh{dc}c{tc8}" for tc8 in range(4 * tt, 4 * tt + 4)])
                            if dc == 0:
                                kw["acquire"] = [f"pf{gb}"]
                            if dc == NDC - 1:
                                kw["release"] = [f"pf{gb}"]
                            EM("tensor", lambda e, dc=dc, fc=fc, tt=tt, gb=gb: e.matmul(
                                pf[gb][:, 0:512], wgv(dc)[:, fc * P:(fc + 1) * P],
                                xh(dc)[:, tt * 512:(tt + 1) * 512],
                                start=(dc == 0), stop=(dc == NDC - 1)), **kw)
                        EM("scalar", lambda e, gsb=gsb, gb=gb: e.activation(gsb, pf[gb][:, 0:512], AF.Silu),
                           reads=[f"pf{gb}"], writes=[f"gs{gpar}"])
                        for dc in range(NDC):
                            kw = dict(reads=[f"wu{dc}"] + [f"xh{dc}c{tc8}" for tc8 in range(4 * tt, 4 * tt + 4)])
                            if dc == 0:
                                kw["acquire"] = [f"pf{ub}"]
                            if dc == NDC - 1:
                                kw["release"] = [f"pf{ub}"]
                            EM("tensor", lambda e, dc=dc, fc=fc, tt=tt, ub=ub: e.matmul(
                                pf[ub][:, 0:512], wuv(dc)[:, fc * P:(fc + 1) * P],
                                xh(dc)[:, tt * 512:(tt + 1) * 512],
                                start=(dc == 0), stop=(dc == NDC - 1)), **kw)
                        EM("vector", lambda e, fc=fc, tt=tt, gsb=gsb, ub=ub: e.tensor_tensor(
                            hidv(fc)[:, tt * 512:(tt + 1) * 512], pf[ub][:, 0:512], gsb, op=ALU.mult),
                            reads=[f"pf{ub}", f"gs{gpar}"], writes=[f"hid{fc}t{tt}"])
                load_wd(e_i, 0, 0)
                load_wd(e_i, 1, 1)
                for og in range(NOG):
                    par = og % 2
                    for tc8 in range(8):
                        tt = tc8 // 4
                        bank = 2 + ((og * 8 + tc8) % 4)
                        for fc in range(NFC):
                            kw = dict(reads=[f"wd{par}_{fc}", f"hid{fc}t{tt}"])
                            if fc == 0:
                                kw["acquire"] = [f"pf{bank}"]
                            if fc == NFC - 1:
                                kw["release"] = [f"pf{bank}"]
                            EM("tensor", lambda e, fc=fc, tc8=tc8, og=og, par=par, bank=bank: e.matmul(
                                pf[bank][:, 0:512],
                                hidv(fc)[:, tc8 * P:(tc8 + 1) * P],
                                wdblk(par, fc)[:],
                                start=(fc == 0), stop=(fc == NFC - 1)), **kw)
                        tc16 = half * 8 + tc8
                        EM("vector", lambda e, tc8=tc8, og=og, bank=bank, e_i=e_i, tc16=tc16: e.scalar_tensor_tensor(
                            slotX(tc8)[:, og * 512:(og + 1) * 512],
                            pf[bank][:, 0:512],
                            weffc(tc16)[:, e_i:e_i + 1],
                            slotX(tc8)[:, og * 512:(og + 1) * 512],
                            op0=ALU.mult, op1=ALU.add),
                            reads=[f"pf{bank}", f"weff{tc16}", f"X{tc8}"], writes=[f"X{tc8}"])
                    if og + 2 < NOG:
                        load_wd(e_i, og + 2, (og + 2) % 2)
                    if og == 1 and e_i + 1 < E:
                        load_gu(e_i + 1)
            if store_out is not None:
                for tc8 in range(8):
                    tc16 = half * 8 + tc8
                    EM("sync", lambda e, tc8=tc8, tc16=tc16: e.dma_start(
                        out=store_out[tc16 * P:(tc16 + 1) * P, :], in_=slotX(tc8)),
                        reads=[f"X{tc8}"], writes=[f"OUT{tc16}"])

        def emit_phaseA_half(half):
            emit_ffn_half(half, x_d, None,
                          wsrc=("f32" if half == 0 else "bf16"), writeback=(half == 0))
            for tc8 in range(8):
                emit_ln_tc(half, tc8, router=False, dst_res_prefix="xh", call_tag="ln2")
            for dc in range(NDC):
                EM("sync", lambda e, dc=dc, half=half: e.dma_start(
                    out=xh2_s[dc * P:(dc + 1) * P, half * TH:(half + 1) * TH], in_=xh(dc)[:]),
                    reads=[f"xh{dc}c{tc8}" for tc8 in range(8)], writes=[f"XH2d{dc}h{half}"])
            for tc8 in range(8):
                tc16 = half * 8 + tc8
                EM("sync", lambda e, tc8=tc8, tc16=tc16: e.dma_start(
                    out=x1_s[tc16 * P:(tc16 + 1) * P, :], in_=slotX(tc8)),
                    reads=[f"X{tc8}"], writes=[f"X1S{tc16}"])

        def emit_attnproj():
            ctx.region_switch([f"xh2f{dc}" for dc in range(NDC)],
                              [f"X{t}" for t in range(8)])
            ctx.region_switch([f"kp{t}" for t in range(16)] + [f"v{t}" for t in range(16)]
                              + ["qstg0", "qstg1", "escr0", "escr1"],
                              [f"xh{dc}c{tc8}" for dc in range(NDC) for tc8 in range(8)]
                              + ["xhtm0", "xhtm1"])
            ctx.region_switch(["wblk0", "wblk1"],
                              [f"wg{dc}" for dc in range(NDC)] + [f"wu{dc}" for dc in range(NDC)]
                              + [f"wd{p}_{fc}" for p in range(2) for fc in range(NFC)])
            ctx.region_switch([f"pf3h{h}" for h in range(16)], ["pf3"])
            ctx.region_switch([f"pf4h{hh}" for hh in range(4)], ["pf4"])
            ctx.region_switch([f"pf5h{hh}" for hh in range(4)], ["pf5"])
            for dc in range(NDC):
                EM("sync", lambda e, dc=dc: e.dma_start(out=xh2f(dc), in_=xh2_s[dc * P:(dc + 1) * P, :]),
                   reads=[f"XH2d{dc}h0", f"XH2d{dc}h1"], writes=[f"xh2f{dc}"])

            def load_wblk(src, og, par):
                EM("gpsimd", lambda e: e.dma_start(
                    out=ar3[:, par * 8192:(par + 1) * 8192], in_=src[og]),
                    writes=[f"wblk{par}"])

            def proj_group(tc16, par, bank, bias_slot):
                for dc in range(NDC):
                    kw = dict(reads=[f"xh2f{dc}", f"wblk{par}"])
                    if dc == 0:
                        kw["acquire"] = [f"pf{bank}"]
                    EM("tensor", lambda e, dc=dc, tc16=tc16, par=par, bank=bank: e.matmul(
                        pf[bank][:, 0:512], xh2f(dc)[:, tc16 * P:(tc16 + 1) * P],
                        wblk(par, dc)[:], start=(dc == 0), stop=False), **kw)
                EM("tensor", lambda e, tc16=tc16, bank=bank, bias_slot=bias_slot: e.matmul(
                    pf[bank][:, 0:512], ones_r[0:1, 0:P],
                    biasrow[0:1, bias_slot * 512:(bias_slot + 1) * 512],
                    start=False, stop=True),
                    reads=["ones_r", f"bias{bias_slot}"], release=[f"pf{bank}"])

            def elu_evac(bank, dst, dres):
                EM("vector", lambda e: e.tensor_scalar(escr(0), pf[bank][:, 0:512], 0.0, None, op0=ALU.min),
                   reads=[f"pf{bank}"], writes=["escr0"])
                EM("scalar", lambda e: e.activation(escr(0), escr(0), AF.Exp),
                   reads=["escr0"], writes=["escr0"])
                EM("vector", lambda e: e.tensor_scalar(escr(1), pf[bank][:, 0:512], 0.0, None, op0=ALU.max),
                   reads=[f"pf{bank}"], writes=["escr1"])
                EM("vector", lambda e: e.tensor_tensor(dst, escr(0), escr(1), op=ALU.add),
                   reads=["escr0", "escr1"], writes=[dres])

            nbias = [0]

            def load_bias(src, slot, og):
                EM("gpsimd", lambda e: e.dma_start(
                    out=biasrow[0:1, slot * 512:(slot + 1) * 512], in_=src[0:1, og * 512:(og + 1) * 512]),
                    writes=[f"bias{slot}"])

            pidx = 0
            for og in range(NOG):
                load_bias(bq_d, 0, og)
                load_bias(bk_d, 1, og)
                load_bias(bv_d, 2, og)
                load_wblk(wq_d, og, pidx % 2)
                for tc16 in range(16):
                    bank = tc16 % 2
                    proj_group(tc16, pidx % 2, bank, 0)
                    elu_evac(bank, qstg(tc16 % 2)[:], f"qstg{tc16 % 2}")
                    EM("sync", lambda e, tc16=tc16, og=og: e.dma_start(
                        out=qpt_s[tc16 * P:(tc16 + 1) * P, og * 512:(og + 1) * 512],
                        in_=qstg(tc16 % 2)[:]),
                        reads=[f"qstg{tc16 % 2}"], writes=[f"QPT{tc16}o{og}"])
                pidx += 1
                load_wblk(wk_d, og, pidx % 2)
                for tc16 in range(16):
                    bank = tc16 % 2
                    proj_group(tc16, pidx % 2, bank, 1)
                    elu_evac(bank, kpblk(tc16)[:], f"kp{tc16}")
                pidx += 1
                load_wblk(wv_d, og, pidx % 2)
                for tc16 in range(16):
                    bank = tc16 % 2
                    proj_group(tc16, pidx % 2, bank, 2)
                    EM("vector", lambda e, tc16=tc16, bank=bank: e.tensor_copy(vblk(tc16)[:], pf[bank][:, 0:512]),
                       reads=[f"pf{bank}"], writes=[f"v{tc16}"])
                pidx += 1
                kvbank = 4 + (og % 2)
                for hh in range(4):
                    for tc16 in range(16):
                        kw = dict(reads=[f"kp{tc16}", f"v{tc16}"])
                        if tc16 == 0:
                            kw["acquire"] = [f"pf{kvbank}h{hh}"]
                        if tc16 == 15:
                            kw["release"] = [f"pf{kvbank}h{hh}"]
                        EM("tensor", lambda e, hh=hh, tc16=tc16, kvbank=kvbank: e.matmul(
                            pf[kvbank][:, hh * P:(hh + 1) * P],
                            kpblk(tc16)[:, hh * P:(hh + 1) * P],
                            vblk(tc16)[:, hh * P:(hh + 1) * P],
                            start=(tc16 == 0), stop=(tc16 == 15)), **kw)
                for hh in range(4):
                    for tc16 in range(16):
                        kw = dict(reads=[f"kp{tc16}", "ones_c"])
                        if tc16 == 0:
                            kw["acquire"] = [f"pf3h{og * 4 + hh}"]
                        if tc16 == 15:
                            kw["release"] = [f"pf3h{og * 4 + hh}"]
                        EM("tensor", lambda e, hh=hh, tc16=tc16, og=og: e.matmul(
                            pf[3][:, og * 4 + hh: og * 4 + hh + 1],
                            kpblk(tc16)[:, hh * P:(hh + 1) * P],
                            ones_c[0:P, 0:1],
                            start=(tc16 == 0), stop=(tc16 == 15)), **kw)
                EM("vector", lambda e, og=og, kvbank=kvbank: e.tensor_copy(
                    kvsb[:, og * 512:(og + 1) * 512], pf[kvbank][:, 0:512]),
                    reads=[f"pf{kvbank}h{hh2}" for hh2 in range(4)], writes=[f"kvsb{og}"])
            EM("vector", lambda e: e.tensor_copy(kvsb[:, 2048:2064], pf[3][:, 0:16]),
               reads=[f"pf3h{h2}" for h2 in range(16)], writes=["kvsb_kp"])
            EM("gpsimd", lambda e: e.dma_start(out=kvi_s.ap()[:, :], in_=ar4[:, 0:2080]),
               reads=[f"kvsb{og}" for og in range(4)] + ["kvsb_kp"], writes=["KVI"])
            if _SKIP_CC:
                EM("gpsimd", lambda e: e.dma_start(out=kvo_s.ap()[:, :], in_=kvi_s.ap()[:, :]),
                   reads=["KVI"], writes=["KVO"])
            else:
                EM("gpsimd", lambda e: e.collective_compute(
                    "AllReduce", ALU.add,
                    replica_groups=[[0, 1], [2, 3], [4, 5], [6, 7]],
                    ins=[kvi_s.ap().opt()], outs=[kvo_s.ap().opt()]).then_inc(cc_sem),
                    reads=["KVI"], writes=[], inc=0)
                EM("gpsimd", lambda e: (e.wait_ge(cc_sem, 1), e.nop())[1],
                   reads=[], writes=["KVO"], inc=16)
            ctx.region_switch(["kvsball"], [f"kvsb{og}" for og in range(4)] + ["kvsb_kp"])
            EM("gpsimd", lambda e: e.dma_start(out=ar4[:, 0:2080], in_=kvo_s.ap()[:, :]),
               reads=["KVO"], writes=["kvsball"])

        def emit_phaseB():
            ctx.region_switch([f"wo{vc}" for vc in range(NDC)],
                              [f"xh2f{dc}" for dc in range(NDC)])
            ctx.region_switch([f"qpfm{qc}" for qc in range(NDC)] + [f"qptm{ts}" for ts in range(4)]
                              + ["atm0", "atm1"],
                              [f"kp{t}" for t in range(16)] + [f"v{t}" for t in range(16)]
                              + ["qstg0", "qstg1", "escr0", "escr1"])
            ctx.region_switch([f"afm{vc}" for vc in range(NDC)] + ["x1t0", "x1t1", "x2t0", "x2t1"],
                              ["wblk0", "wblk1"])
            ctx.region_switch(["kvbf", "zsel"], [f"hid{fc}t{tt}" for fc in range(NFC) for tt in range(2)])
            ctx.region_switch(["pf3"], [f"pf3h{h}" for h in range(16)])
            ctx.region_switch(["pf4"], [f"pf4h{hh}" for hh in range(4)])
            ctx.region_switch(["pf5"], [f"pf5h{hh}" for hh in range(4)])
            EM("vector", lambda e: e.tensor_copy(kvbf[:], kvsb[:, 0:2048]),
               reads=["kvsball"], writes=["kvbf"])
            EM("vector", lambda e: e.memset(zsel[:], 0.0), writes=["zsel"])
            for h in range(H):
                EM("vector", lambda e, h=h: e.tensor_copy(zsel[:, h * 16 + h: h * 16 + h + 1],
                                                          kvsb[:, 2048 + h: 2048 + h + 1]),
                   reads=["kvsball"], writes=["zsel"])
            EM("gpsimd", lambda e: e.dma_start(out=biasrow[0:1, :], in_=bo_d[0:1, :]),
               writes=["bias0", "bias1", "bias2", "bias3"])
            EM("gpsimd", lambda e: e.dma_start(out=ar1b[:, 0:32768], in_=wo_d[:, :]),
               writes=[f"wo{vc}" for vc in range(NDC)])

            for tt in range(4):
                for ts in range(4):
                    tc16 = tt * 4 + ts
                    EM("sync", lambda e, ts=ts, tc16=tc16: e.dma_start(
                        out=qptm(ts)[:], in_=qpt_s[tc16 * P:(tc16 + 1) * P, :]),
                        reads=[f"QPT{tc16}o{og}" for og in range(4)], writes=[f"qptm{ts}"])
                    for qc in range(NDC):
                        bank = qc % 2
                        EM("tensor", lambda e, qc=qc, ts=ts, bank=bank: e.transpose(
                            pb[bank][:, 0:P], qptm(ts)[:, qc * P:(qc + 1) * P], identB[:]),
                            reads=[f"qptm{ts}", "identB"], acquire=[f"pb{bank}"], release=[f"pb{bank}"])
                        EM("vector", lambda e, qc=qc, ts=ts, bank=bank: e.tensor_copy(
                            qpfm(qc)[:, ts * P:(ts + 1) * P], pb[bank][:, 0:P]),
                            reads=[f"pb{bank}"], writes=[f"qpfm{qc}ts{ts}"])
                    zpar = ts % 2
                    for qc in range(NDC):
                        kw = dict(reads=[f"qpfm{qc}ts{ts}", "zsel"])
                        if qc == 0:
                            kw["acquire"] = ["pf0"]
                        if qc == NDC - 1:
                            kw["release"] = ["pf0"]
                        EM("tensor", lambda e, qc=qc, ts=ts: e.matmul(
                            pf[0][:, 0:16], qpfm(qc)[:, ts * P:(ts + 1) * P],
                            zsel[:, qc * 16:(qc + 1) * 16],
                            start=(qc == 0), stop=(qc == NDC - 1)), **kw)
                    EM("vector", lambda e, zpar=zpar: e.tensor_scalar(zinvv(zpar), pf[0][:, 0:16], Z_EPS, None, op0=ALU.add),
                       reads=["pf0"], writes=[f"zinv{zpar}"])
                    EM("vector", lambda e, zpar=zpar: e.reciprocal(zinvv(zpar), zinvv(zpar)),
                       reads=[f"zinv{zpar}"], writes=[f"zinv{zpar}"])
                    apar = ts % 2
                    for og in range(NOG):
                        bank = 1 + (og % 2)
                        for hh in range(4):
                            h = og * 4 + hh
                            kw = dict(reads=[f"qpfm{h}ts{ts}", "kvbf"])
                            if hh == 0:
                                kw["acquire"] = [f"pf{bank}"]
                            if hh == 3:
                                kw["release"] = [f"pf{bank}"]
                            EM("tensor", lambda e, h=h, hh=hh, ts=ts, bank=bank: e.matmul(
                                pf[bank][:, hh * P:(hh + 1) * P],
                                qpfm(h)[:, ts * P:(ts + 1) * P],
                                kvbf[:, h * P:(h + 1) * P],
                                start=(hh == 0), stop=(hh == 3)), **kw)
                        for hh in range(4):
                            h = og * 4 + hh
                            EM("vector", lambda e, h=h, hh=hh, og=og, bank=bank, apar=apar, zpar=zpar: e.tensor_scalar(
                                atm(apar)[:, h * P:(h + 1) * P], pf[bank][:, hh * P:(hh + 1) * P],
                                zinvv(zpar)[:, h:h + 1], None, op0=ALU.mult),
                                reads=[f"pf{bank}", f"zinv{zpar}"], writes=[f"atm{apar}b{h}"])
                    for vc in range(NDC):
                        bank = vc % 2
                        EM("tensor", lambda e, vc=vc, apar=apar, bank=bank: e.transpose(
                            pb[bank][:, 0:P], atm(apar)[:, vc * P:(vc + 1) * P], identB[:]),
                            reads=[f"atm{apar}b{vc}", "identB"], acquire=[f"pb{bank}"], release=[f"pb{bank}"])
                        EM("vector", lambda e, vc=vc, ts=ts, bank=bank: e.tensor_copy(
                            afm(vc)[:, ts * P:(ts + 1) * P], pb[bank][:, 0:P]),
                            reads=[f"pb{bank}"], writes=[f"afm{vc}ts{ts}"])
                    xpar = ts % 2
                    EM("sync", lambda e, xpar=xpar, tc16=tc16: e.dma_start(
                        out=x1t(xpar), in_=x1_s[tc16 * P:(tc16 + 1) * P, :]),
                        reads=[f"X1S{tc16}"], writes=[f"x1t{xpar}"])
                    for og in range(NOG):
                        bank = 3 + (og % 2)
                        for vc in range(NDC):
                            kw = dict(reads=[f"afm{vc}ts{ts}", f"wo{vc}"])
                            if vc == 0:
                                kw["acquire"] = [f"pf{bank}"]
                            EM("tensor", lambda e, vc=vc, ts=ts, og=og, bank=bank: e.matmul(
                                pf[bank][:, 0:512], afm(vc)[:, ts * P:(ts + 1) * P],
                                wovw(vc)[:, og * 512:(og + 1) * 512],
                                start=(vc == 0), stop=False), **kw)
                        EM("tensor", lambda e, og=og, bank=bank: e.matmul(
                            pf[bank][:, 0:512], ones_r[0:1, 0:P],
                            biasrow[0:1, og * 512:(og + 1) * 512],
                            start=False, stop=True),
                            reads=["ones_r", f"bias{og}"], release=[f"pf{bank}"])
                        EM("vector", lambda e, og=og, bank=bank, xpar=xpar: e.tensor_tensor(
                            x2t(xpar)[:, og * 512:(og + 1) * 512], pf[bank][:, 0:512],
                            x1t(xpar)[:, og * 512:(og + 1) * 512], op=ALU.add),
                            reads=[f"pf{bank}", f"x1t{xpar}"], writes=[f"x2t{xpar}"])
                    EM("sync", lambda e, xpar=xpar, tc16=tc16: e.dma_start(
                        out=x2_s[tc16 * P:(tc16 + 1) * P, :], in_=x2t(xpar)),
                        reads=[f"x2t{xpar}"], writes=[f"X2S{tc16}", f"SRC2_{tc16}"])

        def emit_phaseC():
            ctx.region_switch([f"X{t}" for t in range(8)], [f"wo{vc}" for vc in range(NDC)])
            ctx.region_switch([f"xh{dc}c{tc8}" for dc in range(NDC) for tc8 in range(8)]
                              + ["xhtm0", "xhtm1"],
                              [f"qpfm{qc}ts{ts}" for qc in range(NDC) for ts in range(4)]
                              + [f"qptm{ts}" for ts in range(4)]
                              + [f"atm{i}b{h}" for i in range(2) for h in range(H)])
            ctx.region_switch([f"wg{dc}" for dc in range(NDC)] + [f"wu{dc}" for dc in range(NDC)]
                              + [f"wd{p}_{fc}" for p in range(2) for fc in range(NFC)],
                              [f"afm{vc}ts{ts}" for vc in range(NDC) for ts in range(4)]
                              + ["x1t0", "x1t1", "x2t0", "x2t1"])
            ctx.region_switch([f"hid{fc}t{tt}" for fc in range(NFC) for tt in range(2)],
                              ["kvbf", "zsel"])
            for half in range(2):
                for tc8 in range(8):
                    tc16 = half * 8 + tc8
                    ctx.region_switch([f"SRC{tc16}"], [f"SRC2_{tc16}"])
                emit_ffn_half(half, x2_s, out_d, wsrc="bf16")

        emit_phaseA_half(0)
        emit_phaseA_half(1)
        if _BUILD_UPTO == "ffn1":
            for tc16 in range(16):
                EM("sync", lambda e, tc16=tc16: e.dma_start(
                    out=out_d[tc16 * P:(tc16 + 1) * P, :], in_=x1_s[tc16 * P:(tc16 + 1) * P, :]),
                    reads=[f"X1S{tc16}"], writes=[])
        else:
            emit_attnproj()
            emit_phaseB()
            if _BUILD_UPTO == "attn":
                for tc16 in range(16):
                    EM("sync", lambda e, tc16=tc16: e.dma_start(
                        out=out_d[tc16 * P:(tc16 + 1) * P, :], in_=x2_s[tc16 * P:(tc16 + 1) * P, :]),
                        reads=[f"X2S{tc16}"], writes=[])
            else:
                emit_phaseC()

        ctx.final_wait("sync", ["sync", "gpsimd"])
        ctx.run(block)
    return nc, ctx


def kernel(L0_attn_hidden_states, L0_attn_q_w, L0_attn_q_b, L0_attn_k_w, L0_attn_k_b,
           L0_attn_v_w, L0_attn_v_b, L0_attn_o_w, L0_attn_o_b,
           L0_ffn_w_gate_router, L0_ffn_w_gate_proj, L0_ffn_w_up_proj, L0_ffn_w_down_proj):
    f32 = np.float32
    hs = np.asarray(L0_attn_hidden_states, f32)
    c = np.ascontiguousarray

    def pack_qkv(w):
        return c(np.asarray(w, f32).T.reshape(16, P, 4, 512)
                 .transpose(2, 1, 0, 3).reshape(4, P, 8192))

    wq = pack_qkv(L0_attn_q_w)
    wk = pack_qkv(L0_attn_k_w)
    wv = pack_qkv(L0_attn_v_w)
    wo = c(np.asarray(L0_attn_o_w, f32).T.reshape(16, P, 2048)
           .transpose(1, 0, 2).reshape(P, 32768))
    bq = c(np.asarray(L0_attn_q_b, f32).reshape(1, D))
    bk = c(np.asarray(L0_attn_k_b, f32).reshape(1, D))
    bv = c(np.asarray(L0_attn_v_b, f32).reshape(1, D))
    bo = c(np.asarray(L0_attn_o_b, f32).reshape(1, D))
    wr = c(np.asarray(L0_ffn_w_gate_router, f32).T)
    def pack_gu(w):
        return c(np.asarray(w, f32).transpose(0, 2, 1).reshape(E, 16, P, FF)
                 .transpose(0, 2, 1, 3).reshape(E, P, 12288))

    wg = pack_gu(L0_ffn_w_gate_proj)
    wu = pack_gu(L0_ffn_w_up_proj)
    wd = c(np.asarray(L0_ffn_w_down_proj, f32).transpose(0, 2, 1)
           .reshape(E, NFC, P, 4, 512).transpose(0, 3, 2, 1, 4)
           .reshape(E, 4, P, 3072))
    ident = np.eye(P, dtype=f32)

    nc, _ = build_program()
    shared = {"wq": wq, "wk": wk, "wv": wv, "wo": wo,
              "bq": bq, "bk": bk, "bv": bv, "bo": bo,
              "wr": wr, "wg": wg, "wu": wu, "wd": wd, "ident": ident}
    in_maps = []
    for core in range(8):
        b, h = core // 2, core % 2
        xc = c(hs[b, h * T:(h + 1) * T, :])
        in_maps.append({"x": xc, **shared})
    res = run_bass_kernel_spmd(nc, in_maps, core_ids=list(range(8)))
    out = np.empty((4, 4096, D), f32)
    for core in range(8):
        b, h = core // 2, core % 2
        out[b, h * T:(h + 1) * T, :] = res.results[core]["out"]
    return out


# version 11
# speedup vs baseline: 1.5169x; 1.0611x over previous
import numpy as np
import concourse.bass as bass
import concourse.mybir as mybir
from concourse.bass_utils import run_bass_kernel_spmd

F32 = mybir.dt.float32
BF16 = mybir.dt.bfloat16
AF = mybir.ActivationFunctionType
ALU = mybir.AluOpType
AX = mybir.AxisListType

P = 128
T = 2048
TH = 1024
D = 2048
FF = 768
E = 8
H = 16
NDC = D // P
NFC = FF // P
NOG = D // 512
LN_EPS = 1e-5
Z_EPS = 1e-6

_BUILD_UPTO = "full"
_SERIALIZE = False
_DMA_CONSERVATIVE = True
_SKIP_CC = False


class Ctx:

    ENGINES = ("sync", "gpsimd", "vector", "scalar", "tensor")
    DMA_ENGINES = ("sync", "gpsimd")

    def __init__(self, nc, stack):
        self.nc = nc
        self.sems = {e: stack.enter_context(nc.semaphore(f"sem_{e}")) for e in self.ENGINES}
        self.cnt = {e: 0 for e in self.ENGINES}
        self.last_wait = {}
        self.writer = {}
        self.readers = {}
        self.plans = {e: [] for e in self.ENGINES}
        self.n_waits = 0
        self.n_instr = 0

    def _need(self, needs, eng, cnt):
        if cnt > needs.get(eng, 0):
            needs[eng] = cnt

    serialize_now = False

    def emit(self, eng, fn, reads=(), writes=(), acquire=(), release=(), inc=None):
        needs = {}
        if _SERIALIZE or self.serialize_now:
            for teng in self.ENGINES:
                if self.cnt[teng] > 0:
                    needs[teng] = self.cnt[teng]
        for r in reads:
            for weng, wcnt in self.writer.get(r, {}).items():
                self._need(needs, weng, wcnt)
        for wres in tuple(writes) + tuple(acquire):
            for weng, wcnt in self.writer.get(wres, {}).items():
                self._need(needs, weng, wcnt)
            for reng, rcnt in self.readers.get(wres, {}).items():
                self._need(needs, reng, rcnt)
        if _DMA_CONSERVATIVE:
            for teng in self.DMA_ENGINES:
                if teng in needs:
                    needs[teng] = self.cnt[teng]
        wl = []
        for teng, val in needs.items():
            key = (eng, teng)
            if self.last_wait.get(key, 0) >= val:
                continue
            self.last_wait[key] = val
            wl.append((self.sems[teng], val))
        if inc is None:
            inc = 16 if eng in self.DMA_ENGINES else 1
        self.cnt[eng] += inc
        my = self.cnt[eng]
        sem = self.sems[eng]
        self.n_waits += len(wl)
        self.n_instr += 1

        def thunk(e, fn=fn, wl=wl, sem=sem, inc=inc):
            for s, v in wl:
                e.wait_ge(s, v)
            ins = fn(e)
            if inc:
                ins.then_inc(sem, inc)

        self.plans[eng].append(thunk)
        for r in reads:
            self.readers.setdefault(r, {})[eng] = my
        for wres in tuple(writes) + tuple(release):
            self.writer[wres] = {eng: my}
            self.readers[wres] = {}
        return my

    def region_switch(self, news, olds):
        merged = {}
        for o in olds:
            for eng, cnt in self.writer.pop(o, {}).items():
                if cnt > merged.get(eng, 0):
                    merged[eng] = cnt
            for eng, cnt in self.readers.pop(o, {}).items():
                if cnt > merged.get(eng, 0):
                    merged[eng] = cnt
        for n in news:
            w = dict(merged)
            for eng, cnt in self.writer.get(n, {}).items():
                if cnt > w.get(eng, 0):
                    w[eng] = cnt
            for eng, cnt in self.readers.get(n, {}).items():
                if cnt > w.get(eng, 0):
                    w[eng] = cnt
            self.writer[n] = w
            self.readers[n] = {}

    def final_wait(self, eng, targets):
        wl = [(self.sems[t], self.cnt[t]) for t in targets if self.cnt[t] > 0]

        def thunk(e, wl=wl):
            for s, v in wl:
                e.wait_ge(s, v)

        self.plans[eng].append(thunk)

    def run(self, block):
        for name in self.ENGINES:
            plan = self.plans[name]

            def make(plan):
                def body(e):
                    for t in plan:
                        t(e)
                return body

            getattr(block, name)(make(plan))


def build_program():
    nc = bass.Bass("TRN2", target_bir_lowering=False, debug=False)

    x_d = nc.dram_tensor("x", [T, D], F32, kind="ExternalInput").ap()
    wq_d = nc.dram_tensor("wq", [NOG, P, 8192], F32, kind="ExternalInput").ap()
    wk_d = nc.dram_tensor("wk", [NOG, P, 8192], F32, kind="ExternalInput").ap()
    wv_d = nc.dram_tensor("wv", [NOG, P, 8192], F32, kind="ExternalInput").ap()
    wo_d = nc.dram_tensor("wo", [P, 32768], F32, kind="ExternalInput").ap()
    bq_d = nc.dram_tensor("bq", [1, D], F32, kind="ExternalInput").ap()
    bk_d = nc.dram_tensor("bk", [1, D], F32, kind="ExternalInput").ap()
    bv_d = nc.dram_tensor("bv", [1, D], F32, kind="ExternalInput").ap()
    bo_d = nc.dram_tensor("bo", [1, D], F32, kind="ExternalInput").ap()
    wr_d = nc.dram_tensor("wr", [D, E], F32, kind="ExternalInput").ap()
    wg_d = nc.dram_tensor("wg", [E, P, 12288], F32, kind="ExternalInput").ap()
    wu_d = nc.dram_tensor("wu", [E, P, 12288], F32, kind="ExternalInput").ap()
    wd_d = nc.dram_tensor("wd", [E, NOG, P, 3072], F32, kind="ExternalInput").ap()
    id_d = nc.dram_tensor("ident", [P, P], F32, kind="ExternalInput").ap()
    out_d = nc.dram_tensor("out", [T, D], F32, kind="ExternalOutput").ap()

    wg_s = nc.dram_tensor("wg_s", [E, P, 12288], BF16).ap()
    wu_s = nc.dram_tensor("wu_s", [E, P, 12288], BF16).ap()
    wd_s = nc.dram_tensor("wd_s", [E, NOG, P, 3072], BF16).ap()
    xh2_s = nc.dram_tensor("xh2_s", [D, T], BF16).ap()
    qpt_s = nc.dram_tensor("qpt_s", [T, D], BF16).ap()
    x1_s = nc.dram_tensor("x1_s", [T, D], F32).ap()
    x2_s = nc.dram_tensor("x2_s", [T, D], F32).ap()
    kvi_s = nc.dram_tensor("kvi_s", [P, 2080], F32)
    kvo_s = nc.dram_tensor("kvo_s", [P, 2080], F32)

    from contextlib import ExitStack
    with ExitStack() as st:
        ar1 = st.enter_context(nc.sbuf_tensor("ar1", [P, 16384], F32))
        ar2 = st.enter_context(nc.sbuf_tensor("ar2", [P, 20480], BF16))
        ar3 = st.enter_context(nc.sbuf_tensor("ar3", [P, 30720], BF16))
        ar4 = st.enter_context(nc.sbuf_tensor("ar4", [P, 3840], F32))
        ar5 = st.enter_context(nc.sbuf_tensor("ar5", [P, 6144], BF16))
        biasrow = st.enter_context(nc.sbuf_tensor("biasrow", [1, D], BF16))
        identB = st.enter_context(nc.sbuf_tensor("identB", [P, P], BF16))
        ones_r = st.enter_context(nc.sbuf_tensor("ones_r", [1, P], BF16))
        ones_c = st.enter_context(nc.sbuf_tensor("ones_c", [P, 1], BF16))
        wrt = st.enter_context(nc.sbuf_tensor("wrt", [P, P], BF16))

        pf = [st.enter_context(nc.psum_tensor(f"pf{i}", [P, 512], F32)) for i in range(6)]
        pb = [st.enter_context(nc.psum_tensor(f"pb{i}", [P, 1024], BF16)) for i in range(2)]
        cc_sem = st.enter_context(nc.semaphore("cc_sem"))

        ar1b = ar1.bitcast(BF16)
        ar2f = ar2.bitcast(F32)
        ar3f = ar3.bitcast(F32)

        ctx = Ctx(nc, st)
        block = st.enter_context(nc.Block())

        def slotX(tc8):
            return ar1[:, tc8 * D:(tc8 + 1) * D]

        def xh2f(dc):
            return ar1b[:, dc * D:(dc + 1) * D]

        def wovw(vc):
            return ar1b[:, vc * D:(vc + 1) * D]

        def xh(dc):
            return ar2[:, dc * TH:(dc + 1) * TH]

        def xhtm(i):
            return ar2[:, 16384 + i * D: 16384 + (i + 1) * D]

        def kpblk(tc16):
            return ar2[:, tc16 * 512:(tc16 + 1) * 512]

        def vblk(tc16):
            return ar2[:, 8192 + tc16 * 512: 8192 + (tc16 + 1) * 512]

        def qstg(i):
            return ar2[:, 16384 + i * 512: 16384 + (i + 1) * 512]

        def escr(i):
            return ar2f[:, 8704 + i * 512: 8704 + (i + 1) * 512]

        def qpfm(qc):
            return ar2[:, qc * 512:(qc + 1) * 512]

        def qptm(ts):
            return ar2[:, 8192 + ts * D: 8192 + (ts + 1) * D]

        def atm(i):
            return ar2[:, 16384 + i * D: 16384 + (i + 1) * D]

        def wgv(dc):
            return ar3[:, dc * FF:(dc + 1) * FF]

        def wuv(dc):
            return ar3[:, 12288 + dc * FF: 12288 + (dc + 1) * FF]

        def wdblk(par, fc):
            return ar3[:, 24576 + par * 3072 + fc * 512: 24576 + par * 3072 + (fc + 1) * 512]

        def wblk(par, dc):
            return ar3[:, par * 8192 + dc * 512: par * 8192 + (dc + 1) * 512]

        def afm(vc):
            return ar3[:, vc * 512:(vc + 1) * 512]

        def x1t(i):
            return ar3f[:, 4096 + i * D: 4096 + (i + 1) * D]

        def x2t(i):
            return ar3f[:, 8192 + i * D: 8192 + (i + 1) * D]

        kvsb = ar4[:, 0:2080]
        STAT0 = 2080

        def statc(par, j):
            return ar4[:, STAT0 + par * 8 + j: STAT0 + par * 8 + j + 1]

        WEFF0 = 2096

        def weffc(tc16):
            return ar4[:, WEFF0 + tc16 * E: WEFF0 + (tc16 + 1) * E]

        def gsv(i):
            return ar4[:, 2224 + i * 512: 2224 + (i + 1) * 512]

        def zinvv(par):
            return ar4[:, 3248 + par * 16: 3248 + (par + 1) * 16]

        def rscr(par, j):
            return ar4[:, 3280 + par * 24 + j * 8: 3280 + par * 24 + (j + 1) * 8]

        kvbf = ar5[:, 0:2048]
        zsel = ar5[:, 2048:2304]

        def hidv(fc):
            return ar5[:, fc * TH:(fc + 1) * TH]

        EM = ctx.emit

        EM("gpsimd", lambda e: e.dma_start(out=identB[:], in_=id_d[:, :]), writes=["identB"])
        EM("gpsimd", lambda e: e.dma_start(out=wrt[:].rearrange("p (c e) -> p c e", e=E),
                                           in_=wr_d.rearrange("(c p) e -> p c e", p=P)), writes=["wrt"])
        EM("vector", lambda e: e.memset(ones_r[:], 1.0), writes=["ones_r"])
        EM("vector", lambda e: e.memset(ones_c[:], 1.0), writes=["ones_c"])

        def emit_ln_tc(half, tc8, router, dst_res_prefix, call_tag):
            tc16 = half * 8 + tc8
            par = tc8 % 2
            xt = slotX(tc8)
            xres = f"X{tc8}"
            tmres = f"xhtm{par}"
            tm = xhtm(par)
            sums, sumsq, mcol, vcol = (statc(par, j) for j in range(4))
            rstd, nmr, nm1 = (statc(par, j) for j in range(4, 7))
            sres = f"stat{par}"
            EM("scalar", lambda e: e.activation(tm[:, 0:D], xt, AF.Square, accum_out=sumsq),
               reads=[xres], writes=[tmres, sres + "sq"])
            EM("vector", lambda e: e.reduce_sum(sums, xt, axis=AX.X),
               reads=[xres], writes=[sres + "s"])
            EM("vector", lambda e: e.tensor_scalar(mcol, sums, 1.0 / D, None, op0=ALU.mult),
               reads=[sres + "s"], writes=[sres + "m"])
            EM("vector", lambda e: e.tensor_scalar(vcol, sumsq, 1.0 / D, None, op0=ALU.mult),
               reads=[sres + "sq"], writes=[sres + "v"])
            EM("vector", lambda e: e.scalar_tensor_tensor(vcol, mcol, mcol, vcol, op0=ALU.mult, op1=ALU.subtract),
               reads=[sres + "m", sres + "v"], writes=[sres + "v"])
            EM("vector", lambda e: e.tensor_scalar(vcol, vcol, -1.0, LN_EPS, op0=ALU.mult, op1=ALU.add),
               reads=[sres + "v"], writes=[sres + "v"])
            EM("scalar", lambda e: e.activation(rstd, vcol, AF.Sqrt),
               reads=[sres + "v"], writes=[sres + "r"])
            EM("vector", lambda e: e.reciprocal(rstd, rstd),
               reads=[sres + "r"], writes=[sres + "r"])
            EM("vector", lambda e: e.tensor_scalar(nmr, mcol, rstd, -1.0, op0=ALU.mult, op1=ALU.mult),
               reads=[sres + "m", sres + "r"], writes=[sres + "n"])
            EM("scalar", lambda e: e.activation(tm[:, 0:D], xt, AF.Identity, bias=nmr, scale=rstd),
               reads=[xres, sres + "r", sres + "n"], writes=[tmres])
            for dc in range(NDC):
                bank = dc % 2
                EM("tensor", lambda e, dc=dc, bank=bank: e.transpose(
                    pb[bank][:, 0:P], tm[:, dc * P:(dc + 1) * P], identB[:]),
                    reads=[tmres, "identB"], acquire=[f"pb{bank}"], release=[f"pb{bank}"])
                EM("vector", lambda e, dc=dc, bank=bank: e.tensor_copy(
                    xh(dc)[:, tc8 * P:(tc8 + 1) * P], pb[bank][:, 0:P]),
                    reads=[f"pb{bank}"], writes=[f"{dst_res_prefix}{dc}c{tc8}"])
            if router:
                ps = pf[4]
                for dc in range(NDC):
                    kw = dict(reads=[f"{dst_res_prefix}{dc}c{tc8}", "wrt"])
                    if dc == 0:
                        kw["acquire"] = ["pf4"]
                    if dc == NDC - 1:
                        kw["release"] = ["pf4"]
                    EM("tensor", lambda e, dc=dc: e.matmul(
                        ps[:, 0:E], xh(dc)[:, tc8 * P:(tc8 + 1) * P], wrt[:, dc * E:(dc + 1) * E],
                        start=(dc == 0), stop=(dc == NDC - 1)), **kw)
                m1 = rscr(par, 0)[:, 0:1]
                m2c = rscr(par, 0)[:, 2:3]
                nm1c = statc(par, 6)
                msk = rscr(par, 1)
                unn = rscr(par, 2)
                den = rscr(par, 0)[:, 1:2]
                rr = f"rt{par}"
                EM("vector", lambda e: e.reduce_max(m1, ps[0:P, 0:E], axis=AX.X),
                   reads=["pf4"], writes=[rr + "m1"])
                EM("vector", lambda e: e.tensor_scalar(msk, ps[0:P, 0:E], m1, None, op0=ALU.is_ge),
                   reads=["pf4", rr + "m1"], writes=[rr + "msk"])
                EM("vector", lambda e: e.scalar_tensor_tensor(msk, msk, -1e30, ps[0:P, 0:E], op0=ALU.mult, op1=ALU.add),
                   reads=["pf4", rr + "msk"], writes=[rr + "msk"])
                EM("vector", lambda e: e.reduce_max(m2c, msk, axis=AX.X),
                   reads=[rr + "msk"], writes=[rr + "m2"])
                EM("vector", lambda e: e.tensor_scalar(msk, ps[0:P, 0:E], m2c, None, op0=ALU.is_ge),
                   reads=["pf4", rr + "m2"], writes=[rr + "msk"])
                EM("vector", lambda e: e.tensor_scalar(nm1c, m1, -1.0, None, op0=ALU.mult),
                   reads=[rr + "m1"], writes=[rr + "nm1"])
                EM("scalar", lambda e: e.activation(unn, ps[0:P, 0:E], AF.Exp, bias=nm1c),
                   reads=["pf4", rr + "nm1"], writes=[rr + "unn"])
                EM("vector", lambda e: e.tensor_tensor(unn, unn, msk, op=ALU.mult),
                   reads=[rr + "unn", rr + "msk"], writes=[rr + "unn"])
                EM("vector", lambda e: e.reduce_sum(den, unn, axis=AX.X),
                   reads=[rr + "unn"], writes=[rr + "den"])
                EM("vector", lambda e: e.reciprocal(den, den),
                   reads=[rr + "den"], writes=[rr + "den"])
                EM("vector", lambda e: e.tensor_scalar(weffc(tc16), unn, den, 0.5, op0=ALU.mult, op1=ALU.mult),
                   reads=[rr + "unn", rr + "den"], writes=[f"weff{tc16}"])

        def emit_ffn_half(half, x_src, store_out, wsrc="bf16", writeback=False):
            for tc8 in range(8):
                tc16 = half * 8 + tc8
                EM("sync", lambda e, tc8=tc8, tc16=tc16: e.dma_start(
                    out=slotX(tc8), in_=x_src[tc16 * P:(tc16 + 1) * P, :]),
                    writes=[f"X{tc8}"], reads=[f"SRC{tc16}"])
            for tc8 in range(8):
                emit_ln_tc(half, tc8, router=True, dst_res_prefix="xh", call_tag="ffn")

            wgreg = ar3[:, 0:12288]
            wureg = ar3[:, 12288:24576]

            def wdreg(par):
                return ar3[:, 24576 + par * 3072: 24576 + (par + 1) * 3072]

            wgres = [f"wg{dc}" for dc in range(NDC)]
            wures = [f"wu{dc}" for dc in range(NDC)]

            def load_gu(e_i):
                if wsrc == "f32":
                    EM("gpsimd", lambda e, e_i=e_i: e.dma_start(out=wgreg, in_=wg_d[e_i]),
                       writes=wgres)
                    EM("gpsimd", lambda e, e_i=e_i: e.dma_start(out=wureg, in_=wu_d[e_i]),
                       writes=wures)
                    if writeback:
                        EM("sync", lambda e, e_i=e_i: e.dma_start(out=wg_s[e_i], in_=wgreg),
                           reads=wgres, writes=[f"wgS{e_i}"])
                        EM("sync", lambda e, e_i=e_i: e.dma_start(out=wu_s[e_i], in_=wureg),
                           reads=wures, writes=[f"wuS{e_i}"])
                else:
                    EM("sync", lambda e, e_i=e_i: e.dma_start(out=wgreg, in_=wg_s[e_i]),
                       reads=[f"wgS{e_i}"], writes=wgres)
                    EM("sync", lambda e, e_i=e_i: e.dma_start(out=wureg, in_=wu_s[e_i]),
                       reads=[f"wuS{e_i}"], writes=wures)

            def load_wd(e_i, og, par):
                wdres = [f"wd{par}_{fc}" for fc in range(NFC)]
                if wsrc == "f32":
                    EM("gpsimd", lambda e, e_i=e_i, og=og, par=par: e.dma_start(
                        out=wdreg(par), in_=wd_d[e_i, og]), writes=wdres)
                    if writeback:
                        EM("sync", lambda e, e_i=e_i, og=og, par=par: e.dma_start(
                            out=wd_s[e_i, og], in_=wdreg(par)),
                            reads=wdres, writes=[f"wdS{e_i}_{og}"])
                else:
                    EM("gpsimd", lambda e, e_i=e_i, og=og, par=par: e.dma_start(
                        out=wdreg(par), in_=wd_s[e_i, og]),
                        reads=[f"wdS{e_i}_{og}"], writes=wdres)

            for e_i in range(E):
                if e_i == 0:
                    load_gu(0)
                for tt in range(2):
                    xh_reads = [f"xh{dc}c{tc8}" for dc in range(1) for tc8 in range(4 * tt, 4 * tt + 4)]
                    for fc in range(NFC):
                        gpar = fc % 2
                        gsb = gsv(gpar)
                        for dc in range(NDC):
                            kw = dict(reads=[f"wg{dc}"] + [f"xh{dc}c{tc8}" for tc8 in range(4 * tt, 4 * tt + 4)])
                            if dc == 0:
                                kw["acquire"] = ["pf0"]
                            if dc == NDC - 1:
                                kw["release"] = ["pf0"]
                            EM("tensor", lambda e, dc=dc, fc=fc, tt=tt: e.matmul(
                                pf[0][:, 0:512], wgv(dc)[:, fc * P:(fc + 1) * P],
                                xh(dc)[:, tt * 512:(tt + 1) * 512],
                                start=(dc == 0), stop=(dc == NDC - 1)), **kw)
                        EM("scalar", lambda e, gsb=gsb: e.activation(gsb, pf[0][:, 0:512], AF.Silu),
                           reads=["pf0"], writes=[f"gs{gpar}"])
                        for dc in range(NDC):
                            kw = dict(reads=[f"wu{dc}"] + [f"xh{dc}c{tc8}" for tc8 in range(4 * tt, 4 * tt + 4)])
                            if dc == 0:
                                kw["acquire"] = ["pf1"]
                            if dc == NDC - 1:
                                kw["release"] = ["pf1"]
                            EM("tensor", lambda e, dc=dc, fc=fc, tt=tt: e.matmul(
                                pf[1][:, 0:512], wuv(dc)[:, fc * P:(fc + 1) * P],
                                xh(dc)[:, tt * 512:(tt + 1) * 512],
                                start=(dc == 0), stop=(dc == NDC - 1)), **kw)
                        EM("vector", lambda e, fc=fc, tt=tt, gsb=gsb: e.tensor_tensor(
                            hidv(fc)[:, tt * 512:(tt + 1) * 512], pf[1][:, 0:512], gsb, op=ALU.mult),
                            reads=["pf1", f"gs{gpar}"], writes=[f"hid{fc}t{tt}"])
                load_wd(e_i, 0, 0)
                load_wd(e_i, 1, 1)
                for og in range(NOG):
                    par = og % 2
                    for tc8 in range(8):
                        tt = tc8 // 4
                        bank = 2 + ((og * 8 + tc8) % 4)
                        for fc in range(NFC):
                            kw = dict(reads=[f"wd{par}_{fc}", f"hid{fc}t{tt}"])
                            if fc == 0:
                                kw["acquire"] = [f"pf{bank}"]
                            if fc == NFC - 1:
                                kw["release"] = [f"pf{bank}"]
                            EM("tensor", lambda e, fc=fc, tc8=tc8, og=og, par=par, bank=bank: e.matmul(
                                pf[bank][:, 0:512],
                                hidv(fc)[:, tc8 * P:(tc8 + 1) * P],
                                wdblk(par, fc)[:],
                                start=(fc == 0), stop=(fc == NFC - 1)), **kw)
                        tc16 = half * 8 + tc8
                        EM("vector", lambda e, tc8=tc8, og=og, bank=bank, e_i=e_i, tc16=tc16: e.scalar_tensor_tensor(
                            slotX(tc8)[:, og * 512:(og + 1) * 512],
                            pf[bank][:, 0:512],
                            weffc(tc16)[:, e_i:e_i + 1],
                            slotX(tc8)[:, og * 512:(og + 1) * 512],
                            op0=ALU.mult, op1=ALU.add),
                            reads=[f"pf{bank}", f"weff{tc16}", f"X{tc8}"], writes=[f"X{tc8}"])
                    if og == 0:
                        load_wd(e_i, 2, 0)
                    if wsrc == "f32":
                        if og == 1:
                            load_wd(e_i, 3, 1)
                            if e_i + 1 < E:
                                load_gu(e_i + 1)
                    else:
                        if og == 1 and e_i + 1 < E:
                            load_gu(e_i + 1)
                        if og == 2:
                            load_wd(e_i, 3, 1)
            if store_out is not None:
                for tc8 in range(8):
                    tc16 = half * 8 + tc8
                    EM("sync", lambda e, tc8=tc8, tc16=tc16: e.dma_start(
                        out=store_out[tc16 * P:(tc16 + 1) * P, :], in_=slotX(tc8)),
                        reads=[f"X{tc8}"], writes=[f"OUT{tc16}"])

        def emit_phaseA_half(half):
            emit_ffn_half(half, x_d, None,
                          wsrc=("f32" if half == 0 else "bf16"), writeback=(half == 0))
            for tc8 in range(8):
                emit_ln_tc(half, tc8, router=False, dst_res_prefix="xh", call_tag="ln2")
            for dc in range(NDC):
                EM("sync", lambda e, dc=dc, half=half: e.dma_start(
                    out=xh2_s[dc * P:(dc + 1) * P, half * TH:(half + 1) * TH], in_=xh(dc)[:]),
                    reads=[f"xh{dc}c{tc8}" for tc8 in range(8)], writes=[f"XH2d{dc}h{half}"])
            for tc8 in range(8):
                tc16 = half * 8 + tc8
                EM("sync", lambda e, tc8=tc8, tc16=tc16: e.dma_start(
                    out=x1_s[tc16 * P:(tc16 + 1) * P, :], in_=slotX(tc8)),
                    reads=[f"X{tc8}"], writes=[f"X1S{tc16}"])

        def emit_attnproj():
            ctx.region_switch([f"xh2f{dc}" for dc in range(NDC)],
                              [f"X{t}" for t in range(8)])
            ctx.region_switch([f"kp{t}" for t in range(16)] + [f"v{t}" for t in range(16)]
                              + ["qstg0", "qstg1", "escr0", "escr1"],
                              [f"xh{dc}c{tc8}" for dc in range(NDC) for tc8 in range(8)]
                              + ["xhtm0", "xhtm1"])
            ctx.region_switch(["wblk0", "wblk1"],
                              [f"wg{dc}" for dc in range(NDC)] + [f"wu{dc}" for dc in range(NDC)]
                              + [f"wd{p}_{fc}" for p in range(2) for fc in range(NFC)])
            ctx.region_switch([f"pf3h{h}" for h in range(16)], ["pf3"])
            ctx.region_switch([f"pf4h{hh}" for hh in range(4)], ["pf4"])
            ctx.region_switch([f"pf5h{hh}" for hh in range(4)], ["pf5"])
            for dc in range(NDC):
                EM("sync", lambda e, dc=dc: e.dma_start(out=xh2f(dc), in_=xh2_s[dc * P:(dc + 1) * P, :]),
                   reads=[f"XH2d{dc}h0", f"XH2d{dc}h1"], writes=[f"xh2f{dc}"])

            def load_wblk(src, og, par):
                EM("gpsimd", lambda e: e.dma_start(
                    out=ar3[:, par * 8192:(par + 1) * 8192], in_=src[og]),
                    writes=[f"wblk{par}"])

            def proj_group(tc16, par, bank, bias_slot):
                for dc in range(NDC):
                    kw = dict(reads=[f"xh2f{dc}", f"wblk{par}"])
                    if dc == 0:
                        kw["acquire"] = [f"pf{bank}"]
                    EM("tensor", lambda e, dc=dc, tc16=tc16, par=par, bank=bank: e.matmul(
                        pf[bank][:, 0:512], xh2f(dc)[:, tc16 * P:(tc16 + 1) * P],
                        wblk(par, dc)[:], start=(dc == 0), stop=False), **kw)
                EM("tensor", lambda e, tc16=tc16, bank=bank, bias_slot=bias_slot: e.matmul(
                    pf[bank][:, 0:512], ones_r[0:1, 0:P],
                    biasrow[0:1, bias_slot * 512:(bias_slot + 1) * 512],
                    start=False, stop=True),
                    reads=["ones_r", f"bias{bias_slot}"], release=[f"pf{bank}"])

            def elu_evac(bank, dst, dres):
                EM("vector", lambda e: e.tensor_scalar(escr(0), pf[bank][:, 0:512], 0.0, None, op0=ALU.min),
                   reads=[f"pf{bank}"], writes=["escr0"])
                EM("scalar", lambda e: e.activation(escr(0), escr(0), AF.Exp),
                   reads=["escr0"], writes=["escr0"])
                EM("vector", lambda e: e.tensor_scalar(escr(1), pf[bank][:, 0:512], 0.0, None, op0=ALU.max),
                   reads=[f"pf{bank}"], writes=["escr1"])
                EM("vector", lambda e: e.tensor_tensor(dst, escr(0), escr(1), op=ALU.add),
                   reads=["escr0", "escr1"], writes=[dres])

            nbias = [0]

            def load_bias(src, slot, og):
                EM("gpsimd", lambda e: e.dma_start(
                    out=biasrow[0:1, slot * 512:(slot + 1) * 512], in_=src[0:1, og * 512:(og + 1) * 512]),
                    writes=[f"bias{slot}"])

            pidx = 0
            for og in range(NOG):
                load_bias(bq_d, 0, og)
                load_bias(bk_d, 1, og)
                load_bias(bv_d, 2, og)
                load_wblk(wq_d, og, pidx % 2)
                for tc16 in range(16):
                    bank = tc16 % 2
                    proj_group(tc16, pidx % 2, bank, 0)
                    elu_evac(bank, qstg(tc16 % 2)[:], f"qstg{tc16 % 2}")
                    EM("sync", lambda e, tc16=tc16, og=og: e.dma_start(
                        out=qpt_s[tc16 * P:(tc16 + 1) * P, og * 512:(og + 1) * 512],
                        in_=qstg(tc16 % 2)[:]),
                        reads=[f"qstg{tc16 % 2}"], writes=[f"QPT{tc16}o{og}"])
                pidx += 1
                load_wblk(wk_d, og, pidx % 2)
                for tc16 in range(16):
                    bank = tc16 % 2
                    proj_group(tc16, pidx % 2, bank, 1)
                    elu_evac(bank, kpblk(tc16)[:], f"kp{tc16}")
                pidx += 1
                load_wblk(wv_d, og, pidx % 2)
                for tc16 in range(16):
                    bank = tc16 % 2
                    proj_group(tc16, pidx % 2, bank, 2)
                    EM("vector", lambda e, tc16=tc16, bank=bank: e.tensor_copy(vblk(tc16)[:], pf[bank][:, 0:512]),
                       reads=[f"pf{bank}"], writes=[f"v{tc16}"])
                pidx += 1
                kvbank = 4 + (og % 2)
                for hh in range(4):
                    for tc16 in range(16):
                        kw = dict(reads=[f"kp{tc16}", f"v{tc16}"])
                        if tc16 == 0:
                            kw["acquire"] = [f"pf{kvbank}h{hh}"]
                        if tc16 == 15:
                            kw["release"] = [f"pf{kvbank}h{hh}"]
                        EM("tensor", lambda e, hh=hh, tc16=tc16, kvbank=kvbank: e.matmul(
                            pf[kvbank][:, hh * P:(hh + 1) * P],
                            kpblk(tc16)[:, hh * P:(hh + 1) * P],
                            vblk(tc16)[:, hh * P:(hh + 1) * P],
                            start=(tc16 == 0), stop=(tc16 == 15)), **kw)
                for hh in range(4):
                    for tc16 in range(16):
                        kw = dict(reads=[f"kp{tc16}", "ones_c"])
                        if tc16 == 0:
                            kw["acquire"] = [f"pf3h{og * 4 + hh}"]
                        if tc16 == 15:
                            kw["release"] = [f"pf3h{og * 4 + hh}"]
                        EM("tensor", lambda e, hh=hh, tc16=tc16, og=og: e.matmul(
                            pf[3][:, og * 4 + hh: og * 4 + hh + 1],
                            kpblk(tc16)[:, hh * P:(hh + 1) * P],
                            ones_c[0:P, 0:1],
                            start=(tc16 == 0), stop=(tc16 == 15)), **kw)
                EM("vector", lambda e, og=og, kvbank=kvbank: e.tensor_copy(
                    kvsb[:, og * 512:(og + 1) * 512], pf[kvbank][:, 0:512]),
                    reads=[f"pf{kvbank}h{hh2}" for hh2 in range(4)], writes=[f"kvsb{og}"])
            EM("vector", lambda e: e.tensor_copy(kvsb[:, 2048:2064], pf[3][:, 0:16]),
               reads=[f"pf3h{h2}" for h2 in range(16)], writes=["kvsb_kp"])
            EM("gpsimd", lambda e: e.dma_start(out=kvi_s.ap()[:, :], in_=ar4[:, 0:2080]),
               reads=[f"kvsb{og}" for og in range(4)] + ["kvsb_kp"], writes=["KVI"])
            if _SKIP_CC:
                EM("gpsimd", lambda e: e.dma_start(out=kvo_s.ap()[:, :], in_=kvi_s.ap()[:, :]),
                   reads=["KVI"], writes=["KVO"])
            else:
                EM("gpsimd", lambda e: e.collective_compute(
                    "AllReduce", ALU.add,
                    replica_groups=[[0, 1], [2, 3], [4, 5], [6, 7]],
                    ins=[kvi_s.ap().opt()], outs=[kvo_s.ap().opt()]).then_inc(cc_sem),
                    reads=["KVI"], writes=[], inc=0)
                EM("gpsimd", lambda e: (e.wait_ge(cc_sem, 1), e.nop())[1],
                   reads=[], writes=["KVO"], inc=16)
            ctx.region_switch(["kvsball"], [f"kvsb{og}" for og in range(4)] + ["kvsb_kp"])
            EM("gpsimd", lambda e: e.dma_start(out=ar4[:, 0:2080], in_=kvo_s.ap()[:, :]),
               reads=["KVO"], writes=["kvsball"])

        def emit_phaseB():
            ctx.region_switch([f"wo{vc}" for vc in range(NDC)],
                              [f"xh2f{dc}" for dc in range(NDC)])
            ctx.region_switch([f"qpfm{qc}" for qc in range(NDC)] + [f"qptm{ts}" for ts in range(4)]
                              + ["atm0", "atm1"],
                              [f"kp{t}" for t in range(16)] + [f"v{t}" for t in range(16)]
                              + ["qstg0", "qstg1", "escr0", "escr1"])
            ctx.region_switch([f"afm{vc}" for vc in range(NDC)] + ["x1t0", "x1t1", "x2t0", "x2t1"],
                              ["wblk0", "wblk1"])
            ctx.region_switch(["kvbf", "zsel"], [f"hid{fc}t{tt}" for fc in range(NFC) for tt in range(2)])
            ctx.region_switch(["pf3"], [f"pf3h{h}" for h in range(16)])
            ctx.region_switch(["pf4"], [f"pf4h{hh}" for hh in range(4)])
            ctx.region_switch(["pf5"], [f"pf5h{hh}" for hh in range(4)])
            EM("vector", lambda e: e.tensor_copy(kvbf[:], kvsb[:, 0:2048]),
               reads=["kvsball"], writes=["kvbf"])
            EM("vector", lambda e: e.memset(zsel[:], 0.0), writes=["zsel"])
            for h in range(H):
                EM("vector", lambda e, h=h: e.tensor_copy(zsel[:, h * 16 + h: h * 16 + h + 1],
                                                          kvsb[:, 2048 + h: 2048 + h + 1]),
                   reads=["kvsball"], writes=["zsel"])
            EM("gpsimd", lambda e: e.dma_start(out=biasrow[0:1, :], in_=bo_d[0:1, :]),
               writes=["bias0", "bias1", "bias2", "bias3"])
            EM("gpsimd", lambda e: e.dma_start(out=ar1b[:, 0:32768], in_=wo_d[:, :]),
               writes=[f"wo{vc}" for vc in range(NDC)])

            for tt in range(4):
                for ts in range(4):
                    tc16 = tt * 4 + ts
                    EM("sync", lambda e, ts=ts, tc16=tc16: e.dma_start(
                        out=qptm(ts)[:], in_=qpt_s[tc16 * P:(tc16 + 1) * P, :]),
                        reads=[f"QPT{tc16}o{og}" for og in range(4)], writes=[f"qptm{ts}"])
                    for qc in range(NDC):
                        bank = qc % 2
                        EM("tensor", lambda e, qc=qc, ts=ts, bank=bank: e.transpose(
                            pb[bank][:, 0:P], qptm(ts)[:, qc * P:(qc + 1) * P], identB[:]),
                            reads=[f"qptm{ts}", "identB"], acquire=[f"pb{bank}"], release=[f"pb{bank}"])
                        EM("vector", lambda e, qc=qc, ts=ts, bank=bank: e.tensor_copy(
                            qpfm(qc)[:, ts * P:(ts + 1) * P], pb[bank][:, 0:P]),
                            reads=[f"pb{bank}"], writes=[f"qpfm{qc}ts{ts}"])
                    zpar = ts % 2
                    for qc in range(NDC):
                        kw = dict(reads=[f"qpfm{qc}ts{ts}", "zsel"])
                        if qc == 0:
                            kw["acquire"] = ["pf0"]
                        if qc == NDC - 1:
                            kw["release"] = ["pf0"]
                        EM("tensor", lambda e, qc=qc, ts=ts: e.matmul(
                            pf[0][:, 0:16], qpfm(qc)[:, ts * P:(ts + 1) * P],
                            zsel[:, qc * 16:(qc + 1) * 16],
                            start=(qc == 0), stop=(qc == NDC - 1)), **kw)
                    EM("vector", lambda e, zpar=zpar: e.tensor_scalar(zinvv(zpar), pf[0][:, 0:16], Z_EPS, None, op0=ALU.add),
                       reads=["pf0"], writes=[f"zinv{zpar}"])
                    EM("vector", lambda e, zpar=zpar: e.reciprocal(zinvv(zpar), zinvv(zpar)),
                       reads=[f"zinv{zpar}"], writes=[f"zinv{zpar}"])
                    apar = ts % 2
                    for og in range(NOG):
                        bank = 1 + (og % 2)
                        for hh in range(4):
                            h = og * 4 + hh
                            kw = dict(reads=[f"qpfm{h}ts{ts}", "kvbf"])
                            if hh == 0:
                                kw["acquire"] = [f"pf{bank}"]
                            if hh == 3:
                                kw["release"] = [f"pf{bank}"]
                            EM("tensor", lambda e, h=h, hh=hh, ts=ts, bank=bank: e.matmul(
                                pf[bank][:, hh * P:(hh + 1) * P],
                                qpfm(h)[:, ts * P:(ts + 1) * P],
                                kvbf[:, h * P:(h + 1) * P],
                                start=(hh == 0), stop=(hh == 3)), **kw)
                        for hh in range(4):
                            h = og * 4 + hh
                            EM("vector", lambda e, h=h, hh=hh, og=og, bank=bank, apar=apar, zpar=zpar: e.tensor_scalar(
                                atm(apar)[:, h * P:(h + 1) * P], pf[bank][:, hh * P:(hh + 1) * P],
                                zinvv(zpar)[:, h:h + 1], None, op0=ALU.mult),
                                reads=[f"pf{bank}", f"zinv{zpar}"], writes=[f"atm{apar}b{h}"])
                    for vc in range(NDC):
                        bank = vc % 2
                        EM("tensor", lambda e, vc=vc, apar=apar, bank=bank: e.transpose(
                            pb[bank][:, 0:P], atm(apar)[:, vc * P:(vc + 1) * P], identB[:]),
                            reads=[f"atm{apar}b{vc}", "identB"], acquire=[f"pb{bank}"], release=[f"pb{bank}"])
                        EM("vector", lambda e, vc=vc, ts=ts, bank=bank: e.tensor_copy(
                            afm(vc)[:, ts * P:(ts + 1) * P], pb[bank][:, 0:P]),
                            reads=[f"pb{bank}"], writes=[f"afm{vc}ts{ts}"])
                    xpar = ts % 2
                    EM("sync", lambda e, xpar=xpar, tc16=tc16: e.dma_start(
                        out=x1t(xpar), in_=x1_s[tc16 * P:(tc16 + 1) * P, :]),
                        reads=[f"X1S{tc16}"], writes=[f"x1t{xpar}"])
                    for og in range(NOG):
                        bank = 3 + (og % 2)
                        for vc in range(NDC):
                            kw = dict(reads=[f"afm{vc}ts{ts}", f"wo{vc}"])
                            if vc == 0:
                                kw["acquire"] = [f"pf{bank}"]
                            EM("tensor", lambda e, vc=vc, ts=ts, og=og, bank=bank: e.matmul(
                                pf[bank][:, 0:512], afm(vc)[:, ts * P:(ts + 1) * P],
                                wovw(vc)[:, og * 512:(og + 1) * 512],
                                start=(vc == 0), stop=False), **kw)
                        EM("tensor", lambda e, og=og, bank=bank: e.matmul(
                            pf[bank][:, 0:512], ones_r[0:1, 0:P],
                            biasrow[0:1, og * 512:(og + 1) * 512],
                            start=False, stop=True),
                            reads=["ones_r", f"bias{og}"], release=[f"pf{bank}"])
                        EM("vector", lambda e, og=og, bank=bank, xpar=xpar: e.tensor_tensor(
                            x2t(xpar)[:, og * 512:(og + 1) * 512], pf[bank][:, 0:512],
                            x1t(xpar)[:, og * 512:(og + 1) * 512], op=ALU.add),
                            reads=[f"pf{bank}", f"x1t{xpar}"], writes=[f"x2t{xpar}"])
                    EM("sync", lambda e, xpar=xpar, tc16=tc16: e.dma_start(
                        out=x2_s[tc16 * P:(tc16 + 1) * P, :], in_=x2t(xpar)),
                        reads=[f"x2t{xpar}"], writes=[f"X2S{tc16}", f"SRC2_{tc16}"])

        def emit_phaseC():
            ctx.region_switch([f"X{t}" for t in range(8)], [f"wo{vc}" for vc in range(NDC)])
            ctx.region_switch([f"xh{dc}c{tc8}" for dc in range(NDC) for tc8 in range(8)]
                              + ["xhtm0", "xhtm1"],
                              [f"qpfm{qc}ts{ts}" for qc in range(NDC) for ts in range(4)]
                              + [f"qptm{ts}" for ts in range(4)]
                              + [f"atm{i}b{h}" for i in range(2) for h in range(H)])
            ctx.region_switch([f"wg{dc}" for dc in range(NDC)] + [f"wu{dc}" for dc in range(NDC)]
                              + [f"wd{p}_{fc}" for p in range(2) for fc in range(NFC)],
                              [f"afm{vc}ts{ts}" for vc in range(NDC) for ts in range(4)]
                              + ["x1t0", "x1t1", "x2t0", "x2t1"])
            ctx.region_switch([f"hid{fc}t{tt}" for fc in range(NFC) for tt in range(2)],
                              ["kvbf", "zsel"])
            for half in range(2):
                for tc8 in range(8):
                    tc16 = half * 8 + tc8
                    ctx.region_switch([f"SRC{tc16}"], [f"SRC2_{tc16}"])
                emit_ffn_half(half, x2_s, out_d, wsrc="bf16")

        emit_phaseA_half(0)
        emit_phaseA_half(1)
        if _BUILD_UPTO == "ffn1":
            for tc16 in range(16):
                EM("sync", lambda e, tc16=tc16: e.dma_start(
                    out=out_d[tc16 * P:(tc16 + 1) * P, :], in_=x1_s[tc16 * P:(tc16 + 1) * P, :]),
                    reads=[f"X1S{tc16}"], writes=[])
        else:
            emit_attnproj()
            emit_phaseB()
            if _BUILD_UPTO == "attn":
                for tc16 in range(16):
                    EM("sync", lambda e, tc16=tc16: e.dma_start(
                        out=out_d[tc16 * P:(tc16 + 1) * P, :], in_=x2_s[tc16 * P:(tc16 + 1) * P, :]),
                        reads=[f"X2S{tc16}"], writes=[])
            else:
                emit_phaseC()

        ctx.final_wait("sync", ["sync", "gpsimd"])
        ctx.run(block)
    return nc, ctx


def kernel(L0_attn_hidden_states, L0_attn_q_w, L0_attn_q_b, L0_attn_k_w, L0_attn_k_b,
           L0_attn_v_w, L0_attn_v_b, L0_attn_o_w, L0_attn_o_b,
           L0_ffn_w_gate_router, L0_ffn_w_gate_proj, L0_ffn_w_up_proj, L0_ffn_w_down_proj):
    f32 = np.float32
    hs = np.asarray(L0_attn_hidden_states, f32)
    c = np.ascontiguousarray

    def pack_qkv(w):
        return c(np.asarray(w, f32).T.reshape(16, P, 4, 512)
                 .transpose(2, 1, 0, 3).reshape(4, P, 8192))

    wq = pack_qkv(L0_attn_q_w)
    wk = pack_qkv(L0_attn_k_w)
    wv = pack_qkv(L0_attn_v_w)
    wo = c(np.asarray(L0_attn_o_w, f32).T.reshape(16, P, 2048)
           .transpose(1, 0, 2).reshape(P, 32768))
    bq = c(np.asarray(L0_attn_q_b, f32).reshape(1, D))
    bk = c(np.asarray(L0_attn_k_b, f32).reshape(1, D))
    bv = c(np.asarray(L0_attn_v_b, f32).reshape(1, D))
    bo = c(np.asarray(L0_attn_o_b, f32).reshape(1, D))
    wr = c(np.asarray(L0_ffn_w_gate_router, f32).T)
    def pack_gu(w):
        return c(np.asarray(w, f32).transpose(0, 2, 1).reshape(E, 16, P, FF)
                 .transpose(0, 2, 1, 3).reshape(E, P, 12288))

    wg = pack_gu(L0_ffn_w_gate_proj)
    wu = pack_gu(L0_ffn_w_up_proj)
    wd = c(np.asarray(L0_ffn_w_down_proj, f32).transpose(0, 2, 1)
           .reshape(E, NFC, P, 4, 512).transpose(0, 3, 2, 1, 4)
           .reshape(E, 4, P, 3072))
    ident = np.eye(P, dtype=f32)

    nc, _ = build_program()
    shared = {"wq": wq, "wk": wk, "wv": wv, "wo": wo,
              "bq": bq, "bk": bk, "bv": bv, "bo": bo,
              "wr": wr, "wg": wg, "wu": wu, "wd": wd, "ident": ident}
    in_maps = []
    for core in range(8):
        b, h = core // 2, core % 2
        xc = c(hs[b, h * T:(h + 1) * T, :])
        in_maps.append({"x": xc, **shared})
    res = run_bass_kernel_spmd(nc, in_maps, core_ids=list(range(8)))
    out = np.empty((4, 4096, D), f32)
    for core in range(8):
        b, h = core // 2, core % 2
        out[b, h * T:(h + 1) * T, :] = res.results[core]["out"]
    return out
